# version 02
# baseline (speedup 1.0000x reference)
import numpy as np
import ml_dtypes
import concourse.bass as bass
import concourse.mybir as mybir
from concourse.bass_utils import run_bass_kernel_spmd
from contextlib import ExitStack

F32 = mybir.dt.float32
BF16 = mybir.dt.bfloat16
AF = mybir.ActivationFunctionType
ALU = mybir.AluOpType

D = 1024
S = 2048
BS = 512
NB = S // BS
NSEQ = 4
NH = 16
NEG = -1.0e38
EPS = 1e-5

ENGINES = ("sync", "tensor", "vector", "scalar", "gpsimd")


class Tok:
    __slots__ = ("sem", "val", "eng", "dma")

    def __init__(self, sem, val, eng, dma):
        self.sem, self.val, self.eng, self.dma = sem, val, eng, dma


OWN_SEM = {"vector": "cV", "scalar": "cS", "gpsimd": "cG"}


class Sched:

    def __init__(self):
        self.sems = {}
        self.ops = {e: [] for e in ENGINES}

    def op(self, eng, fn, waits=(), sem=None, amt=1, dma=False):
        own = OWN_SEM.get(eng)
        if sem is None and own is not None and not dma:
            sem, amt = own, 1
        ws = []
        for w in waits:
            if w is None:
                continue
            if w.eng == eng and not w.dma and w.sem != OWN_SEM.get(eng):
                continue
            if w.eng == eng and w.sem == own:
                continue
            ws.append(w)
        tok = None
        if sem is not None:
            v = self.sems.get(sem, 0) + amt
            self.sems[sem] = v
            tok = Tok(sem, v, eng, dma)
        self.ops[eng].append((fn, ws, sem, amt))
        return tok

    def emit(self, eng_name, eng, semh):
        own = OWN_SEM.get(eng_name)
        waited = {}
        running = {}
        last_own = 0
        for fn, ws, sem, amt in self.ops[eng_name]:
            if own is not None and last_own > 0 and waited.get(own, -1) < last_own:
                eng.wait_ge(semh[own], last_own)
                waited[own] = last_own
            for w in ws:
                if waited.get(w.sem, -1) >= w.val:
                    continue
                eng.wait_ge(semh[w.sem], w.val)
                waited[w.sem] = w.val
            inst = fn(eng)
            if sem is not None:
                running[sem] = running.get(sem, 0) + amt
                inst.then_inc(semh[sem], amt)
                if sem == own:
                    last_own = running[sem]


def valid_arange(n, c):
    if n == 0:
        if c < 4:
            return None
        alo, ahi = c - 4, 3
    else:
        alo, ahi = max(0, c - 4), min(3, c)
    return (alo, ahi) if alo <= ahi else None


def build_core(niter=NSEQ * NB):
    nc = bass.Bass("TRN2", target_bir_lowering=False)
    st = ExitStack()

    xin = nc.dram_tensor("x", [NSEQ * S, D], F32, kind="ExternalInput").ap()
    out = nc.dram_tensor("out", [NSEQ * S, D], F32, kind="ExternalOutput").ap()
    wnames = ["wq", "wk", "wv", "wo", "wr", "wk2", "wv2"]
    wdram = {n: nc.dram_tensor(n, [D, D], F32, kind="ExternalInput").ap() for n in wnames}
    bq_d = nc.dram_tensor("bq", [128, 8], F32, kind="ExternalInput").ap()
    bk_d = nc.dram_tensor("bk", [128, 8], F32, kind="ExternalInput").ap()
    bv_d = nc.dram_tensor("bv", [1, D], F32, kind="ExternalInput").ap()
    bo_d = nc.dram_tensor("bo", [1, D], F32, kind="ExternalInput").ap()
    mur_d = nc.dram_tensor("mur", [128, 8], F32, kind="ExternalInput").ap()
    muk_d = nc.dram_tensor("muk", [128, 8], F32, kind="ExternalInput").ap()
    cos_d = nc.dram_tensor("cosd", [4, 128, S], F32, kind="ExternalInput").ap()
    sin_d = nc.dram_tensor("sind", [4, 128, S], F32, kind="ExternalInput").ap()
    iden_d = nc.dram_tensor("iden", [128, 128], BF16, kind="ExternalInput").ap()
    tri0_d = nc.dram_tensor("tri0", [128, 128], BF16, kind="ExternalInput").ap()
    tri4_d = nc.dram_tensor("tri4", [128, 128], BF16, kind="ExternalInput").ap()
    hstage = nc.dram_tensor("hstage", [BS, D], BF16).ap()
    h2stage = nc.dram_tensor("h2stage", [BS, D], BF16).ap()
    wbf = {n: nc.dram_tensor(n + "_bf", [8, 128, D], BF16).ap() for n in ["wo", "wr", "wk2", "wv2"]}

    sb = lambda name, shape, dtype: st.enter_context(nc.sbuf_tensor(name, shape, dtype))
    W = {n: sb("W" + n, [128, 8 * D], BF16) for n in ["wq", "wk", "wv"]}
    WS = [sb("WS%d" % i, [128, 8 * D], BF16) for i in range(2)]
    cosb = [sb("cosb%d" % i, [128, BS], F32) for i in range(2)]
    sinb = [sb("sinb%d" % i, [128, BS], F32) for i in range(2)]
    xsl = [sb("xsl%d" % i, [128, D], F32) for i in range(2)]
    htm = sb("htm", [128, 4 * D], BF16)
    hfm = sb("hfm", [128, 8 * BS], BF16)
    qx = sb("qx", [128, 8 * BS], BF16)
    ax = sb("ax", [128, 8 * BS], BF16)
    krot = [sb("krot%d" % i, [128, 8 * BS], BF16) for i in range(2)]
    vext = [sb("vext%d" % i, [128, 4 * 1040], BF16) for i in range(2)]
    pk = sb("pk", [128, 8 * BS], BF16)
    aof = [sb("aof%d" % i, [128, D], F32) for i in range(2)]
    ybuf = sb("ybuf", [128, 4 * D], F32)
    scr = sb("scr", [128, D], F32)
    usb = [sb("usb%d" % i, [64, BS], BF16) for i in range(2)]
    recipb = [sb("recipb%d" % i, [1, BS], BF16) for i in range(2)]
    iden = sb("idens", [128, 128], BF16)
    tri0 = sb("tri0s", [128, 128], BF16)
    tri4 = sb("tri4s", [128, 128], BF16)
    onesb = sb("onesb", [1, 128], BF16)
    neg4 = sb("neg4", [128, 1], F32)
    bqs = sb("bqs", [128, 8], F32)
    bks = sb("bks", [128, 8], F32)
    bvr = sb("bvr", [1, D], BF16)
    bor = sb("bor", [1, D], BF16)
    mixcol = sb("mixcol", [128, 1], BF16)
    relu_t = sb("relu_t", [128, BS], BF16)
    mur_f = sb("mur_f", [128, 8], F32)
    muk_f = sb("muk_f", [128, 8], F32)
    murb = sb("murb", [128, 8], BF16)
    mukb = sb("mukb", [128, 8], BF16)
    h2last = sb("h2last", [128, 8], BF16)
    ssum = sb("ssum", [128, 1], F32)
    nmt = sb("nmt", [128, 1], F32)
    svt = sb("svt", [128, 1], F32)
    vart = sb("vart", [128, 1], F32)
    sdt = sb("sdt", [128, 1], F32)
    rst = sb("rst", [128, 1], F32)

    pb = [st.enter_context(nc.psum_tensor("pb%d" % i, [128, BS], F32)) for i in range(8)]

    semnames = ["cT", "cV", "cS", "cG", "const", "pr0", "pr1", "x0", "x1",
                "cos0", "cos1", "sin0", "sin1", "w0", "w1", "hfm", "hw", "z", "wbf"]
    semh = {n: st.enter_context(nc.semaphore(n)) for n in semnames}

    sch = Sched()
    bank_free = [None] * 8
    buf = {}

    def dma(eng, stream, outap, inap, waits=()):
        return sch.op(eng, lambda e, o=outap, i=inap: e.dma_start(out=o, in_=i),
                      waits=waits, sem=stream, amt=16, dma=True)

    def dmat(eng, stream, outap, inap, waits=()):
        return sch.op(eng, lambda e, o=outap, i=inap: e.dma_start_transpose(o, i),
                      waits=waits, sem=stream, amt=16, dma=True)

    csem = {"tensor": "cT", "vector": "cV", "scalar": "cS", "gpsimd": "cG"}

    def cop(eng, fn, waits=(), tok=False):
        return sch.op(eng, fn, waits=waits, sem=csem[eng] if tok else None)

    dma("sync", "const", iden[:, :], iden_d[:, :])
    dma("sync", "const", tri0[:, :], tri0_d[:, :])
    dma("sync", "const", tri4[:, :], tri4_d[:, :])
    dma("sync", "const", bqs[:, :], bq_d[:, :])
    dma("sync", "const", bks[:, :], bk_d[:, :])
    dma("sync", "const", scr[0:1, :], bv_d[:, :])
    dma("sync", "const", aof[0][0:1, :], bo_d[:, :])
    dma("sync", "const", mur_f[:, :], mur_d[:, :])
    t_const = dma("sync", "const", muk_f[:, :], muk_d[:, :])
    t_id = t_t0 = t_t4 = t_bq = t_bk = t_bv = t_bo = t_mr = t_mk = t_const
    t_small = cop("vector", lambda e: e.memset(neg4[:, :], -4.0), tok=True)
    t_ones = cop("vector", lambda e: e.memset(onesb[:, :], 1.0), tok=True)
    t_bvc = cop("vector", lambda e: e.tensor_copy(bvr[:, :], scr[0:1, :]), waits=[t_bv], tok=True)
    t_boc = cop("vector", lambda e: e.tensor_copy(bor[:, :], aof[0][0:1, :]), waits=[t_bo], tok=True)
    t_mrc = cop("vector", lambda e: e.tensor_copy(murb[:, :], mur_f[:, :]), waits=[t_mr], tok=True)
    t_mkc = cop("vector", lambda e: e.tensor_copy(mukb[:, :], muk_f[:, :]), waits=[t_mk], tok=True)
    t_vones = []
    for b in range(2):
        for tc in range(4):
            ap = bass.AP(vext[b][:, :].tensor, 1040 * tc + 64, [[4160, 128], [65, 16]])
            t_vones.append(cop("vector", lambda e, a=ap: e.memset(a, 1.0), tok=True))
    t_h2l = cop("vector", lambda e: e.memset(h2last[:, :], 0.0), tok=True)

    stg = [scr, aof[0]]
    t_stg = [t_bvc, t_boc]
    t_wres = {}
    t_wscr = {}
    prev_ws_write = {0: None, 1: None}
    kk = 0
    for wn in wnames:
        toks = []
        for ci in range(8):
            sl = kk % 2
            tl = dma("sync", "pr%d" % sl, stg[sl][:, :], wdram[wn][128 * ci:128 * (ci + 1), :],
                     waits=[t_stg[sl]])
            if wn in W:
                tc = cop("vector", lambda e, w=W[wn], c=ci, s=stg[sl]: e.tensor_copy(
                    w[:, D * c:D * (c + 1)], s[:, :]), waits=[tl], tok=True)
            else:
                di = 0 if wn != "wv2" else 1
                dst = WS[di]
                tc = cop("vector", lambda e, w=dst, c=ci, s=stg[sl]: e.tensor_copy(
                    w[:, D * c:D * (c + 1)], s[:, :]),
                    waits=[tl, prev_ws_write[di]], tok=True)
                tw = dma("scalar", "wbf", wbf[wn][ci, :, :], dst[:, D * ci:D * (ci + 1)], waits=[tc])
                toks.append(tw)
            t_stg[sl] = tc
            kk += 1
        if wn in W:
            t_wres[wn] = t_stg[(kk - 1) % 2]
        else:
            t_wscr[wn] = toks
            prev_ws_write[0 if wn != "wv2" else 1] = toks[-1]
    t_wscr_all = t_wscr["wv2"][-1]
    for wn in t_wscr:
        t_wscr[wn] = [t_wscr_all]
    buf["WS0"] = t_wscr_all
    buf["WS1"] = t_wscr_all
    buf["scr"] = t_stg[0]
    buf[("aof", 0)] = t_stg[1]

    def ln_stats(xap, waits):
        t1 = cop("vector", lambda e: e.reduce_sum(ssum[:, :], xap, axis=mybir.AxisListType.X),
                 waits=list(waits), tok=True)
        cop("vector", lambda e: e.tensor_scalar(nmt[:, :], ssum[:, :], -1.0 / D, None, ALU.mult))
        cop("vector", lambda e: e.scalar_tensor_tensor(
            scr[:, :], xap, nmt[:, :], xap, ALU.add, ALU.mult, accum_out=svt[:, :]),
            waits=[buf.get("scr")])
        t4 = cop("vector", lambda e: e.tensor_scalar(vart[:, :], svt[:, :], 1.0 / D, EPS,
                                                     ALU.mult, ALU.add), tok=True)
        buf["scr"] = t4
        t5 = cop("scalar", lambda e: e.activation(out=sdt[:, :], in_=vart[:, :], func=AF.Sqrt),
                 waits=[t4], tok=True)
        t6 = cop("vector", lambda e: e.reciprocal(rst[:, :], sdt[:, :]), waits=[t5], tok=True)
        return t6

    def stream_weight(wn, slot, after):
        ws = [buf.get("WS%d" % slot)] + t_wscr[wn] + list(after)
        t = dma("sync", "w%d" % slot, WS[slot][:, :].rearrange("p (c d) -> p c d", c=8),
                wbf[wn][:, :, :].rearrange("c p d -> p c d"), waits=ws)
        buf["WS%d" % slot] = None
        return t

    t_z_stores = []
    x_free = [None, None]
    prev_z = None
    kv_cons = [None, None]
    kv_tok = [{}, {}]
    h2l_tok = t_h2l
    h2l_cons = None
    next_wo = next_wr = None
    buf["qx"] = None
    buf["ax"] = None
    buf["hfm"] = None
    buf["htm"] = None
    buf["pk"] = None

    for it in range(niter):
        s, n = it // NB, it % NB
        par = n % 2
        rows0 = s * S + n * BS

        if it == 0:
            t_wo = stream_weight("wo", 0, [])
            t_wr = stream_weight("wr", 1, [])
        else:
            t_wo, t_wr = next_wo, next_wr

        t_ln = [None] * 4
        t_x1 = [None] * 4
        for j in range(4):
            sl = j % 2
            w = [x_free[sl]] if j < 2 else [t_ln[j - 2]]
            t_x1[j] = dma("sync", "x%d" % sl, xsl[sl][:, :],
                          xin[rows0 + 128 * j: rows0 + 128 * (j + 1), :], waits=w)
            t_rs = ln_stats(xsl[sl][:, :], [t_x1[j]])
            t_ln[j] = cop("vector", lambda e, sl=sl, j=j: e.tensor_scalar(
                htm[:, D * j:D * (j + 1)], xsl[sl][:, :], nmt[:, :], rst[:, :], ALU.add, ALU.mult),
                waits=[t_rs, buf.get("htm")], tok=True)
        t_hw = [dma("scalar", "hw", hstage[128 * j:128 * (j + 1), :], htm[:, D * j:D * (j + 1)],
                    waits=[t_ln[j]]) for j in range(4)]
        t_hr = [dmat("sync", "hfm", hfm[:, BS * c:BS * (c + 1)], hstage[:, 128 * c:128 * (c + 1)],
                     waits=[t_hw[-1], buf.get("hfm")]) for c in range(8)]
        t_hr = [t_hr[-1]] * 8

        t_cos = {}

        def load_cos(p, n):
            slp = p % 2
            tc1 = dma("sync", "cos%d" % slp, cosb[slp][:, :], cos_d[p, :, n * BS:(n + 1) * BS],
                      waits=[buf.get(("cos", slp))])
            tc2 = dma("sync", "sin%d" % slp, sinb[slp][:, :], sin_d[p, :, n * BS:(n + 1) * BS],
                      waits=[buf.get(("cos", slp))])
            t_cos[p] = (tc1, tc2)

        def proj_pair(wsb, wwait, dst, dst_free, bias_sb, p, banks):
            mm_toks = []
            for bi, dc in enumerate((p, p + 4)):
                bk = banks[bi]
                def mmgroup(e, bk=bk, dc=dc, wsb=wsb):
                    for ci in range(8):
                        ins = e.matmul(pb[bk][:, :],
                                       wsb[:, D * ci + 128 * dc:D * ci + 128 * (dc + 1)],
                                       hfm[:, BS * ci:BS * (ci + 1)],
                                       start=(ci == 0), stop=(ci == 7))
                    return ins
                mm_toks.append(cop("tensor", mmgroup,
                                   waits=[bank_free[bk], wwait] + t_hr, tok=True))
            slp = p % 2
            blo, bhi = banks
            cw = [t_cos[p][0], t_cos[p][1], t_bq, t_bk, buf.get("scr"), dst_free] + mm_toks
            cop("vector", lambda e: e.scalar_tensor_tensor(
                scr[:, 0:BS], pb[blo][:, :], bias_sb[:, p:p + 1], cosb[slp][:, :],
                ALU.add, ALU.mult), waits=cw)
            cop("vector", lambda e: e.scalar_tensor_tensor(
                scr[:, BS:2 * BS], pb[bhi][:, :], bias_sb[:, p + 4:p + 5], sinb[slp][:, :],
                ALU.add, ALU.mult))
            t3 = cop("vector", lambda e: e.tensor_sub(
                dst[:, BS * p:BS * (p + 1)], scr[:, 0:BS], scr[:, BS:2 * BS]), tok=True)
            cop("vector", lambda e: e.scalar_tensor_tensor(
                scr[:, 0:BS], pb[blo][:, :], bias_sb[:, p:p + 1], sinb[slp][:, :],
                ALU.add, ALU.mult))
            t5 = cop("vector", lambda e: e.scalar_tensor_tensor(
                scr[:, BS:2 * BS], pb[bhi][:, :], bias_sb[:, p + 4:p + 5], cosb[slp][:, :],
                ALU.add, ALU.mult), tok=True)
            t6 = cop("vector", lambda e: e.tensor_add(
                dst[:, BS * (p + 4):BS * (p + 5)], scr[:, 0:BS], scr[:, BS:2 * BS]), tok=True)
            bank_free[blo] = t5
            bank_free[bhi] = t5
            buf["scr"] = t6
            buf[("cos", slp)] = t6
            return [t3, t6]

        q_toks = []
        k_toks = []
        load_cos(0, n)
        load_cos(1, n)
        for p in range(4):
            q_toks += proj_pair(W["wq"], t_wres["wq"], qx, buf.get("qx"), bqs, p,
                                (0, 1) if p % 2 == 0 else (2, 3))
            k_toks += proj_pair(W["wk"], t_wres["wk"], krot[par], kv_cons[par], bks, p,
                                (4, 5) if p % 2 == 0 else (6, 7))
            if p + 2 <= 3:
                load_cos(p + 2, n)
        kv_tok[par]["k"] = list(k_toks)

        v_toks = []
        for tj in range(4):
            for hh in range(2):
                bk = (tj * 2 + hh) % 2
                def vmm(e, bk=bk, tj=tj, hh=hh):
                    for ci in range(8):
                        e.matmul(pb[bk][:, :],
                                 hfm[:, BS * ci + 128 * tj:BS * ci + 128 * (tj + 1)],
                                 W["wv"][:, D * ci + BS * hh:D * ci + BS * (hh + 1)],
                                 start=(ci == 0), stop=False)
                    return e.matmul(pb[bk][:, :], onesb[:, :], bvr[:, BS * hh:BS * (hh + 1)],
                                    start=False, stop=True)
                tmm = cop("tensor", vmm, waits=[bank_free[bk], t_wres["wv"], t_bvc, t_ones,
                                                kv_cons[par]], tok=True)
                oap = bass.AP(vext[par][:, :].tensor, 1040 * tj + 520 * hh,
                              [[4160, 128], [65, 8], [1, 64]])
                tev = cop("scalar", lambda e, oap=oap, bk=bk: e.copy(oap, pb[bk][:, :]),
                          waits=[tmm] + t_vones, tok=True)
                bank_free[bk] = tev
                v_toks.append(tev)
        kv_tok[par]["v"] = v_toks

        sc_banks = [0, 1, 2]
        t_attn_evac = []
        t_sc_all = []
        t_av_prev = None
        for h in range(NH):
            md, po = h // 2, 64 * (h % 2)
            chunks = [c for c in range(8) if valid_arange(n, c) is not None]
            exp_toks = []
            for idx, c in enumerate(chunks):
                alo, ahi = valid_arange(n, c)
                qlo, qhi = 128 * alo, 128 * (ahi + 1)
                bk = sc_banks[idx % 3]
                src_par = par if c >= 4 else 1 - par
                tc = c % 4
                ksrc = krot[src_par]
                tri = (tri0, 128 * c) if c <= 3 else (tri4, 128 * (c - 4))
                def smm(e, bk=bk, ksrc=ksrc, md=md, po=po, tc=tc, qlo=qlo, qhi=qhi, tri=tri):
                    e.matmul(pb[bk][:, qlo:qhi],
                             ksrc[po:po + 64, BS * md + 128 * tc:BS * md + 128 * (tc + 1)],
                             qx[po:po + 64, BS * md + qlo:BS * md + qhi],
                             start=True, stop=False)
                    tm, off = tri
                    return e.matmul(pb[bk][:, off:off + 128], iden[:, :], tm[:, :],
                                    start=False, stop=True)
                tmm = cop("tensor", smm,
                          waits=[bank_free[bk], t_id, t_t0, t_t4] + kv_tok[src_par]["k"] + q_toks,
                          tok=True)
                t_sc_all.append(tmm)
                texp = cop("scalar", lambda e, bk=bk, c=c, qlo=qlo, qhi=qhi: e.activation(
                    out=pk[:, BS * c + qlo:BS * c + qhi], in_=pb[bk][:, qlo:qhi],
                    func=AF.Exp, bias=neg4[:, :], scale=0.125),
                    waits=[tmm, t_small, buf.get("pk"), t_av_prev], tok=True)
                exp_toks.append(texp)
                bank_free[bk] = texp
            ub = 4 + h % 2
            def avgroup(e, chunks=chunks, n=n, ub=ub, par=par, h=h):
                first = True
                last_c = chunks[-1]
                for c in chunks:
                    alo, ahi = valid_arange(n, c)
                    src_par = par if c >= 4 else 1 - par
                    tc = c % 4
                    for a in range(alo, ahi + 1):
                        ins = e.matmul(pb[ub][0:65, 128 * a:128 * (a + 1)],
                                       vext[src_par][:, 1040 * tc + 65 * h:1040 * tc + 65 * (h + 1)],
                                       pk[:, BS * c + 128 * a:BS * c + 128 * (a + 1)],
                                       start=first, stop=(c == last_c and a == ahi))
                        first = False
                return ins
            avw = [bank_free[ub]] + exp_toks + kv_tok[par]["v"] + \
                  (kv_tok[1 - par].get("v", []) if n > 0 else [])
            t_av = cop("tensor", avgroup, waits=avw, tok=True)
            t_av_prev = t_av
            uslot = h % 2
            t_uc = cop("scalar", lambda e, u=usb[uslot], ub=ub: e.copy(u[:, :], pb[ub][0:64, :]),
                       waits=[t_av, buf.get(("usb", uslot))], tok=True)
            t_rc = cop("vector", lambda e, rb=recipb[uslot], ub=ub: e.reciprocal(
                rb[:, :], pb[ub][64:65, :]), waits=[t_av, buf.get(("recip", uslot))], tok=True)
            bank_free[ub] = t_rc if False else t_uc
            bb = 6 + h % 2
            t_bc = cop("tensor", lambda e, bb=bb, rb=recipb[uslot]: e.matmul(
                pb[bb][0:64, :], onesb[:, 0:64], rb[:, :], start=True, stop=True),
                waits=[t_rc, bank_free[bb], t_ones], tok=True)
            buf[("recip", uslot)] = t_bc
            t_ae = cop("vector", lambda e, md=md, po=po, bb=bb, u=usb[uslot]: e.tensor_mul(
                ax[po:po + 64, BS * md:BS * (md + 1)], u[:, :], pb[bb][0:64, :]),
                waits=[t_bc, t_uc, buf.get("ax")], tok=True)
            bank_free[bb] = t_ae
            buf[("usb", uslot)] = t_ae
            bank_free[ub] = t_ae
            t_attn_evac.append(t_ae)
        buf["pk"] = t_av_prev
        kv_cons[0] = kv_cons[1] = t_av_prev

        wo_users = []
        t_y = [None] * 4
        t_x2 = [None] * 4
        for j in range(2):
            t_x2[j] = dma("sync", "x%d" % (j % 2), xsl[j % 2][:, :],
                          xin[rows0 + 128 * j: rows0 + 128 * (j + 1), :],
                          waits=[t_ln[j + 2]])
        for tj in range(4):
            t_oev = []
            for hh in range(2):
                bk = hh
                def omm(e, bk=bk, tj=tj, hh=hh):
                    for dc in range(8):
                        e.matmul(pb[bk][:, :], ax[:, BS * dc + 128 * tj:BS * dc + 128 * (tj + 1)],
                                 WS[0][:, D * dc + BS * hh:D * dc + BS * (hh + 1)],
                                 start=(dc == 0), stop=False)
                    return e.matmul(pb[bk][:, :], onesb[:, :], bor[:, BS * hh:BS * hh + BS],
                                    start=False, stop=True)
                tmm = cop("tensor", omm, waits=[bank_free[bk], t_wo, t_boc] + t_attn_evac,
                          tok=True)
                wo_users.append(tmm)
                sl = tj % 2
                tev = cop("scalar", lambda e, sl=sl, hh=hh, bk=bk: e.copy(
                    aof[sl][:, BS * hh:BS * (hh + 1)], pb[bk][:, :]),
                    waits=[tmm, buf.get(("aof", sl))], tok=True)
                t_oev.append(tev)
                bank_free[bk] = tev
            sl = tj % 2
            t_rs = ln_stats(aof[sl][:, :], t_oev)
            tt1 = cop("vector", lambda e, sl=sl: e.tensor_scalar(
                scr[:, :], aof[sl][:, :], nmt[:, :], rst[:, :], ALU.add, ALU.mult),
                waits=[t_rs, buf.get("scr")], tok=True)
            buf[("aof", sl)] = tt1
            t_y[tj] = cop("vector", lambda e, sl=sl, tj=tj: e.tensor_add(
                ybuf[:, D * tj:D * (tj + 1)], scr[:, :], xsl[sl][:, :]),
                waits=[tt1, t_x2[tj], prev_z], tok=True)
            buf["scr"] = t_y[tj]
            x_free[sl] = t_y[tj]
            if tj < 2:
                t_x2[tj + 2] = dma("sync", "x%d" % sl, xsl[sl][:, :],
                                   xin[rows0 + 128 * (tj + 2): rows0 + 128 * (tj + 3), :],
                                   waits=[t_y[tj]])
        t_h2 = []
        for j in range(4):
            t_rs = ln_stats(ybuf[:, D * j:D * (j + 1)], [t_y[j]])
            t_h2.append(cop("vector", lambda e, j=j: e.tensor_scalar(
                htm[:, D * j:D * (j + 1)], ybuf[:, D * j:D * (j + 1)], nmt[:, :], rst[:, :],
                ALU.add, ALU.mult), waits=[t_rs, t_hw[-1]], tok=True))
        t_h2w = [dma("scalar", "hw", h2stage[128 * j:128 * (j + 1), :], htm[:, D * j:D * (j + 1)],
                     waits=[t_h2[j]]) for j in range(4)]
        t_h2r = [dmat("sync", "hfm", hfm[:, BS * c:BS * (c + 1)], h2stage[:, 128 * c:128 * (c + 1)],
                      waits=[t_h2w[-1]] + q_toks + k_toks + v_toks + t_sc_all) for c in range(8)]
        t_h2r = [t_h2r[-1]] * 8

        t_wk2 = stream_weight("wk2", 0, wo_users)

        t_mixr = []
        t_mixk = []
        for ci in range(8):
            off = BS * ci
            tdc = cop("vector", lambda e, off=off: e.tensor_sub(
                pk[:, 0:BS - 1], hfm[:, off + 1:off + BS], hfm[:, off:off + BS - 1]),
                waits=t_h2r + [buf.get("pk"), h2l_cons], tok=True)
            txr = cop("vector", lambda e, off=off, ci=ci: e.scalar_tensor_tensor(
                qx[:, off + 1:off + BS], pk[:, 0:BS - 1], murb[:, ci:ci + 1],
                hfm[:, off:off + BS - 1], ALU.mult, ALU.add),
                waits=[tdc, t_mrc] + t_sc_all, tok=True)
            txk = cop("vector", lambda e, off=off, ci=ci: e.scalar_tensor_tensor(
                ax[:, off + 1:off + BS], pk[:, 0:BS - 1], mukb[:, ci:ci + 1],
                hfm[:, off:off + BS - 1], ALU.mult, ALU.add),
                waits=[t_mkc] + wo_users, tok=True)
            cop("vector", lambda e, off=off, ci=ci: e.tensor_sub(
                mixcol[:, :], hfm[:, off:off + 1], h2last[:, ci:ci + 1]),
                waits=[h2l_tok])
            tr0 = cop("vector", lambda e, off=off, ci=ci: e.scalar_tensor_tensor(
                qx[:, off:off + 1], mixcol[:, :], murb[:, ci:ci + 1],
                h2last[:, ci:ci + 1], ALU.mult, ALU.add), tok=True)
            tk0 = cop("vector", lambda e, off=off, ci=ci: e.scalar_tensor_tensor(
                ax[:, off:off + 1], mixcol[:, :], mukb[:, ci:ci + 1],
                h2last[:, ci:ci + 1], ALU.mult, ALU.add), tok=True)
            t_mixr += [txr, tr0]
            t_mixk += [txk, tk0]
        buf["pk"] = t_mixk[-1]
        lastap = bass.AP(hfm[:, :].tensor, BS - 1, [[8 * BS, 128], [BS, 8]])
        if n == NB - 1:
            h2l_tok = cop("vector", lambda e: e.memset(h2last[:, :], 0.0), tok=True)
        else:
            h2l_tok = cop("vector", lambda e, a=lastap: e.tensor_copy(h2last[:, :], a), tok=True)

        wr_users = []
        t_rsg = []
        for tj in range(4):
            for hh in range(2):
                bk = 2 + hh
                def rmm(e, bk=bk, tj=tj, hh=hh):
                    for ci in range(8):
                        ins = e.matmul(pb[bk][:, :],
                                       qx[:, BS * ci + 128 * tj:BS * ci + 128 * (tj + 1)],
                                       WS[1][:, D * ci + BS * hh:D * ci + BS * (hh + 1)],
                                       start=(ci == 0), stop=(ci == 7))
                    return ins
                tmm = cop("tensor", rmm, waits=[bank_free[bk], t_wr] + t_mixr, tok=True)
                wr_users.append(tmm)
                trs = cop("scalar", lambda e, bk=bk, tj=tj, hh=hh: e.activation(
                    out=htm[:, D * tj + BS * hh:D * tj + BS * (hh + 1)], in_=pb[bk][:, :],
                    func=AF.Sigmoid), waits=[tmm, t_h2w[-1], buf.get(("r", tj, hh))], tok=True)
                t_rsg.append(trs)
                bank_free[bk] = trs

        t_wv2 = stream_weight("wv2", 1, wr_users)

        wk2_users = []
        t_ka = []
        for dk in range(8):
            bk = 4 + dk % 2
            def kmm(e, bk=bk, dk=dk):
                for ci in range(8):
                    ins = e.matmul(pb[bk][:, :],
                                   WS[0][:, D * ci + 128 * dk:D * ci + 128 * (dk + 1)],
                                   ax[:, BS * ci:BS * (ci + 1)],
                                   start=(ci == 0), stop=(ci == 7))
                return ins
            tmm = cop("tensor", kmm, waits=[bank_free[bk], t_wk2] + t_mixk, tok=True)
            wk2_users.append(tmm)
            trel = cop("scalar", lambda e, bk=bk: e.activation(
                out=relu_t[:, :], in_=pb[bk][:, :], func=AF.Relu),
                waits=[tmm, buf.get("relu_t")], tok=True)
            tka = cop("vector", lambda e, dk=dk: e.tensor_mul(
                pk[:, BS * dk:BS * (dk + 1)], relu_t[:, :], relu_t[:, :]),
                waits=[trel] + t_mixk + t_mixr, tok=True)
            buf["relu_t"] = tka
            t_ka.append(tka)
            bank_free[bk] = trel

        wv2_users = []
        t_fe_all = []
        for tj in range(4):
            t_fev = []
            for hh in range(2):
                bk = 6 + hh
                def v2mm(e, bk=bk, tj=tj, hh=hh):
                    for dk in range(8):
                        ins = e.matmul(pb[bk][:, :],
                                       pk[:, BS * dk + 128 * tj:BS * dk + 128 * (tj + 1)],
                                       WS[1][:, D * dk + BS * hh:D * dk + BS * (hh + 1)],
                                       start=(dk == 0), stop=(dk == 7))
                    return ins
                tmm = cop("tensor", v2mm, waits=[bank_free[bk], t_wv2] + t_ka, tok=True)
                wv2_users.append(tmm)
                sl = tj % 2
                tfe = cop("vector", lambda e, sl=sl, hh=hh, bk=bk, tj=tj: e.tensor_mul(
                    aof[sl][:, BS * hh:BS * (hh + 1)], pb[bk][:, :],
                    htm[:, D * tj + BS * hh:D * tj + BS * (hh + 1)]),
                    waits=[tmm, buf.get(("aof", sl))] + t_rsg, tok=True)
                t_fev.append(tfe)
                t_fe_all.append(tfe)
                buf[("r", tj, hh)] = tfe
                bank_free[bk] = tfe
            sl = tj % 2
            t_rs = ln_stats(aof[sl][:, :], t_fev)
            tt1 = cop("vector", lambda e, sl=sl: e.tensor_scalar(
                scr[:, :], aof[sl][:, :], nmt[:, :], rst[:, :], ALU.add, ALU.mult),
                waits=[t_rs, buf.get("scr")], tok=True)
            buf[("aof", sl)] = tt1
            tz = cop("vector", lambda e, sl=sl, tj=tj: e.tensor_add(
                ybuf[:, D * tj:D * (tj + 1)], scr[:, :], ybuf[:, D * tj:D * (tj + 1)]),
                waits=[tt1], tok=True)
            buf["scr"] = tz
            tzd = dma("scalar", "z", out[rows0 + 128 * tj: rows0 + 128 * (tj + 1), :],
                      ybuf[:, D * tj:D * (tj + 1)], waits=[tz])
            t_z_stores.append(tzd)
        h2l_cons = t_mixk[-1]
        prev_z = t_z_stores[-1]
        buf["htm"] = t_fe_all[-1]
        buf["hfm"] = h2l_tok
        buf["qx"] = wr_users[-1]
        buf["ax"] = wk2_users[-1]
        buf["pk"] = wv2_users[-1]

        if it + 1 < niter:
            next_wo = stream_weight("wo", 0, wk2_users)
            next_wr = stream_weight("wr", 1, wv2_users)

    total_z = sch.sems.get("z", 0)
    sch.op("sync", lambda e: e.wait_ge(semh["z"], total_z))

    with nc.allow_low_precision(reason="bf16 matmul operand production"):
        with nc.Block() as block:
            @block.sync
            def _(eng):
                sch.emit("sync", eng, semh)

            @block.tensor
            def _(eng):
                sch.emit("tensor", eng, semh)

            @block.vector
            def _(eng):
                sch.emit("vector", eng, semh)

            @block.scalar
            def _(eng):
                sch.emit("scalar", eng, semh)

            @block.gpsimd
            def _(eng):
                sch.emit("gpsimd", eng, semh)

    st.close()
    return nc


def make_consts():
    half = D // 2
    j = np.arange(half, dtype=np.float64)
    inv = 1.0 / (10000.0 ** (j / half))
    t = np.arange(S, dtype=np.float64)
    ang = t[None, :] * inv[:, None]
    cos = np.cos(ang).astype(np.float32).reshape(4, 128, S)
    sin = np.sin(ang).astype(np.float32).reshape(4, 128, S)
    iden = np.eye(128, dtype=ml_dtypes.bfloat16)
    ii = np.arange(128)
    tri0 = np.where(ii[:, None] >= ii[None, :], 0.0, NEG).astype(ml_dtypes.bfloat16)
    tri4 = np.where(ii[:, None] <= ii[None, :], 0.0, NEG).astype(ml_dtypes.bfloat16)
    return cos, sin, iden, tri0, tri4


def make_inmap(shard, weights):
    cos, sin, iden, tri0, tri4 = make_consts()
    m = {
        "x": np.ascontiguousarray(np.asarray(shard).reshape(NSEQ * S, D)),
        "cosd": cos, "sind": sin, "iden": iden, "tri0": tri0, "tri4": tri4,
    }
    (Wq, bq, Wk, bk, Wv, bv, Wo, bo, Wr, Wk2, Wv2, mur, muk) = weights
    for nme, w in [("wq", Wq), ("wk", Wk), ("wv", Wv), ("wo", Wo),
                   ("wr", Wr), ("wk2", Wk2), ("wv2", Wv2)]:
        m[nme] = np.ascontiguousarray(np.asarray(w).T)
    m["bq"] = np.ascontiguousarray(np.asarray(bq).reshape(8, 128).T)
    m["bk"] = np.ascontiguousarray(np.asarray(bk).reshape(8, 128).T)
    m["bv"] = np.ascontiguousarray(np.asarray(bv).reshape(1, D))
    m["bo"] = np.ascontiguousarray(np.asarray(bo).reshape(1, D))
    m["mur"] = np.ascontiguousarray(np.asarray(mur).reshape(8, 128).T)
    m["muk"] = np.ascontiguousarray(np.asarray(muk).reshape(8, 128).T)
    return m


def kernel(L0_attn_hidden_state, L0_attn_Wq, L0_attn_bq, L0_attn_Wk, L0_attn_bk,
           L0_attn_Wv, L0_attn_bv, L0_attn_Wo, L0_attn_bo,
           L0_ffn_W_r, L0_ffn_W_k, L0_ffn_W_v, L0_ffn_mu_r, L0_ffn_mu_k):
    x = np.asarray(L0_attn_hidden_state)
    weights = (L0_attn_Wq, L0_attn_bq, L0_attn_Wk, L0_attn_bk, L0_attn_Wv, L0_attn_bv,
               L0_attn_Wo, L0_attn_bo, L0_ffn_W_r, L0_ffn_W_k, L0_ffn_W_v,
               L0_ffn_mu_r, L0_ffn_mu_k)
    nc = build_core()
    in_maps = [make_inmap(x[4 * i:4 * (i + 1)], weights) for i in range(8)]
    res = run_bass_kernel_spmd(nc, in_maps, core_ids=list(range(8)))
    outs = [np.asarray(r["out"]).reshape(NSEQ, S, D) for r in res.results]
    return np.concatenate(outs, axis=0)


# version 03
# speedup vs baseline: 1.0751x; 1.0751x over previous
import numpy as np
import ml_dtypes
import concourse.bass as bass
import concourse.mybir as mybir
from concourse.bass_utils import run_bass_kernel_spmd
from contextlib import ExitStack

F32 = mybir.dt.float32
BF16 = mybir.dt.bfloat16
AF = mybir.ActivationFunctionType
ALU = mybir.AluOpType

D = 1024
S = 2048
BS = 512
NB = S // BS
NSEQ = 4
NH = 16
NEG = -1.0e38
EPS = 1e-5

ENGINES = ("sync", "tensor", "vector", "scalar", "gpsimd")


class Tok:
    __slots__ = ("sem", "val", "eng", "dma")

    def __init__(self, sem, val, eng, dma):
        self.sem, self.val, self.eng, self.dma = sem, val, eng, dma


OWN_SEM = {"vector": "cV", "scalar": "cS", "gpsimd": "cG"}


class Sched:

    def __init__(self):
        self.sems = {}
        self.ops = {e: [] for e in ENGINES}

    def op(self, eng, fn, waits=(), sem=None, amt=1, dma=False):
        own = OWN_SEM.get(eng)
        if sem is None and own is not None and not dma:
            sem, amt = own, 1
        ws = []
        for w in waits:
            if w is None:
                continue
            if w.eng == eng and not w.dma and w.sem != OWN_SEM.get(eng):
                continue
            if w.eng == eng and w.sem == own:
                continue
            ws.append(w)
        tok = None
        if sem is not None:
            v = self.sems.get(sem, 0) + amt
            self.sems[sem] = v
            tok = Tok(sem, v, eng, dma)
        self.ops[eng].append((fn, ws, sem, amt))
        return tok

    def emit(self, eng_name, eng, semh):
        own = OWN_SEM.get(eng_name)
        waited = {}
        running = {}
        last_own = 0
        for fn, ws, sem, amt in self.ops[eng_name]:
            if own is not None and last_own > 0 and waited.get(own, -1) < last_own:
                eng.wait_ge(semh[own], last_own)
                waited[own] = last_own
            for w in ws:
                if waited.get(w.sem, -1) >= w.val:
                    continue
                eng.wait_ge(semh[w.sem], w.val)
                waited[w.sem] = w.val
            inst = fn(eng)
            if sem is not None:
                running[sem] = running.get(sem, 0) + amt
                inst.then_inc(semh[sem], amt)
                if sem == own:
                    last_own = running[sem]


def valid_arange(n, c):
    if n == 0:
        if c < 4:
            return None
        alo, ahi = c - 4, 3
    else:
        alo, ahi = max(0, c - 4), min(3, c)
    return (alo, ahi) if alo <= ahi else None


def build_core(niter=NSEQ * NB):
    nc = bass.Bass("TRN2", target_bir_lowering=False)
    st = ExitStack()

    xin = nc.dram_tensor("x", [NSEQ * S, D], F32, kind="ExternalInput").ap()
    out = nc.dram_tensor("out", [NSEQ * S, D], F32, kind="ExternalOutput").ap()
    wnames = ["wq", "wk", "wv", "wo", "wr", "wk2", "wv2"]
    wdram = {n: nc.dram_tensor(n, [D, D], F32, kind="ExternalInput").ap() for n in wnames}
    bq_d = nc.dram_tensor("bq", [128, 8], F32, kind="ExternalInput").ap()
    bk_d = nc.dram_tensor("bk", [128, 8], F32, kind="ExternalInput").ap()
    bv_d = nc.dram_tensor("bv", [1, D], F32, kind="ExternalInput").ap()
    bo_d = nc.dram_tensor("bo", [1, D], F32, kind="ExternalInput").ap()
    mur_d = nc.dram_tensor("mur", [128, 8], F32, kind="ExternalInput").ap()
    muk_d = nc.dram_tensor("muk", [128, 8], F32, kind="ExternalInput").ap()
    cos_d = nc.dram_tensor("cosd", [4, 128, S], F32, kind="ExternalInput").ap()
    sin_d = nc.dram_tensor("sind", [4, 128, S], F32, kind="ExternalInput").ap()
    tri0_d = nc.dram_tensor("tri0", [128, 128], BF16, kind="ExternalInput").ap()
    tri4_d = nc.dram_tensor("tri4", [128, 128], BF16, kind="ExternalInput").ap()
    hstage = nc.dram_tensor("hstage", [BS, D], BF16).ap()
    h2stage = nc.dram_tensor("h2stage", [BS, D], BF16).ap()
    wbf = {n: nc.dram_tensor(n + "_bf", [8, 128, D], BF16).ap() for n in ["wo", "wr", "wk2", "wv2"]}

    sb = lambda name, shape, dtype: st.enter_context(nc.sbuf_tensor(name, shape, dtype))
    W = {n: sb("W" + n, [128, 8 * D], BF16) for n in ["wq", "wk", "wv"]}
    WS = [sb("WS%d" % i, [128, 8 * D], BF16) for i in range(2)]
    cosb = [sb("cosb%d" % i, [128, BS], F32) for i in range(2)]
    sinb = [sb("sinb%d" % i, [128, BS], F32) for i in range(2)]
    xsl = [sb("xsl%d" % i, [128, D], F32) for i in range(2)]
    htm = sb("htm", [128, 4 * D], BF16)
    hfm = sb("hfm", [128, 8 * BS], BF16)
    qx = sb("qx", [128, 8 * BS], BF16)
    ax = sb("ax", [128, 8 * BS], BF16)
    krot = [sb("krot%d" % i, [128, 8 * BS], BF16) for i in range(2)]
    vext = [sb("vext%d" % i, [128, 4 * 1040], BF16) for i in range(2)]
    pk2 = [sb("pk%d" % i, [128, 8 * BS], BF16) for i in range(2)]
    pk = pk2[0]
    aof = [sb("aof0", [128, D], F32), pk2[1][:, 2048:4096].bitcast(F32)]
    ybuf = sb("ybuf", [128, 4 * D], F32)
    scr = sb("scr", [128, D], F32)
    urt = sb("urt", [128, BS], BF16)
    usb = [urt[0:64, :], urt[64:128, :]]
    recipb = [ybuf[0:1, 256 * i:256 * (i + 1)].bitcast(BF16) for i in range(2)]
    tri0 = sb("tri0s", [128, 128], BF16)
    tri4 = sb("tri4s", [128, 128], BF16)
    onesb = sb("onesb", [1, 128], BF16)
    neg4 = sb("neg4", [128, 1], F32)
    bqs = sb("bqs", [128, 8], F32)
    bks = sb("bks", [128, 8], F32)
    bvr = sb("bvr", [1, D], BF16)
    bor = sb("bor", [1, D], BF16)
    mixcol = sb("mixcol", [128, 1], BF16)
    relu_t = urt
    mur_f = sb("mur_f", [128, 8], F32)
    muk_f = sb("muk_f", [128, 8], F32)
    murb = sb("murb", [128, 8], BF16)
    mukb = sb("mukb", [128, 8], BF16)
    h2last = sb("h2last", [128, 8], BF16)
    ssum = sb("ssum", [128, 1], F32)
    nmt = sb("nmt", [128, 1], F32)
    svt = sb("svt", [128, 1], F32)
    vart = sb("vart", [128, 1], F32)
    sdt = sb("sdt", [128, 1], F32)
    rst = sb("rst", [128, 1], F32)

    pb = [st.enter_context(nc.psum_tensor("pb%d" % i, [128, BS], F32)) for i in range(8)]

    semnames = ["cT", "cV", "cS", "cG", "const", "pr0", "pr1", "x0", "x1",
                "cos0", "cos1", "sin0", "sin1", "w0", "w1", "hfm", "hw", "z", "wbf"]
    semh = {n: st.enter_context(nc.semaphore(n)) for n in semnames}

    sch = Sched()
    bank_free = [None] * 8
    buf = {}

    def dma(eng, stream, outap, inap, waits=()):
        return sch.op(eng, lambda e, o=outap, i=inap: e.dma_start(out=o, in_=i),
                      waits=waits, sem=stream, amt=16, dma=True)

    def dmat(eng, stream, outap, inap, waits=()):
        return sch.op(eng, lambda e, o=outap, i=inap: e.dma_start_transpose(o, i),
                      waits=waits, sem=stream, amt=16, dma=True)

    csem = {"tensor": "cT", "vector": "cV", "scalar": "cS", "gpsimd": "cG"}

    def cop(eng, fn, waits=(), tok=False):
        return sch.op(eng, fn, waits=waits, sem=csem[eng] if tok else None)

    dma("sync", "const", tri0[:, :], tri0_d[:, :])
    dma("sync", "const", tri4[:, :], tri4_d[:, :])
    dma("sync", "const", bqs[:, :], bq_d[:, :])
    dma("sync", "const", bks[:, :], bk_d[:, :])
    dma("sync", "const", scr[0:1, :], bv_d[:, :])
    dma("sync", "const", aof[0][0:1, :], bo_d[:, :])
    dma("sync", "const", mur_f[:, :], mur_d[:, :])
    t_const = dma("sync", "const", muk_f[:, :], muk_d[:, :])
    t_t0 = t_t4 = t_bq = t_bk = t_bv = t_bo = t_mr = t_mk = t_const
    t_small = cop("vector", lambda e: e.memset(neg4[:, :], -4.0), tok=True)
    t_ones = cop("vector", lambda e: e.memset(onesb[:, :], 1.0), tok=True)
    t_bvc = cop("vector", lambda e: e.tensor_copy(bvr[:, :], scr[0:1, :]), waits=[t_bv], tok=True)
    t_boc = cop("vector", lambda e: e.tensor_copy(bor[:, :], aof[0][0:1, :]), waits=[t_bo], tok=True)
    t_mrc = cop("vector", lambda e: e.tensor_copy(murb[:, :], mur_f[:, :]), waits=[t_mr], tok=True)
    t_mkc = cop("vector", lambda e: e.tensor_copy(mukb[:, :], muk_f[:, :]), waits=[t_mk], tok=True)
    t_vones = []
    for b in range(2):
        for tc in range(4):
            ap = bass.AP(vext[b][:, :].tensor, 1040 * tc + 64, [[4160, 128], [65, 16]])
            t_vones.append(cop("vector", lambda e, a=ap: e.memset(a, 1.0), tok=True))
    t_h2l = cop("vector", lambda e: e.memset(h2last[:, :], 0.0), tok=True)
    t_pk0 = cop("vector", lambda e: e.memset(pk2[0][:, :], 0.0), tok=True)
    t_pk1 = cop("vector", lambda e: e.memset(pk2[1][:, :], 0.0), tok=True)
    buf[("pk", 0)] = t_pk0
    buf[("pk", 1)] = t_pk1

    stg = [scr, aof[0]]
    t_stg = [t_bvc, t_boc]
    t_wres = {}
    t_wscr = {}
    prev_ws_write = {0: None, 1: None}
    kk = 0
    for wn in wnames:
        toks = []
        for ci in range(8):
            sl = kk % 2
            tl = dma("sync", "pr%d" % sl, stg[sl][:, :], wdram[wn][128 * ci:128 * (ci + 1), :],
                     waits=[t_stg[sl]])
            if wn in W:
                tc = cop("vector", lambda e, w=W[wn], c=ci, s=stg[sl]: e.tensor_copy(
                    w[:, D * c:D * (c + 1)], s[:, :]), waits=[tl], tok=True)
            else:
                di = 0 if wn != "wv2" else 1
                dst = WS[di]
                tc = cop("vector", lambda e, w=dst, c=ci, s=stg[sl]: e.tensor_copy(
                    w[:, D * c:D * (c + 1)], s[:, :]),
                    waits=[tl, prev_ws_write[di]], tok=True)
                tw = dma("scalar", "wbf", wbf[wn][ci, :, :], dst[:, D * ci:D * (ci + 1)], waits=[tc])
                toks.append(tw)
            t_stg[sl] = tc
            kk += 1
        if wn in W:
            t_wres[wn] = t_stg[(kk - 1) % 2]
        else:
            t_wscr[wn] = toks
            prev_ws_write[0 if wn != "wv2" else 1] = toks[-1]
    t_wscr_all = t_wscr["wv2"][-1]
    for wn in t_wscr:
        t_wscr[wn] = [t_wscr_all]
    buf["WS0"] = t_wscr_all
    buf["WS1"] = t_wscr_all
    buf["scr"] = t_stg[0]
    buf[("aof", 0)] = t_stg[1]

    def ln_stats(xap, waits):
        t1 = cop("vector", lambda e: e.reduce_sum(ssum[:, :], xap, axis=mybir.AxisListType.X),
                 waits=list(waits), tok=True)
        cop("vector", lambda e: e.tensor_scalar(nmt[:, :], ssum[:, :], -1.0 / D, None, ALU.mult))
        cop("vector", lambda e: e.scalar_tensor_tensor(
            scr[:, :], xap, nmt[:, :], xap, ALU.add, ALU.mult, accum_out=svt[:, :]),
            waits=[buf.get("scr")])
        t4 = cop("vector", lambda e: e.tensor_scalar(vart[:, :], svt[:, :], 1.0 / D, EPS,
                                                     ALU.mult, ALU.add), tok=True)
        buf["scr"] = t4
        t5 = cop("scalar", lambda e: e.activation(out=sdt[:, :], in_=vart[:, :], func=AF.Sqrt),
                 waits=[t4], tok=True)
        t6 = cop("vector", lambda e: e.reciprocal(rst[:, :], sdt[:, :]), waits=[t5], tok=True)
        return t6

    def stream_weight(wn, slot, after):
        ws = [buf.get("WS%d" % slot)] + t_wscr[wn] + list(after)
        t = dma("gpsimd", "w%d" % slot, WS[slot][:, :].rearrange("p (c d) -> p c d", c=8),
                wbf[wn][:, :, :].rearrange("c p d -> p c d"), waits=ws)
        buf["WS%d" % slot] = None
        return t

    t_z_stores = []
    x_free = [None, None]
    prev_z = None
    kv_cons = [None, None]
    kv_tok = [{}, {}]
    h2l_tok = t_h2l
    h2l_cons = None
    next_wo = next_wr = None
    buf["qx"] = None
    buf["ax"] = None
    buf["hfm"] = None
    buf["htm"] = None
    buf["pk"] = None

    for it in range(niter):
        s, n = it // NB, it % NB
        par = n % 2
        rows0 = s * S + n * BS

        if it == 0:
            t_wo = stream_weight("wo", 0, [])
            t_wr = stream_weight("wr", 1, [])
        else:
            t_wo, t_wr = next_wo, next_wr

        t_ln = [None] * 4
        t_x1 = [None] * 4
        for j in range(4):
            sl = j % 2
            w = [x_free[sl]] if j < 2 else [t_ln[j - 2]]
            t_x1[j] = dma("sync", "x%d" % sl, xsl[sl][:, :],
                          xin[rows0 + 128 * j: rows0 + 128 * (j + 1), :], waits=w)
            t_rs = ln_stats(xsl[sl][:, :], [t_x1[j]])
            t_ln[j] = cop("vector", lambda e, sl=sl, j=j: e.tensor_scalar(
                htm[:, D * j:D * (j + 1)], xsl[sl][:, :], nmt[:, :], rst[:, :], ALU.add, ALU.mult),
                waits=[t_rs, buf.get("htm")], tok=True)
        t_hw = [dma("scalar", "hw", hstage[128 * j:128 * (j + 1), :], htm[:, D * j:D * (j + 1)],
                    waits=[t_ln[j]]) for j in range(4)]
        t_hr = [dmat("sync", "hfm", hfm[:, BS * c:BS * (c + 1)], hstage[:, 128 * c:128 * (c + 1)],
                     waits=[t_hw[-1], buf.get("hfm")]) for c in range(8)]
        t_hr = [t_hr[-1]] * 8

        t_cos = {}

        def load_cos(p, n):
            slp = p % 2
            tc1 = dma("sync", "cos%d" % slp, cosb[slp][:, :], cos_d[p, :, n * BS:(n + 1) * BS],
                      waits=[buf.get(("cos", slp))])
            tc2 = dma("sync", "sin%d" % slp, sinb[slp][:, :], sin_d[p, :, n * BS:(n + 1) * BS],
                      waits=[buf.get(("cos", slp))])
            t_cos[p] = (tc1, tc2)

        def proj_pair(wsb, wwait, dst, dst_free, bias_sb, p, banks):
            mm_toks = []
            for bi, dc in enumerate((p, p + 4)):
                bk = banks[bi]
                def mmgroup(e, bk=bk, dc=dc, wsb=wsb):
                    for ci in range(8):
                        ins = e.matmul(pb[bk][:, :],
                                       wsb[:, D * ci + 128 * dc:D * ci + 128 * (dc + 1)],
                                       hfm[:, BS * ci:BS * (ci + 1)],
                                       start=(ci == 0), stop=(ci == 7))
                    return ins
                mm_toks.append(cop("tensor", mmgroup,
                                   waits=[bank_free[bk], wwait] + t_hr, tok=True))
            slp = p % 2
            blo, bhi = banks
            cw = [t_cos[p][0], t_cos[p][1], t_bq, t_bk, buf.get("scr"), dst_free] + mm_toks
            cop("vector", lambda e: e.scalar_tensor_tensor(
                scr[:, 0:BS], pb[blo][:, :], bias_sb[:, p:p + 1], cosb[slp][:, :],
                ALU.add, ALU.mult), waits=cw)
            cop("vector", lambda e: e.scalar_tensor_tensor(
                scr[:, BS:2 * BS], pb[bhi][:, :], bias_sb[:, p + 4:p + 5], sinb[slp][:, :],
                ALU.add, ALU.mult))
            t3 = cop("vector", lambda e: e.tensor_sub(
                dst[:, BS * p:BS * (p + 1)], scr[:, 0:BS], scr[:, BS:2 * BS]), tok=True)
            cop("vector", lambda e: e.scalar_tensor_tensor(
                scr[:, 0:BS], pb[blo][:, :], bias_sb[:, p:p + 1], sinb[slp][:, :],
                ALU.add, ALU.mult))
            t5 = cop("vector", lambda e: e.scalar_tensor_tensor(
                scr[:, BS:2 * BS], pb[bhi][:, :], bias_sb[:, p + 4:p + 5], cosb[slp][:, :],
                ALU.add, ALU.mult), tok=True)
            t6 = cop("vector", lambda e: e.tensor_add(
                dst[:, BS * (p + 4):BS * (p + 5)], scr[:, 0:BS], scr[:, BS:2 * BS]), tok=True)
            bank_free[blo] = t5
            bank_free[bhi] = t5
            buf["scr"] = t6
            buf[("cos", slp)] = t6
            return [t3, t6]

        q_toks = []
        k_toks = []
        load_cos(0, n)
        load_cos(1, n)
        for p in range(4):
            q_toks += proj_pair(W["wq"], t_wres["wq"], qx, buf.get("qx"), bqs, p,
                                (0, 1) if p % 2 == 0 else (2, 3))
            k_toks += proj_pair(W["wk"], t_wres["wk"], krot[par], kv_cons[par], bks, p,
                                (4, 5) if p % 2 == 0 else (6, 7))
            if p + 2 <= 3:
                load_cos(p + 2, n)
        kv_tok[par]["k"] = list(k_toks)

        v_toks = []
        for tj in range(4):
            for hh in range(2):
                bk = (tj * 2 + hh) % 2
                def vmm(e, bk=bk, tj=tj, hh=hh):
                    for ci in range(8):
                        e.matmul(pb[bk][:, :],
                                 hfm[:, BS * ci + 128 * tj:BS * ci + 128 * (tj + 1)],
                                 W["wv"][:, D * ci + BS * hh:D * ci + BS * (hh + 1)],
                                 start=(ci == 0), stop=False)
                    return e.matmul(pb[bk][:, :], onesb[:, :], bvr[:, BS * hh:BS * (hh + 1)],
                                    start=False, stop=True)
                tmm = cop("tensor", vmm, waits=[bank_free[bk], t_wres["wv"], t_bvc, t_ones,
                                                kv_cons[par]], tok=True)
                oap = bass.AP(vext[par][:, :].tensor, 1040 * tj + 520 * hh,
                              [[4160, 128], [65, 8], [1, 64]])
                tev = cop("scalar", lambda e, oap=oap, bk=bk: e.copy(oap, pb[bk][:, :]),
                          waits=[tmm] + t_vones, tok=True)
                bank_free[bk] = tev
                v_toks.append(tev)
        kv_tok[par]["v"] = v_toks

        sc_banks = [0, 1, 2]
        t_attn_evac = []
        t_sc_all = []
        chunks = [c for c in range(8) if valid_arange(n, c) is not None]
        av_segs = []
        hi_mark = None
        for c in chunks:
            alo, ahi = valid_arange(n, c)
            if hi_mark is None:
                av_segs.append((c, alo, ahi))
                hi_mark = ahi
            else:
                if alo <= hi_mark:
                    av_segs.append((c, alo, min(ahi, hi_mark)))
                if ahi > hi_mark:
                    av_segs.append((c, hi_mark + 1, ahi))
                    hi_mark = ahi
        head_order = [0, 1, 8, 9, 2, 3, 10, 11, 4, 5, 12, 13, 6, 7, 14, 15]
        hstate = {}

        def plan_scores(h):
            md, po = h // 2, 64 * (h % 2)
            qdep = q_toks[2 * md] if md < 4 else q_toks[2 * (md - 4) + 1]
            kdep = k_toks[2 * md] if md < 4 else k_toks[2 * (md - 4) + 1]
            pku = pk2[h % 2]
            exp_toks = []
            for idx, c in enumerate(chunks):
                alo, ahi = valid_arange(n, c)
                qlo, qhi = 128 * alo, 128 * (ahi + 1)
                bk = sc_banks[idx % 3]
                src_par = par if c >= 4 else 1 - par
                tc = c % 4
                ksrc = krot[src_par]
                kw = [kdep] if src_par == par else kv_tok[src_par]["k"]
                tmm = cop("tensor", lambda e, bk=bk, ksrc=ksrc, md=md, po=po, tc=tc,
                          qlo=qlo, qhi=qhi: e.matmul(
                    pb[bk][:, qlo:qhi],
                    ksrc[po:po + 64, BS * md + 128 * tc:BS * md + 128 * (tc + 1)],
                    qx[po:po + 64, BS * md + qlo:BS * md + qhi],
                    start=True, stop=True),
                    waits=[bank_free[bk], qdep] + kw, tok=True)
                t_sc_all.append(tmm)
                texp = cop("scalar", lambda e, bk=bk, c=c, qlo=qlo, qhi=qhi, pku=pku: e.activation(
                    out=pku[:, BS * c + qlo:BS * c + qhi], in_=pb[bk][:, qlo:qhi],
                    func=AF.Exp, bias=neg4[:, :], scale=0.125),
                    waits=[tmm, t_small, buf.get(("pk", h % 2))], tok=True)
                exp_toks.append(texp)
                bank_free[bk] = texp
            gm = []
            pkt = pku[:, :].tensor
            if n > 0:
                ap = bass.AP(pkt, 0, [[8 * BS, 128], [640, 4], [1, 128]])
                m0 = bass.AP(tri0[:, :].tensor, 0, [[128, 128], [0, 4], [1, 128]])
                gm.append(cop("gpsimd", lambda e, ap=ap, m0=m0: e.tensor_mul(ap, ap, m0),
                              waits=[exp_toks[3], t_t0], tok=True))
            ap4 = bass.AP(pkt, 2048, [[8 * BS, 128], [640, 4], [1, 128]])
            m4 = bass.AP(tri4[:, :].tensor, 0, [[128, 128], [0, 4], [1, 128]])
            gm.append(cop("gpsimd", lambda e, ap=ap4, m4=m4: e.tensor_mul(ap, ap, m4),
                          waits=[exp_toks[-1], t_t4], tok=True))
            hstate[h] = (exp_toks, gm, pku)

        def plan_tail(h):
            md, po = h // 2, 64 * (h % 2)
            exp_toks, gm, pku = hstate.pop(h)
            ub = 4 + h % 2
            def avgroup(e, ub=ub, par=par, h=h, pku=pku, segs=tuple(av_segs)):
                for i, (c, alo, ahi) in enumerate(segs):
                    src_par = par if c >= 4 else 1 - par
                    tc = c % 4
                    ins = e.matmul(pb[ub][0:65, 128 * alo:128 * (ahi + 1)],
                                   vext[src_par][:, 1040 * tc + 65 * h:1040 * tc + 65 * (h + 1)],
                                   pku[:, BS * c + 128 * alo:BS * c + 128 * (ahi + 1)],
                                   start=(i == 0), stop=(i == len(segs) - 1))
                return ins
            avw = [bank_free[ub]] + exp_toks + gm + kv_tok[par]["v"] + \
                  (kv_tok[1 - par].get("v", []) if n > 0 else [])
            t_av = cop("tensor", avgroup, waits=avw, tok=True)
            buf[("pk", h % 2)] = t_av
            uslot = h % 2
            t_uc = cop("scalar", lambda e, u=usb[uslot], ub=ub: e.copy(u, pb[ub][0:64, :]),
                       waits=[t_av, buf.get(("usb", uslot))], tok=True)
            t_rc = cop("vector", lambda e, rb=recipb[uslot], ub=ub: e.reciprocal(
                rb, pb[ub][64:65, :]), waits=[t_av, buf.get(("recip", uslot)), prev_z], tok=True)
            bb = 6 + h % 2
            t_bc = cop("tensor", lambda e, bb=bb, rb=recipb[uslot]: e.matmul(
                pb[bb][0:64, :], onesb[:, 0:64], rb, start=True, stop=True),
                waits=[t_rc, bank_free[bb], t_ones], tok=True)
            buf[("recip", uslot)] = t_bc
            t_ae = cop("vector", lambda e, md=md, po=po, bb=bb, u=usb[uslot]: e.tensor_mul(
                ax[po:po + 64, BS * md:BS * (md + 1)], u, pb[bb][0:64, :]),
                waits=[t_bc, t_uc, buf.get("ax")], tok=True)
            bank_free[bb] = t_ae
            buf[("usb", uslot)] = t_ae
            bank_free[ub] = t_ae
            t_attn_evac.append(t_ae)
            return t_av

        t_av_last = None
        for i in range(NH + 1):
            if i < NH:
                plan_scores(head_order[i])
            if i > 0:
                t_av_last = plan_tail(head_order[i - 1])
        kv_cons[0] = kv_cons[1] = t_av_last

        wo_users = []
        t_y = [None] * 4
        t_x2 = [None] * 4
        for j in range(2):
            t_x2[j] = dma("sync", "x%d" % (j % 2), xsl[j % 2][:, :],
                          xin[rows0 + 128 * j: rows0 + 128 * (j + 1), :],
                          waits=[t_ln[j + 2]])
        for tj in range(4):
            t_oev = []
            for hh in range(2):
                bk = hh
                def omm(e, bk=bk, tj=tj, hh=hh):
                    for dc in range(8):
                        e.matmul(pb[bk][:, :], ax[:, BS * dc + 128 * tj:BS * dc + 128 * (tj + 1)],
                                 WS[0][:, D * dc + BS * hh:D * dc + BS * (hh + 1)],
                                 start=(dc == 0), stop=False)
                    return e.matmul(pb[bk][:, :], onesb[:, :], bor[:, BS * hh:BS * hh + BS],
                                    start=False, stop=True)
                tmm = cop("tensor", omm, waits=[bank_free[bk], t_wo, t_boc] + t_attn_evac,
                          tok=True)
                wo_users.append(tmm)
                sl = tj % 2
                tev = cop("scalar", lambda e, sl=sl, hh=hh, bk=bk: e.copy(
                    aof[sl][:, BS * hh:BS * (hh + 1)], pb[bk][:, :]),
                    waits=[tmm, buf.get(("aof", sl))], tok=True)
                t_oev.append(tev)
                bank_free[bk] = tev
            sl = tj % 2
            t_rs = ln_stats(aof[sl][:, :], t_oev)
            tt1 = cop("vector", lambda e, sl=sl: e.tensor_scalar(
                scr[:, :], aof[sl][:, :], nmt[:, :], rst[:, :], ALU.add, ALU.mult),
                waits=[t_rs, buf.get("scr")], tok=True)
            buf[("aof", sl)] = tt1
            t_y[tj] = cop("gpsimd", lambda e, sl=sl, tj=tj: e.tensor_add(
                ybuf[:, D * tj:D * (tj + 1)], scr[:, :], xsl[sl][:, :]),
                waits=[tt1, t_x2[tj], prev_z], tok=True)
            buf["scr"] = t_y[tj]
            x_free[sl] = t_y[tj]
            if tj < 2:
                t_x2[tj + 2] = dma("sync", "x%d" % sl, xsl[sl][:, :],
                                   xin[rows0 + 128 * (tj + 2): rows0 + 128 * (tj + 3), :],
                                   waits=[t_y[tj]])
        t_h2 = []
        for j in range(4):
            t_rs = ln_stats(ybuf[:, D * j:D * (j + 1)], [t_y[j]])
            t_h2.append(cop("vector", lambda e, j=j: e.tensor_scalar(
                htm[:, D * j:D * (j + 1)], ybuf[:, D * j:D * (j + 1)], nmt[:, :], rst[:, :],
                ALU.add, ALU.mult), waits=[t_rs, t_hw[-1]], tok=True))
        t_h2w = [dma("scalar", "hw", h2stage[128 * j:128 * (j + 1), :], htm[:, D * j:D * (j + 1)],
                     waits=[t_h2[j]]) for j in range(4)]
        t_h2r = [dmat("sync", "hfm", hfm[:, BS * c:BS * (c + 1)], h2stage[:, 128 * c:128 * (c + 1)],
                      waits=[t_h2w[-1]] + q_toks + k_toks + v_toks + t_sc_all) for c in range(8)]
        t_h2r = [t_h2r[-1]] * 8

        t_wk2 = stream_weight("wk2", 0, wo_users)

        t_mixr = []
        t_mixk = []
        for ci in range(8):
            off = BS * ci
            tdc = cop("vector", lambda e, off=off: e.tensor_sub(
                pk2[1][:, 0:BS - 1], hfm[:, off + 1:off + BS], hfm[:, off:off + BS - 1]),
                waits=t_h2r + [buf.get(("pk", 1)), h2l_cons], tok=True)
            txr = cop("vector", lambda e, off=off, ci=ci: e.scalar_tensor_tensor(
                qx[:, off + 1:off + BS], pk2[1][:, 0:BS - 1], murb[:, ci:ci + 1],
                hfm[:, off:off + BS - 1], ALU.mult, ALU.add),
                waits=[tdc, t_mrc] + t_sc_all, tok=True)
            txk = cop("vector", lambda e, off=off, ci=ci: e.scalar_tensor_tensor(
                ax[:, off + 1:off + BS], pk2[1][:, 0:BS - 1], mukb[:, ci:ci + 1],
                hfm[:, off:off + BS - 1], ALU.mult, ALU.add),
                waits=[t_mkc] + wo_users, tok=True)
            cop("vector", lambda e, off=off, ci=ci: e.tensor_sub(
                mixcol[:, :], hfm[:, off:off + 1], h2last[:, ci:ci + 1]),
                waits=[h2l_tok])
            tr0 = cop("vector", lambda e, off=off, ci=ci: e.scalar_tensor_tensor(
                qx[:, off:off + 1], mixcol[:, :], murb[:, ci:ci + 1],
                h2last[:, ci:ci + 1], ALU.mult, ALU.add), tok=True)
            tk0 = cop("vector", lambda e, off=off, ci=ci: e.scalar_tensor_tensor(
                ax[:, off:off + 1], mixcol[:, :], mukb[:, ci:ci + 1],
                h2last[:, ci:ci + 1], ALU.mult, ALU.add), tok=True)
            t_mixr += [txr, tr0]
            t_mixk += [txk, tk0]
        buf["pk"] = t_mixk[-1]
        lastap = bass.AP(hfm[:, :].tensor, BS - 1, [[8 * BS, 128], [BS, 8]])
        if n == NB - 1:
            h2l_tok = cop("vector", lambda e: e.memset(h2last[:, :], 0.0), tok=True)
        else:
            h2l_tok = cop("vector", lambda e, a=lastap: e.tensor_copy(h2last[:, :], a), tok=True)

        wr_users = []
        t_rsg = []
        for tj in range(4):
            for hh in range(2):
                bk = 2 + hh
                def rmm(e, bk=bk, tj=tj, hh=hh):
                    for ci in range(8):
                        ins = e.matmul(pb[bk][:, :],
                                       qx[:, BS * ci + 128 * tj:BS * ci + 128 * (tj + 1)],
                                       WS[1][:, D * ci + BS * hh:D * ci + BS * (hh + 1)],
                                       start=(ci == 0), stop=(ci == 7))
                    return ins
                tmm = cop("tensor", rmm, waits=[bank_free[bk], t_wr] + t_mixr, tok=True)
                wr_users.append(tmm)
                trs = cop("scalar", lambda e, bk=bk, tj=tj, hh=hh: e.activation(
                    out=htm[:, D * tj + BS * hh:D * tj + BS * (hh + 1)], in_=pb[bk][:, :],
                    func=AF.Sigmoid), waits=[tmm, t_h2w[-1], buf.get(("r", tj, hh))], tok=True)
                t_rsg.append(trs)
                bank_free[bk] = trs

        t_wv2 = stream_weight("wv2", 1, wr_users)

        wk2_users = []
        t_ka = []
        for dk in range(8):
            bk = 4 + dk % 2
            def kmm(e, bk=bk, dk=dk):
                for ci in range(8):
                    ins = e.matmul(pb[bk][:, :],
                                   WS[0][:, D * ci + 128 * dk:D * ci + 128 * (dk + 1)],
                                   ax[:, BS * ci:BS * (ci + 1)],
                                   start=(ci == 0), stop=(ci == 7))
                return ins
            tmm = cop("tensor", kmm, waits=[bank_free[bk], t_wk2] + t_mixk, tok=True)
            wk2_users.append(tmm)
            trel = cop("scalar", lambda e, bk=bk: e.activation(
                out=relu_t[:, :], in_=pb[bk][:, :], func=AF.Relu),
                waits=[tmm, buf.get("relu_t"), t_attn_evac[-1], t_attn_evac[-2]], tok=True)
            tka = cop("vector", lambda e, dk=dk: e.tensor_mul(
                pk2[0][:, BS * dk:BS * (dk + 1)], relu_t[:, :], relu_t[:, :]),
                waits=[trel, buf.get(("pk", 0))] + t_mixk + t_mixr, tok=True)
            buf["relu_t"] = tka
            t_ka.append(tka)
            bank_free[bk] = trel

        wv2_users = []
        t_fe_all = []
        for tj in range(4):
            t_fev = []
            for hh in range(2):
                bk = 6 + hh
                def v2mm(e, bk=bk, tj=tj, hh=hh):
                    for dk in range(8):
                        ins = e.matmul(pb[bk][:, :],
                                       pk2[0][:, BS * dk + 128 * tj:BS * dk + 128 * (tj + 1)],
                                       WS[1][:, D * dk + BS * hh:D * dk + BS * (hh + 1)],
                                       start=(dk == 0), stop=(dk == 7))
                    return ins
                tmm = cop("tensor", v2mm, waits=[bank_free[bk], t_wv2] + t_ka, tok=True)
                wv2_users.append(tmm)
                sl = tj % 2
                tfe = cop("vector", lambda e, sl=sl, hh=hh, bk=bk, tj=tj: e.tensor_mul(
                    aof[sl][:, BS * hh:BS * (hh + 1)], pb[bk][:, :],
                    htm[:, D * tj + BS * hh:D * tj + BS * (hh + 1)]),
                    waits=[tmm, buf.get(("aof", sl))] + t_rsg, tok=True)
                t_fev.append(tfe)
                t_fe_all.append(tfe)
                buf[("r", tj, hh)] = tfe
                bank_free[bk] = tfe
            sl = tj % 2
            t_rs = ln_stats(aof[sl][:, :], t_fev)
            tt1 = cop("vector", lambda e, sl=sl: e.tensor_scalar(
                scr[:, :], aof[sl][:, :], nmt[:, :], rst[:, :], ALU.add, ALU.mult),
                waits=[t_rs, buf.get("scr")], tok=True)
            buf[("aof", sl)] = tt1
            tz = cop("gpsimd", lambda e, sl=sl, tj=tj: e.tensor_add(
                ybuf[:, D * tj:D * (tj + 1)], scr[:, :], ybuf[:, D * tj:D * (tj + 1)]),
                waits=[tt1], tok=True)
            buf["scr"] = tz
            if tj == 3:
                buf[("pk", 1)] = tt1
            tzd = dma("scalar", "z", out[rows0 + 128 * tj: rows0 + 128 * (tj + 1), :],
                      ybuf[:, D * tj:D * (tj + 1)], waits=[tz])
            t_z_stores.append(tzd)
        h2l_cons = t_mixk[-1]
        prev_z = t_z_stores[-1]
        buf["htm"] = t_fe_all[-1]
        buf["hfm"] = h2l_tok
        buf["qx"] = wr_users[-1]
        buf["ax"] = wk2_users[-1]
        buf[("pk", 0)] = wv2_users[-1]
        buf[("usb", 0)] = t_ka[-1]
        buf[("usb", 1)] = t_ka[-1]

        if it + 1 < niter:
            next_wo = stream_weight("wo", 0, wk2_users)
            next_wr = stream_weight("wr", 1, wv2_users)

    total_z = sch.sems.get("z", 0)
    sch.op("sync", lambda e: e.wait_ge(semh["z"], total_z))

    with nc.allow_low_precision(reason="bf16 matmul operand production"):
        with nc.Block() as block:
            @block.sync
            def _(eng):
                sch.emit("sync", eng, semh)

            @block.tensor
            def _(eng):
                sch.emit("tensor", eng, semh)

            @block.vector
            def _(eng):
                sch.emit("vector", eng, semh)

            @block.scalar
            def _(eng):
                sch.emit("scalar", eng, semh)

            @block.gpsimd
            def _(eng):
                sch.emit("gpsimd", eng, semh)

    st.close()
    return nc


def make_consts():
    half = D // 2
    j = np.arange(half, dtype=np.float64)
    inv = 1.0 / (10000.0 ** (j / half))
    t = np.arange(S, dtype=np.float64)
    ang = t[None, :] * inv[:, None]
    cos = np.cos(ang).astype(np.float32).reshape(4, 128, S)
    sin = np.sin(ang).astype(np.float32).reshape(4, 128, S)
    iden = np.eye(128, dtype=ml_dtypes.bfloat16)
    ii = np.arange(128)
    tri0 = np.where(ii[:, None] >= ii[None, :], 1.0, 0.0).astype(ml_dtypes.bfloat16)
    tri4 = np.where(ii[:, None] <= ii[None, :], 1.0, 0.0).astype(ml_dtypes.bfloat16)
    return cos, sin, iden, tri0, tri4


def make_inmap(shard, weights):
    cos, sin, iden, tri0, tri4 = make_consts()
    m = {
        "x": np.ascontiguousarray(np.asarray(shard).reshape(NSEQ * S, D)),
        "cosd": cos, "sind": sin, "tri0": tri0, "tri4": tri4,
    }
    (Wq, bq, Wk, bk, Wv, bv, Wo, bo, Wr, Wk2, Wv2, mur, muk) = weights
    for nme, w in [("wq", Wq), ("wk", Wk), ("wv", Wv), ("wo", Wo),
                   ("wr", Wr), ("wk2", Wk2), ("wv2", Wv2)]:
        m[nme] = np.ascontiguousarray(np.asarray(w).T)
    m["bq"] = np.ascontiguousarray(np.asarray(bq).reshape(8, 128).T)
    m["bk"] = np.ascontiguousarray(np.asarray(bk).reshape(8, 128).T)
    m["bv"] = np.ascontiguousarray(np.asarray(bv).reshape(1, D))
    m["bo"] = np.ascontiguousarray(np.asarray(bo).reshape(1, D))
    m["mur"] = np.ascontiguousarray(np.asarray(mur).reshape(8, 128).T)
    m["muk"] = np.ascontiguousarray(np.asarray(muk).reshape(8, 128).T)
    return m


def kernel(L0_attn_hidden_state, L0_attn_Wq, L0_attn_bq, L0_attn_Wk, L0_attn_bk,
           L0_attn_Wv, L0_attn_bv, L0_attn_Wo, L0_attn_bo,
           L0_ffn_W_r, L0_ffn_W_k, L0_ffn_W_v, L0_ffn_mu_r, L0_ffn_mu_k):
    x = np.asarray(L0_attn_hidden_state)
    weights = (L0_attn_Wq, L0_attn_bq, L0_attn_Wk, L0_attn_bk, L0_attn_Wv, L0_attn_bv,
               L0_attn_Wo, L0_attn_bo, L0_ffn_W_r, L0_ffn_W_k, L0_ffn_W_v,
               L0_ffn_mu_r, L0_ffn_mu_k)
    nc = build_core()
    in_maps = [make_inmap(x[4 * i:4 * (i + 1)], weights) for i in range(8)]
    res = run_bass_kernel_spmd(nc, in_maps, core_ids=list(range(8)))
    outs = [np.asarray(r["out"]).reshape(NSEQ, S, D) for r in res.results]
    return np.concatenate(outs, axis=0)


# version 04
# speedup vs baseline: 1.1813x; 1.0988x over previous
import numpy as np
import ml_dtypes
import concourse.bass as bass
import concourse.mybir as mybir
from concourse.bass_utils import run_bass_kernel_spmd
from contextlib import ExitStack

F32 = mybir.dt.float32
BF16 = mybir.dt.bfloat16
AF = mybir.ActivationFunctionType
ALU = mybir.AluOpType

D = 1024
S = 2048
BS = 512
NB = S // BS
NSEQ = 4
NH = 16
NEG = -1.0e38
EPS = 1e-5

ENGINES = ("sync", "tensor", "vector", "scalar", "gpsimd")


class Tok:
    __slots__ = ("sem", "val", "eng", "dma")

    def __init__(self, sem, val, eng, dma):
        self.sem, self.val, self.eng, self.dma = sem, val, eng, dma


OWN_SEM = {"vector": "cV", "scalar": "cS", "gpsimd": "cG"}


class Sched:

    def __init__(self):
        self.sems = {}
        self.ops = {e: [] for e in ENGINES}

    def op(self, eng, fn, waits=(), sem=None, amt=1, dma=False):
        own = OWN_SEM.get(eng)
        if sem is None and own is not None and not dma:
            sem, amt = own, 1
        ws = []
        for w in waits:
            if w is None:
                continue
            if w.eng == eng and not w.dma and w.sem != OWN_SEM.get(eng):
                continue
            if w.eng == eng and w.sem == own:
                continue
            ws.append(w)
        tok = None
        if sem is not None:
            v = self.sems.get(sem, 0) + amt
            self.sems[sem] = v
            tok = Tok(sem, v, eng, dma)
        self.ops[eng].append((fn, ws, sem, amt))
        return tok

    def emit(self, eng_name, eng, semh):
        own = OWN_SEM.get(eng_name)
        waited = {}
        running = {}
        last_own = 0
        for fn, ws, sem, amt in self.ops[eng_name]:
            if own is not None and last_own > 0 and waited.get(own, -1) < last_own:
                eng.wait_ge(semh[own], last_own)
                waited[own] = last_own
            for w in ws:
                if waited.get(w.sem, -1) >= w.val:
                    continue
                eng.wait_ge(semh[w.sem], w.val)
                waited[w.sem] = w.val
            inst = fn(eng)
            if sem is not None:
                running[sem] = running.get(sem, 0) + amt
                inst.then_inc(semh[sem], amt)
                if sem == own:
                    last_own = running[sem]


def valid_arange(n, c):
    if n == 0:
        if c < 4:
            return None
        alo, ahi = c - 4, 3
    else:
        alo, ahi = max(0, c - 4), min(3, c)
    return (alo, ahi) if alo <= ahi else None


def build_core(niter=NSEQ * NB):
    nc = bass.Bass("TRN2", target_bir_lowering=False)
    st = ExitStack()

    xin = nc.dram_tensor("x", [NSEQ * S, D], F32, kind="ExternalInput").ap()
    out = nc.dram_tensor("out", [NSEQ * S, D], F32, kind="ExternalOutput").ap()
    wnames = ["wq", "wk", "wv", "wo", "wr", "wk2", "wv2"]
    wdram = {n: nc.dram_tensor(n, [D, D], F32, kind="ExternalInput").ap() for n in wnames}
    bq_d = nc.dram_tensor("bq", [128, 8], F32, kind="ExternalInput").ap()
    bk_d = nc.dram_tensor("bk", [128, 8], F32, kind="ExternalInput").ap()
    bv_d = nc.dram_tensor("bv", [1, D], F32, kind="ExternalInput").ap()
    bo_d = nc.dram_tensor("bo", [1, D], F32, kind="ExternalInput").ap()
    mur_d = nc.dram_tensor("mur", [128, 8], F32, kind="ExternalInput").ap()
    muk_d = nc.dram_tensor("muk", [128, 8], F32, kind="ExternalInput").ap()
    cos_d = nc.dram_tensor("cosd", [4, 128, S], F32, kind="ExternalInput").ap()
    sin_d = nc.dram_tensor("sind", [4, 128, S], F32, kind="ExternalInput").ap()
    tri0_d = nc.dram_tensor("tri0", [128, 128], BF16, kind="ExternalInput").ap()
    tri4_d = nc.dram_tensor("tri4", [128, 128], BF16, kind="ExternalInput").ap()
    hstage = nc.dram_tensor("hstage", [BS, D], BF16).ap()
    h2stage = nc.dram_tensor("h2stage", [BS, D], BF16).ap()
    wbf = {n: nc.dram_tensor(n + "_bf", [8, 128, D], BF16).ap() for n in ["wo", "wr", "wk2", "wv2"]}

    sb = lambda name, shape, dtype: st.enter_context(nc.sbuf_tensor(name, shape, dtype))
    W = {n: sb("W" + n, [128, 8 * D], BF16) for n in ["wq", "wk", "wv"]}
    WS = [sb("WS%d" % i, [128, 8 * D], BF16) for i in range(2)]
    cosb = [sb("cosb%d" % i, [128, BS], F32) for i in range(2)]
    sinb = [sb("sinb%d" % i, [128, BS], F32) for i in range(2)]
    xsl = [sb("xsl%d" % i, [128, D], F32) for i in range(2)]
    htm = sb("htm", [128, 4 * D], BF16)
    hfm = sb("hfm", [128, 8 * BS], BF16)
    qx = sb("qx", [128, 8 * BS], BF16)
    ax = sb("ax", [128, 8 * BS], BF16)
    krot = [sb("krot%d" % i, [128, 8 * BS], BF16) for i in range(2)]
    vext = [sb("vext%d" % i, [128, 4 * 1040], BF16) for i in range(2)]
    pk2 = [sb("pk%d" % i, [128, 8 * BS], BF16) for i in range(2)]
    pk = pk2[0]
    aof = [sb("aof0", [128, D], F32), pk2[1][:, 2048:4096].bitcast(F32)]
    ybuf = sb("ybuf", [128, 4 * D], F32)
    scr = sb("scr", [128, D], F32)
    urt = sb("urt", [128, BS], BF16)
    usb = [urt[0:64, :], urt[64:128, :]]
    recipb = [ybuf[0:1, 256 * i:256 * (i + 1)].bitcast(BF16) for i in range(2)]
    tri0 = sb("tri0s", [128, 128], BF16)
    tri4 = sb("tri4s", [128, 128], BF16)
    onesb = sb("onesb", [1, 128], BF16)
    neg4 = sb("neg4", [128, 1], F32)
    bqs = sb("bqs", [128, 8], F32)
    bks = sb("bks", [128, 8], F32)
    bvr = sb("bvr", [1, D], BF16)
    bor = sb("bor", [1, D], BF16)
    mixcol = sb("mixcol", [128, 1], BF16)
    relu_t = urt
    mur_f = sb("mur_f", [128, 8], F32)
    muk_f = sb("muk_f", [128, 8], F32)
    murb = sb("murb", [128, 8], BF16)
    mukb = sb("mukb", [128, 8], BF16)
    h2last = sb("h2last", [128, 8], BF16)
    ssum = sb("ssum", [128, 1], F32)
    nmt = sb("nmt", [128, 1], F32)
    svt = sb("svt", [128, 1], F32)
    vart = sb("vart", [128, 1], F32)
    sdt = sb("sdt", [128, 1], F32)
    rst = sb("rst", [128, 1], F32)

    pb = [st.enter_context(nc.psum_tensor("pb%d" % i, [128, BS], F32)) for i in range(8)]

    semnames = ["cT", "cV", "cS", "cG", "const", "pr0", "pr1", "x0", "x1",
                "cos0", "cos1", "sin0", "sin1", "w0", "w1", "hfm", "hw", "z", "wbf"]
    semh = {n: st.enter_context(nc.semaphore(n)) for n in semnames}

    sch = Sched()
    bank_free = [None] * 8
    buf = {}

    def dma(eng, stream, outap, inap, waits=()):
        return sch.op(eng, lambda e, o=outap, i=inap: e.dma_start(out=o, in_=i),
                      waits=waits, sem=stream, amt=16, dma=True)

    def dmat(eng, stream, outap, inap, waits=()):
        return sch.op(eng, lambda e, o=outap, i=inap: e.dma_start_transpose(o, i),
                      waits=waits, sem=stream, amt=16, dma=True)

    csem = {"tensor": "cT", "vector": "cV", "scalar": "cS", "gpsimd": "cG"}

    def cop(eng, fn, waits=(), tok=False):
        return sch.op(eng, fn, waits=waits, sem=csem[eng] if tok else None)

    dma("sync", "const", tri0[:, :], tri0_d[:, :])
    dma("sync", "const", tri4[:, :], tri4_d[:, :])
    dma("sync", "const", bqs[:, :], bq_d[:, :])
    dma("sync", "const", bks[:, :], bk_d[:, :])
    dma("sync", "const", scr[0:1, :], bv_d[:, :])
    dma("sync", "const", aof[0][0:1, :], bo_d[:, :])
    dma("sync", "const", mur_f[:, :], mur_d[:, :])
    t_const = dma("sync", "const", muk_f[:, :], muk_d[:, :])
    t_t0 = t_t4 = t_bq = t_bk = t_bv = t_bo = t_mr = t_mk = t_const
    t_small = cop("vector", lambda e: e.memset(neg4[:, :], -4.0), tok=True)
    t_ones = cop("vector", lambda e: e.memset(onesb[:, :], 1.0), tok=True)
    t_bvc = cop("vector", lambda e: e.tensor_copy(bvr[:, :], scr[0:1, :]), waits=[t_bv], tok=True)
    t_boc = cop("vector", lambda e: e.tensor_copy(bor[:, :], aof[0][0:1, :]), waits=[t_bo], tok=True)
    t_mrc = cop("vector", lambda e: e.tensor_copy(murb[:, :], mur_f[:, :]), waits=[t_mr], tok=True)
    t_mkc = cop("vector", lambda e: e.tensor_copy(mukb[:, :], muk_f[:, :]), waits=[t_mk], tok=True)
    t_vones = []
    for b in range(2):
        for tc in range(4):
            ap = bass.AP(vext[b][:, :].tensor, 1040 * tc + 64, [[4160, 128], [65, 16]])
            t_vones.append(cop("vector", lambda e, a=ap: e.memset(a, 1.0), tok=True))
    t_h2l = cop("vector", lambda e: e.memset(h2last[:, :], 0.0), tok=True)
    t_pk0 = cop("vector", lambda e: e.memset(pk2[0][:, :], 0.0), tok=True)
    t_pk1 = cop("vector", lambda e: e.memset(pk2[1][:, :], 0.0), tok=True)
    buf[("pk", 0)] = t_pk0
    buf[("pk", 1)] = t_pk1

    stg = [scr, aof[0]]
    t_stg = [t_bvc, t_boc]
    t_wres = {}
    t_wscr = {}
    prev_ws_write = {0: None, 1: None}
    kk = 0
    for wn in wnames:
        toks = []
        for ci in range(8):
            sl = kk % 2
            tl = dma("sync", "pr%d" % sl, stg[sl][:, :], wdram[wn][128 * ci:128 * (ci + 1), :],
                     waits=[t_stg[sl]])
            if wn in W:
                tc = cop("vector", lambda e, w=W[wn], c=ci, s=stg[sl]: e.tensor_copy(
                    w[:, D * c:D * (c + 1)], s[:, :]), waits=[tl], tok=True)
            else:
                di = 0 if wn != "wv2" else 1
                dst = WS[di]
                tc = cop("vector", lambda e, w=dst, c=ci, s=stg[sl]: e.tensor_copy(
                    w[:, D * c:D * (c + 1)], s[:, :]),
                    waits=[tl, prev_ws_write[di]], tok=True)
                tw = dma("scalar", "wbf", wbf[wn][ci, :, :], dst[:, D * ci:D * (ci + 1)], waits=[tc])
                toks.append(tw)
            t_stg[sl] = tc
            kk += 1
        if wn in W:
            t_wres[wn] = t_stg[(kk - 1) % 2]
        else:
            t_wscr[wn] = toks
            prev_ws_write[0 if wn != "wv2" else 1] = toks[-1]
    t_wscr_all = t_wscr["wv2"][-1]
    for wn in t_wscr:
        t_wscr[wn] = [t_wscr_all]
    buf["WS0"] = t_wscr_all
    buf["WS1"] = t_wscr_all
    buf["scr"] = t_stg[0]
    buf[("aof", 0)] = t_stg[1]

    def ln_stats(xap, waits):
        t1 = cop("vector", lambda e: e.reduce_sum(ssum[:, :], xap, axis=mybir.AxisListType.X),
                 waits=list(waits), tok=True)
        cop("vector", lambda e: e.tensor_scalar(nmt[:, :], ssum[:, :], -1.0 / D, None, ALU.mult))
        cop("vector", lambda e: e.scalar_tensor_tensor(
            scr[:, :], xap, nmt[:, :], xap, ALU.add, ALU.mult, accum_out=svt[:, :]),
            waits=[buf.get("scr")])
        t4 = cop("vector", lambda e: e.tensor_scalar(vart[:, :], svt[:, :], 1.0 / D, EPS,
                                                     ALU.mult, ALU.add), tok=True)
        buf["scr"] = t4
        t5 = cop("scalar", lambda e: e.activation(out=sdt[:, :], in_=vart[:, :], func=AF.Sqrt),
                 waits=[t4], tok=True)
        t6 = cop("vector", lambda e: e.reciprocal(rst[:, :], sdt[:, :]), waits=[t5], tok=True)
        return t6

    def stream_weight(wn, slot, after):
        ws = [buf.get("WS%d" % slot)] + t_wscr[wn] + list(after)
        t = dma("gpsimd", "w%d" % slot, WS[slot][:, :].rearrange("p (c d) -> p c d", c=8),
                wbf[wn][:, :, :].rearrange("c p d -> p c d"), waits=ws)
        buf["WS%d" % slot] = None
        return t

    t_z_stores = []
    x_free = [None, None]
    prev_z = None
    kv_cons = [None, None]
    kv_tok = [{}, {}]
    h2l_tok = t_h2l
    h2l_cons = None
    next_wo = next_wr = None
    buf["qx"] = None
    buf["ax"] = None
    buf["hfm"] = None
    buf["htm"] = None
    buf["pk"] = None

    for it in range(niter):
        s, n = it // NB, it % NB
        par = n % 2
        rows0 = s * S + n * BS

        if it == 0:
            t_wo = stream_weight("wo", 0, [])
            t_wr = stream_weight("wr", 1, [])
        else:
            t_wo, t_wr = next_wo, next_wr

        t_ln = [None] * 4
        t_x1 = [None] * 4
        for j in range(4):
            sl = j % 2
            w = [x_free[sl]] if j < 2 else [t_ln[j - 2]]
            t_x1[j] = dma("sync", "x%d" % sl, xsl[sl][:, :],
                          xin[rows0 + 128 * j: rows0 + 128 * (j + 1), :], waits=w)
            t_rs = ln_stats(xsl[sl][:, :], [t_x1[j]])
            t_ln[j] = cop("vector", lambda e, sl=sl, j=j: e.tensor_scalar(
                htm[:, D * j:D * (j + 1)], xsl[sl][:, :], nmt[:, :], rst[:, :], ALU.add, ALU.mult),
                waits=[t_rs, buf.get("htm")], tok=True)
        t_hw = [dma("scalar", "hw", hstage[128 * j:128 * (j + 1), :], htm[:, D * j:D * (j + 1)],
                    waits=[t_ln[j]]) for j in range(4)]
        t_hr = [dmat("sync", "hfm", hfm[:, BS * c:BS * (c + 1)], hstage[:, 128 * c:128 * (c + 1)],
                     waits=[t_hw[-1], buf.get("hfm")]) for c in range(8)]
        t_hr = [t_hr[-1]] * 8

        t_cos = {}

        def load_cos(p, n):
            slp = p % 2
            tc1 = dma("sync", "cos%d" % slp, cosb[slp][:, :], cos_d[p, :, n * BS:(n + 1) * BS],
                      waits=[buf.get(("cos", slp))])
            tc2 = dma("sync", "sin%d" % slp, sinb[slp][:, :], sin_d[p, :, n * BS:(n + 1) * BS],
                      waits=[buf.get(("cos", slp))])
            t_cos[p] = (tc1, tc2)

        def proj_pair(wsb, wwait, dst, dst_free, bias_sb, p, banks):
            mm_toks = []
            for bi, dc in enumerate((p, p + 4)):
                bk = banks[bi]
                def mmgroup(e, bk=bk, dc=dc, wsb=wsb):
                    for ci in range(8):
                        ins = e.matmul(pb[bk][:, :],
                                       wsb[:, D * ci + 128 * dc:D * ci + 128 * (dc + 1)],
                                       hfm[:, BS * ci:BS * (ci + 1)],
                                       start=(ci == 0), stop=(ci == 7))
                    return ins
                mm_toks.append(cop("tensor", mmgroup,
                                   waits=[bank_free[bk], wwait] + t_hr, tok=True))
            slp = p % 2
            blo, bhi = banks
            cw = [t_cos[p][0], t_cos[p][1], t_bq, t_bk, buf.get("scr"), dst_free] + mm_toks
            cop("vector", lambda e: e.scalar_tensor_tensor(
                scr[:, 0:BS], pb[blo][:, :], bias_sb[:, p:p + 1], cosb[slp][:, :],
                ALU.add, ALU.mult), waits=cw)
            cop("vector", lambda e: e.scalar_tensor_tensor(
                scr[:, BS:2 * BS], pb[bhi][:, :], bias_sb[:, p + 4:p + 5], sinb[slp][:, :],
                ALU.add, ALU.mult))
            t3 = cop("vector", lambda e: e.tensor_sub(
                dst[:, BS * p:BS * (p + 1)], scr[:, 0:BS], scr[:, BS:2 * BS]), tok=True)
            cop("vector", lambda e: e.scalar_tensor_tensor(
                scr[:, 0:BS], pb[blo][:, :], bias_sb[:, p:p + 1], sinb[slp][:, :],
                ALU.add, ALU.mult))
            t5 = cop("vector", lambda e: e.scalar_tensor_tensor(
                scr[:, BS:2 * BS], pb[bhi][:, :], bias_sb[:, p + 4:p + 5], cosb[slp][:, :],
                ALU.add, ALU.mult), tok=True)
            t6 = cop("vector", lambda e: e.tensor_add(
                dst[:, BS * (p + 4):BS * (p + 5)], scr[:, 0:BS], scr[:, BS:2 * BS]), tok=True)
            bank_free[blo] = t5
            bank_free[bhi] = t5
            buf["scr"] = t6
            buf[("cos", slp)] = t6
            return [t3, t6]

        q_toks = []
        k_toks = []
        load_cos(0, n)
        load_cos(1, n)
        for p in range(4):
            q_toks += proj_pair(W["wq"], t_wres["wq"], qx, buf.get("qx"), bqs, p,
                                (0, 1) if p % 2 == 0 else (2, 3))
            k_toks += proj_pair(W["wk"], t_wres["wk"], krot[par], kv_cons[par], bks, p,
                                (4, 5) if p % 2 == 0 else (6, 7))
            if p + 2 <= 3:
                load_cos(p + 2, n)
        kv_tok[par]["k"] = list(k_toks)

        v_toks = []
        for tj in range(4):
            for hh in range(2):
                bk = (tj * 2 + hh) % 2
                def vmm(e, bk=bk, tj=tj, hh=hh):
                    for ci in range(8):
                        e.matmul(pb[bk][:, :],
                                 hfm[:, BS * ci + 128 * tj:BS * ci + 128 * (tj + 1)],
                                 W["wv"][:, D * ci + BS * hh:D * ci + BS * (hh + 1)],
                                 start=(ci == 0), stop=False)
                    return e.matmul(pb[bk][:, :], onesb[:, :], bvr[:, BS * hh:BS * (hh + 1)],
                                    start=False, stop=True)
                tmm = cop("tensor", vmm, waits=[bank_free[bk], t_wres["wv"], t_bvc, t_ones,
                                                kv_cons[par]], tok=True)
                oap = bass.AP(vext[par][:, :].tensor, 1040 * tj + 520 * hh,
                              [[4160, 128], [65, 8], [1, 64]])
                tev = cop("scalar", lambda e, oap=oap, bk=bk: e.copy(oap, pb[bk][:, :]),
                          waits=[tmm] + t_vones, tok=True)
                bank_free[bk] = tev
                v_toks.append(tev)
        kv_tok[par]["v"] = v_toks

        sc_banks = [0, 1, 2]
        t_attn_evac = []
        t_sc_all = []
        chunks = [c for c in range(8) if valid_arange(n, c) is not None]
        av_segs = []
        hi_mark = None
        for c in chunks:
            alo, ahi = valid_arange(n, c)
            if hi_mark is None:
                av_segs.append((c, alo, ahi))
                hi_mark = ahi
            else:
                if alo <= hi_mark:
                    av_segs.append((c, alo, min(ahi, hi_mark)))
                if ahi > hi_mark:
                    av_segs.append((c, hi_mark + 1, ahi))
                    hi_mark = ahi
        head_order = [0, 1, 8, 9, 2, 3, 10, 11, 4, 5, 12, 13, 6, 7, 14, 15]
        hstate = {}

        def plan_scores(h):
            md, po = h // 2, 64 * (h % 2)
            qdep = q_toks[2 * md] if md < 4 else q_toks[2 * (md - 4) + 1]
            kdep = k_toks[2 * md] if md < 4 else k_toks[2 * (md - 4) + 1]
            pku = pk2[h % 2]
            exp_toks = []
            for idx, c in enumerate(chunks):
                alo, ahi = valid_arange(n, c)
                qlo, qhi = 128 * alo, 128 * (ahi + 1)
                bk = sc_banks[idx % 3]
                src_par = par if c >= 4 else 1 - par
                tc = c % 4
                ksrc = krot[src_par]
                kw = [kdep] if src_par == par else kv_tok[src_par]["k"]
                tmm = cop("tensor", lambda e, bk=bk, ksrc=ksrc, md=md, po=po, tc=tc,
                          qlo=qlo, qhi=qhi: e.matmul(
                    pb[bk][:, qlo:qhi],
                    ksrc[po:po + 64, BS * md + 128 * tc:BS * md + 128 * (tc + 1)],
                    qx[po:po + 64, BS * md + qlo:BS * md + qhi],
                    start=True, stop=True),
                    waits=[bank_free[bk], qdep] + kw, tok=True)
                t_sc_all.append(tmm)
                texp = cop("scalar", lambda e, bk=bk, c=c, qlo=qlo, qhi=qhi, pku=pku: e.activation(
                    out=pku[:, BS * c + qlo:BS * c + qhi], in_=pb[bk][:, qlo:qhi],
                    func=AF.Exp, bias=neg4[:, :], scale=0.125),
                    waits=[tmm, t_small, buf.get(("pk", h % 2))], tok=True)
                exp_toks.append(texp)
                bank_free[bk] = texp
            gm = []
            pkt = pku[:, :].tensor
            if n > 0:
                ap = bass.AP(pkt, 0, [[8 * BS, 128], [640, 4], [1, 128]])
                m0 = bass.AP(tri0[:, :].tensor, 0, [[128, 128], [0, 4], [1, 128]])
                gm.append(cop("gpsimd", lambda e, ap=ap, m0=m0: e.tensor_mul(ap, ap, m0),
                              waits=[exp_toks[3], t_t0], tok=True))
            ap4 = bass.AP(pkt, 2048, [[8 * BS, 128], [640, 4], [1, 128]])
            m4 = bass.AP(tri4[:, :].tensor, 0, [[128, 128], [0, 4], [1, 128]])
            gm.append(cop("gpsimd", lambda e, ap=ap4, m4=m4: e.tensor_mul(ap, ap, m4),
                          waits=[exp_toks[-1], t_t4], tok=True))
            hstate[h] = (exp_toks, gm, pku)

        tails = {}

        def plan_av(h):
            exp_toks, gm, pku = hstate.pop(h)
            ub = 3 + h % 3
            def avgroup(e, ub=ub, par=par, h=h, pku=pku, segs=tuple(av_segs)):
                for i, (c, alo, ahi) in enumerate(segs):
                    src_par = par if c >= 4 else 1 - par
                    tc = c % 4
                    ins = e.matmul(pb[ub][0:65, 128 * alo:128 * (ahi + 1)],
                                   vext[src_par][:, 1040 * tc + 65 * h:1040 * tc + 65 * (h + 1)],
                                   pku[:, BS * c + 128 * alo:BS * c + 128 * (ahi + 1)],
                                   start=(i == 0), stop=(i == len(segs) - 1))
                return ins
            avw = [bank_free[ub]] + exp_toks + gm + kv_tok[par]["v"] + \
                  (kv_tok[1 - par].get("v", []) if n > 0 else [])
            t_av = cop("tensor", avgroup, waits=avw, tok=True)
            buf[("pk", h % 2)] = t_av
            uslot = h % 2
            t_uc = cop("scalar", lambda e, u=usb[uslot], ub=ub: e.copy(u, pb[ub][0:64, :]),
                       waits=[t_av, buf.get(("usb", uslot))], tok=True)
            t_rc = cop("vector", lambda e, rb=recipb[uslot], ub=ub: e.reciprocal(
                rb, pb[ub][64:65, :]), waits=[t_av, buf.get(("recip", uslot)), prev_z], tok=True)
            tails[h] = (ub, uslot, t_uc, t_rc)
            return t_av

        def plan_norm(h):
            md, po = h // 2, 64 * (h % 2)
            ub, uslot, t_uc, t_rc = tails.pop(h)
            bb = 6 + h % 2
            t_bc = cop("tensor", lambda e, bb=bb, rb=recipb[uslot]: e.matmul(
                pb[bb][0:64, :], onesb[:, 0:64], rb, start=True, stop=True),
                waits=[t_rc, bank_free[bb], t_ones], tok=True)
            buf[("recip", uslot)] = t_bc
            t_ae = cop("vector", lambda e, md=md, po=po, bb=bb, u=usb[uslot]: e.tensor_mul(
                ax[po:po + 64, BS * md:BS * (md + 1)], u, pb[bb][0:64, :]),
                waits=[t_bc, t_uc, buf.get("ax")], tok=True)
            bank_free[bb] = t_ae
            buf[("usb", uslot)] = t_ae
            bank_free[ub] = t_ae
            t_attn_evac.append(t_ae)

        t_av_last = None
        for i in range(NH + 2):
            if i < NH:
                plan_scores(head_order[i])
            if 1 <= i <= NH:
                t_av_last = plan_av(head_order[i - 1])
            if i >= 2:
                plan_norm(head_order[i - 2])
        kv_cons[0] = kv_cons[1] = t_av_last

        wo_users = []
        t_y = [None] * 4
        t_x2 = [None] * 4
        for j in range(2):
            t_x2[j] = dma("sync", "x%d" % (j % 2), xsl[j % 2][:, :],
                          xin[rows0 + 128 * j: rows0 + 128 * (j + 1), :],
                          waits=[t_ln[j + 2]])
        for tj in range(4):
            t_oev = []
            for hh in range(2):
                bk = hh
                def omm(e, bk=bk, tj=tj, hh=hh):
                    for dc in range(8):
                        e.matmul(pb[bk][:, :], ax[:, BS * dc + 128 * tj:BS * dc + 128 * (tj + 1)],
                                 WS[0][:, D * dc + BS * hh:D * dc + BS * (hh + 1)],
                                 start=(dc == 0), stop=False)
                    return e.matmul(pb[bk][:, :], onesb[:, :], bor[:, BS * hh:BS * hh + BS],
                                    start=False, stop=True)
                tmm = cop("tensor", omm, waits=[bank_free[bk], t_wo, t_boc] + t_attn_evac,
                          tok=True)
                wo_users.append(tmm)
                sl = tj % 2
                tev = cop("scalar", lambda e, sl=sl, hh=hh, bk=bk: e.copy(
                    aof[sl][:, BS * hh:BS * (hh + 1)], pb[bk][:, :]),
                    waits=[tmm, buf.get(("aof", sl))], tok=True)
                t_oev.append(tev)
                bank_free[bk] = tev
            sl = tj % 2
            t_rs = ln_stats(aof[sl][:, :], t_oev)
            tt1 = cop("vector", lambda e, sl=sl: e.tensor_scalar(
                scr[:, :], aof[sl][:, :], nmt[:, :], rst[:, :], ALU.add, ALU.mult),
                waits=[t_rs, buf.get("scr")], tok=True)
            buf[("aof", sl)] = tt1
            t_y[tj] = cop("gpsimd", lambda e, sl=sl, tj=tj: e.tensor_add(
                ybuf[:, D * tj:D * (tj + 1)], scr[:, :], xsl[sl][:, :]),
                waits=[tt1, t_x2[tj], prev_z], tok=True)
            buf["scr"] = t_y[tj]
            x_free[sl] = t_y[tj]
            if tj < 2:
                t_x2[tj + 2] = dma("sync", "x%d" % sl, xsl[sl][:, :],
                                   xin[rows0 + 128 * (tj + 2): rows0 + 128 * (tj + 3), :],
                                   waits=[t_y[tj]])
        t_h2 = []
        for j in range(4):
            t_rs = ln_stats(ybuf[:, D * j:D * (j + 1)], [t_y[j]])
            t_h2.append(cop("vector", lambda e, j=j: e.tensor_scalar(
                htm[:, D * j:D * (j + 1)], ybuf[:, D * j:D * (j + 1)], nmt[:, :], rst[:, :],
                ALU.add, ALU.mult), waits=[t_rs, t_hw[-1]], tok=True))
        t_h2w = [dma("scalar", "hw", h2stage[128 * j:128 * (j + 1), :], htm[:, D * j:D * (j + 1)],
                     waits=[t_h2[j]]) for j in range(4)]
        t_h2r = [dmat("sync", "hfm", hfm[:, BS * c:BS * (c + 1)], h2stage[:, 128 * c:128 * (c + 1)],
                      waits=[t_h2w[-1]] + q_toks + k_toks + v_toks + t_sc_all) for c in range(8)]
        t_h2r = [t_h2r[-1]] * 8

        t_wk2 = stream_weight("wk2", 0, wo_users)

        t_mixr = []
        t_mixk = []
        for ci in range(8):
            off = BS * ci
            tdc = cop("vector", lambda e, off=off: e.tensor_sub(
                pk2[1][:, 0:BS - 1], hfm[:, off + 1:off + BS], hfm[:, off:off + BS - 1]),
                waits=t_h2r + [buf.get(("pk", 1)), h2l_cons], tok=True)
            txr = cop("vector", lambda e, off=off, ci=ci: e.scalar_tensor_tensor(
                qx[:, off + 1:off + BS], pk2[1][:, 0:BS - 1], murb[:, ci:ci + 1],
                hfm[:, off:off + BS - 1], ALU.mult, ALU.add),
                waits=[tdc, t_mrc] + t_sc_all, tok=True)
            txk = cop("vector", lambda e, off=off, ci=ci: e.scalar_tensor_tensor(
                ax[:, off + 1:off + BS], pk2[1][:, 0:BS - 1], mukb[:, ci:ci + 1],
                hfm[:, off:off + BS - 1], ALU.mult, ALU.add),
                waits=[t_mkc] + wo_users, tok=True)
            cop("vector", lambda e, off=off, ci=ci: e.tensor_sub(
                mixcol[:, :], hfm[:, off:off + 1], h2last[:, ci:ci + 1]),
                waits=[h2l_tok])
            tr0 = cop("vector", lambda e, off=off, ci=ci: e.scalar_tensor_tensor(
                qx[:, off:off + 1], mixcol[:, :], murb[:, ci:ci + 1],
                h2last[:, ci:ci + 1], ALU.mult, ALU.add), tok=True)
            tk0 = cop("vector", lambda e, off=off, ci=ci: e.scalar_tensor_tensor(
                ax[:, off:off + 1], mixcol[:, :], mukb[:, ci:ci + 1],
                h2last[:, ci:ci + 1], ALU.mult, ALU.add), tok=True)
            t_mixr += [txr, tr0]
            t_mixk += [txk, tk0]
        buf["pk"] = t_mixk[-1]
        lastap = bass.AP(hfm[:, :].tensor, BS - 1, [[8 * BS, 128], [BS, 8]])
        if n == NB - 1:
            h2l_tok = cop("vector", lambda e: e.memset(h2last[:, :], 0.0), tok=True)
        else:
            h2l_tok = cop("vector", lambda e, a=lastap: e.tensor_copy(h2last[:, :], a), tok=True)

        wr_users = []
        t_rsg = []
        for tj in range(4):
            tms = []
            for hh in range(2):
                bk = 2 + hh
                def rmm(e, bk=bk, tj=tj, hh=hh):
                    for ci in range(8):
                        ins = e.matmul(pb[bk][:, :],
                                       qx[:, BS * ci + 128 * tj:BS * ci + 128 * (tj + 1)],
                                       WS[1][:, D * ci + BS * hh:D * ci + BS * (hh + 1)],
                                       start=(ci == 0), stop=(ci == 7))
                    return ins
                tms.append(cop("tensor", rmm, waits=[bank_free[bk], t_wr] + t_mixr, tok=True))
            wr_users += tms
            for hh in range(2):
                bk = 2 + hh
                rap = bass.AP(qx[:, :].tensor, 2048 * hh + 128 * tj,
                              [[4096, 128], [512, 4], [1, 128]])
                trs = cop("scalar", lambda e, bk=bk, rap=rap: e.activation(
                    out=rap, in_=pb[bk][:, :], func=AF.Sigmoid),
                    waits=tms + [buf.get(("r", tj, hh))], tok=True)
                t_rsg.append(trs)
                bank_free[bk] = trs

        t_wv2 = stream_weight("wv2", 1, wr_users)

        wk2_users = []
        t_ka = []
        for dk in range(8):
            bk = 4 + dk % 2
            def kmm(e, bk=bk, dk=dk):
                for ci in range(8):
                    ins = e.matmul(pb[bk][:, :],
                                   WS[0][:, D * ci + 128 * dk:D * ci + 128 * (dk + 1)],
                                   ax[:, BS * ci:BS * (ci + 1)],
                                   start=(ci == 0), stop=(ci == 7))
                return ins
            tmm = cop("tensor", kmm, waits=[bank_free[bk], t_wk2] + t_mixk, tok=True)
            wk2_users.append(tmm)
            trel = cop("scalar", lambda e, bk=bk: e.activation(
                out=relu_t[:, :], in_=pb[bk][:, :], func=AF.Relu),
                waits=[tmm, buf.get("relu_t"), t_attn_evac[-1], t_attn_evac[-2]], tok=True)
            tka = cop("vector", lambda e, dk=dk: e.tensor_mul(
                pk2[0][:, BS * dk:BS * (dk + 1)], relu_t[:, :], relu_t[:, :]),
                waits=[trel, buf.get(("pk", 0))] + t_mixk + t_mixr, tok=True)
            buf["relu_t"] = tka
            t_ka.append(tka)
            bank_free[bk] = trel

        wv2_users = []
        t_fe_all = []
        for tj in range(4):
            t_fev = []
            for hh in range(2):
                bk = 6 + hh
                def v2mm(e, bk=bk, tj=tj, hh=hh):
                    for dk in range(8):
                        ins = e.matmul(pb[bk][:, :],
                                       pk2[0][:, BS * dk + 128 * tj:BS * dk + 128 * (tj + 1)],
                                       WS[1][:, D * dk + BS * hh:D * dk + BS * (hh + 1)],
                                       start=(dk == 0), stop=(dk == 7))
                    return ins
                tmm = cop("tensor", v2mm, waits=[bank_free[bk], t_wv2] + t_ka, tok=True)
                wv2_users.append(tmm)
                sl = tj % 2
                rap = bass.AP(qx[:, :].tensor, 2048 * hh + 128 * tj,
                              [[4096, 128], [512, 4], [1, 128]])
                pin = bass.AP(pb[bk][:, :].tensor, 0, [[BS, 128], [128, 4], [1, 128]])
                tfe = cop("vector", lambda e, sl=sl, hh=hh, pin=pin, rap=rap: e.tensor_mul(
                    aof[sl][:, BS * hh:BS * (hh + 1)], pin, rap),
                    waits=[tmm, buf.get(("aof", sl))] + t_rsg, tok=True)
                t_fev.append(tfe)
                t_fe_all.append(tfe)
                buf[("r", tj, hh)] = tfe
                bank_free[bk] = tfe
            sl = tj % 2
            t_rs = ln_stats(aof[sl][:, :], t_fev)
            tt1 = cop("vector", lambda e, sl=sl: e.tensor_scalar(
                scr[:, :], aof[sl][:, :], nmt[:, :], rst[:, :], ALU.add, ALU.mult),
                waits=[t_rs, buf.get("scr")], tok=True)
            buf[("aof", sl)] = tt1
            tz = cop("gpsimd", lambda e, sl=sl, tj=tj: e.tensor_add(
                ybuf[:, D * tj:D * (tj + 1)], scr[:, :], ybuf[:, D * tj:D * (tj + 1)]),
                waits=[tt1], tok=True)
            buf["scr"] = tz
            if tj == 3:
                buf[("pk", 1)] = tt1
            tzd = dma("scalar", "z", out[rows0 + 128 * tj: rows0 + 128 * (tj + 1), :],
                      ybuf[:, D * tj:D * (tj + 1)], waits=[tz])
            t_z_stores.append(tzd)
        h2l_cons = t_mixk[-1]
        prev_z = t_z_stores[-1]
        buf["htm"] = t_h2w[-1]
        buf["hfm"] = h2l_tok
        buf["qx"] = t_fe_all[-1]
        buf["ax"] = wk2_users[-1]
        buf[("pk", 0)] = wv2_users[-1]
        buf[("usb", 0)] = t_ka[-1]
        buf[("usb", 1)] = t_ka[-1]

        if it + 1 < niter:
            next_wo = stream_weight("wo", 0, wk2_users)
            next_wr = stream_weight("wr", 1, wv2_users)

    total_z = sch.sems.get("z", 0)
    sch.op("sync", lambda e: e.wait_ge(semh["z"], total_z))

    with nc.allow_low_precision(reason="bf16 matmul operand production"):
        with nc.Block() as block:
            @block.sync
            def _(eng):
                sch.emit("sync", eng, semh)

            @block.tensor
            def _(eng):
                sch.emit("tensor", eng, semh)

            @block.vector
            def _(eng):
                sch.emit("vector", eng, semh)

            @block.scalar
            def _(eng):
                sch.emit("scalar", eng, semh)

            @block.gpsimd
            def _(eng):
                sch.emit("gpsimd", eng, semh)

    st.close()
    return nc


def make_consts():
    half = D // 2
    j = np.arange(half, dtype=np.float64)
    inv = 1.0 / (10000.0 ** (j / half))
    t = np.arange(S, dtype=np.float64)
    ang = t[None, :] * inv[:, None]
    cos = np.cos(ang).astype(np.float32).reshape(4, 128, S)
    sin = np.sin(ang).astype(np.float32).reshape(4, 128, S)
    iden = np.eye(128, dtype=ml_dtypes.bfloat16)
    ii = np.arange(128)
    tri0 = np.where(ii[:, None] >= ii[None, :], 1.0, 0.0).astype(ml_dtypes.bfloat16)
    tri4 = np.where(ii[:, None] <= ii[None, :], 1.0, 0.0).astype(ml_dtypes.bfloat16)
    return cos, sin, iden, tri0, tri4


def make_inmap(shard, weights):
    cos, sin, iden, tri0, tri4 = make_consts()
    m = {
        "x": np.ascontiguousarray(np.asarray(shard).reshape(NSEQ * S, D)),
        "cosd": cos, "sind": sin, "tri0": tri0, "tri4": tri4,
    }
    (Wq, bq, Wk, bk, Wv, bv, Wo, bo, Wr, Wk2, Wv2, mur, muk) = weights
    for nme, w in [("wq", Wq), ("wk", Wk), ("wv", Wv), ("wo", Wo),
                   ("wr", Wr), ("wk2", Wk2), ("wv2", Wv2)]:
        m[nme] = np.ascontiguousarray(np.asarray(w).T)
    m["bq"] = np.ascontiguousarray(np.asarray(bq).reshape(8, 128).T)
    m["bk"] = np.ascontiguousarray(np.asarray(bk).reshape(8, 128).T)
    m["bv"] = np.ascontiguousarray(np.asarray(bv).reshape(1, D))
    m["bo"] = np.ascontiguousarray(np.asarray(bo).reshape(1, D))
    m["mur"] = np.ascontiguousarray(np.asarray(mur).reshape(8, 128).T)
    m["muk"] = np.ascontiguousarray(np.asarray(muk).reshape(8, 128).T)
    return m


def kernel(L0_attn_hidden_state, L0_attn_Wq, L0_attn_bq, L0_attn_Wk, L0_attn_bk,
           L0_attn_Wv, L0_attn_bv, L0_attn_Wo, L0_attn_bo,
           L0_ffn_W_r, L0_ffn_W_k, L0_ffn_W_v, L0_ffn_mu_r, L0_ffn_mu_k):
    x = np.asarray(L0_attn_hidden_state)
    weights = (L0_attn_Wq, L0_attn_bq, L0_attn_Wk, L0_attn_bk, L0_attn_Wv, L0_attn_bv,
               L0_attn_Wo, L0_attn_bo, L0_ffn_W_r, L0_ffn_W_k, L0_ffn_W_v,
               L0_ffn_mu_r, L0_ffn_mu_k)
    nc = build_core()
    in_maps = [make_inmap(x[4 * i:4 * (i + 1)], weights) for i in range(8)]
    res = run_bass_kernel_spmd(nc, in_maps, core_ids=list(range(8)))
    outs = [np.asarray(r["out"]).reshape(NSEQ, S, D) for r in res.results]
    return np.concatenate(outs, axis=0)


# version 07
# speedup vs baseline: 1.2645x; 1.0705x over previous
import numpy as np
import ml_dtypes
import concourse.bass as bass
import concourse.mybir as mybir
from concourse.bass_utils import run_bass_kernel_spmd
from contextlib import ExitStack

F32 = mybir.dt.float32
BF16 = mybir.dt.bfloat16
AF = mybir.ActivationFunctionType
ALU = mybir.AluOpType

D = 1024
S = 2048
BS = 512
NB = S // BS
NSEQ = 4
NH = 16
NEG = -1.0e38
EPS = 1e-5

ENGINES = ("sync", "tensor", "vector", "scalar", "gpsimd")


class Tok:
    __slots__ = ("sem", "val", "eng", "dma")

    def __init__(self, sem, val, eng, dma):
        self.sem, self.val, self.eng, self.dma = sem, val, eng, dma


OWN_SEM = {"vector": "cV", "scalar": "cS", "gpsimd": "cG"}


class Sched:

    def __init__(self):
        self.sems = {}
        self.ops = {e: [] for e in ENGINES}

    def op(self, eng, fn, waits=(), sem=None, amt=1, dma=False):
        own = OWN_SEM.get(eng)
        if sem is None and own is not None and not dma:
            sem, amt = own, 1
        ws = []
        for w in waits:
            if w is None:
                continue
            if w.eng == eng and not w.dma and w.sem != OWN_SEM.get(eng):
                continue
            if w.eng == eng and w.sem == own:
                continue
            ws.append(w)
        tok = None
        if sem is not None:
            v = self.sems.get(sem, 0) + amt
            self.sems[sem] = v
            tok = Tok(sem, v, eng, dma)
        self.ops[eng].append((fn, ws, sem, amt))
        return tok

    def emit(self, eng_name, eng, semh):
        own = OWN_SEM.get(eng_name)
        waited = {}
        running = {}
        last_own = 0
        for fn, ws, sem, amt in self.ops[eng_name]:
            if own is not None and last_own > 0 and waited.get(own, -1) < last_own:
                eng.wait_ge(semh[own], last_own)
                waited[own] = last_own
            for w in ws:
                if waited.get(w.sem, -1) >= w.val:
                    continue
                eng.wait_ge(semh[w.sem], w.val)
                waited[w.sem] = w.val
            inst = fn(eng)
            if sem is not None:
                running[sem] = running.get(sem, 0) + amt
                inst.then_inc(semh[sem], amt)
                if sem == own:
                    last_own = running[sem]


def valid_arange(n, c):
    if n == 0:
        if c < 4:
            return None
        alo, ahi = c - 4, 3
    else:
        alo, ahi = max(0, c - 4), min(3, c)
    return (alo, ahi) if alo <= ahi else None


def build_core(niter=NSEQ * NB):
    nc = bass.Bass("TRN2", target_bir_lowering=False)
    st = ExitStack()

    xin = nc.dram_tensor("x", [NSEQ * S, D], F32, kind="ExternalInput").ap()
    out = nc.dram_tensor("out", [NSEQ * S, D], F32, kind="ExternalOutput").ap()
    wnames = ["wq", "wk", "wv", "wo", "wr", "wk2", "wv2"]
    wdram = {n: nc.dram_tensor(n, [D, D], F32, kind="ExternalInput").ap() for n in wnames}
    bq_d = nc.dram_tensor("bq", [128, 8], F32, kind="ExternalInput").ap()
    bk_d = nc.dram_tensor("bk", [128, 8], F32, kind="ExternalInput").ap()
    bv_d = nc.dram_tensor("bv", [1, D], F32, kind="ExternalInput").ap()
    bo_d = nc.dram_tensor("bo", [1, D], F32, kind="ExternalInput").ap()
    mur_d = nc.dram_tensor("mur", [128, 8], F32, kind="ExternalInput").ap()
    muk_d = nc.dram_tensor("muk", [128, 8], F32, kind="ExternalInput").ap()
    cos_d = nc.dram_tensor("cosd", [4, 128, S], F32, kind="ExternalInput").ap()
    sin_d = nc.dram_tensor("sind", [4, 128, S], F32, kind="ExternalInput").ap()
    tri0_d = nc.dram_tensor("tri0", [128, 128], BF16, kind="ExternalInput").ap()
    tri4_d = nc.dram_tensor("tri4", [128, 128], BF16, kind="ExternalInput").ap()
    hstage = nc.dram_tensor("hstage", [BS, D], BF16).ap()
    h2stage = nc.dram_tensor("h2stage", [BS, D], BF16).ap()
    wbf = {n: nc.dram_tensor(n + "_bf", [8, 128, D], BF16).ap() for n in ["wo", "wr", "wk2", "wv2"]}

    sb = lambda name, shape, dtype: st.enter_context(nc.sbuf_tensor(name, shape, dtype))
    W = {n: sb("W" + n, [128, 8 * D], BF16) for n in ["wq", "wk", "wv"]}
    WS = [sb("WS%d" % i, [128, 8 * D], BF16) for i in range(2)]
    cosb = [sb("cosb%d" % i, [128, BS], F32) for i in range(2)]
    sinb = [sb("sinb%d" % i, [128, BS], F32) for i in range(2)]
    xsl = [sb("xsl%d" % i, [128, D], F32) for i in range(2)]
    htm = sb("htm", [128, 4 * D], BF16)
    hfm = sb("hfm", [128, 8 * BS], BF16)
    qx = sb("qx", [128, 8 * BS], BF16)
    ax = sb("ax", [128, 8 * BS], BF16)
    krot = [sb("krot%d" % i, [128, 8 * BS], BF16) for i in range(2)]
    vext = [sb("vext%d" % i, [128, 4 * 1040], BF16) for i in range(2)]
    pk2 = [sb("pk%d" % i, [128, 8 * BS], BF16) for i in range(2)]
    pk = pk2[0]
    aof = [sb("aof0", [128, D], F32), pk2[1][:, 2048:4096].bitcast(F32)]
    ybuf = sb("ybuf", [128, 4 * D], F32)
    scr = sb("scr", [128, D], F32)
    urt = sb("urt", [128, BS], BF16)
    usb = [urt[0:64, :], urt[64:128, :]]
    recipb = [ybuf[0:1, 256 * i:256 * (i + 1)].bitcast(BF16) for i in range(2)]
    tri0 = sb("tri0s", [128, 128], BF16)
    tri4 = sb("tri4s", [128, 128], BF16)
    onesb = sb("onesb", [1, 128], BF16)
    neg4 = sb("neg4", [128, 1], F32)
    bqs = sb("bqs", [128, 8], F32)
    bks = sb("bks", [128, 8], F32)
    bvr = sb("bvr", [1, D], BF16)
    bor = sb("bor", [1, D], BF16)
    mixcol = sb("mixcol", [128, 1], BF16)
    relu_t = urt
    mur_f = sb("mur_f", [128, 8], F32)
    muk_f = sb("muk_f", [128, 8], F32)
    murb = sb("murb", [128, 8], BF16)
    mukb = sb("mukb", [128, 8], BF16)
    h2last = sb("h2last", [128, 8], BF16)
    ssum = sb("ssum", [128, 1], F32)
    nmt = sb("nmt", [128, 1], F32)
    svt = sb("svt", [128, 1], F32)
    vart = sb("vart", [128, 1], F32)
    sdt = sb("sdt", [128, 1], F32)
    rst = sb("rst", [128, 1], F32)

    pb = [st.enter_context(nc.psum_tensor("pb%d" % i, [128, BS], F32)) for i in range(8)]

    semnames = ["cT", "cV", "cS", "cG", "const", "pr0", "pr1", "x0", "x1",
                "cos0", "cos1", "sin0", "sin1", "w0", "w1", "hfm", "hw", "z", "wbf"]
    semh = {n: st.enter_context(nc.semaphore(n)) for n in semnames}

    sch = Sched()
    bank_free = [None] * 8
    buf = {}

    def dma(eng, stream, outap, inap, waits=()):
        return sch.op(eng, lambda e, o=outap, i=inap: e.dma_start(out=o, in_=i),
                      waits=waits, sem=stream, amt=16, dma=True)

    def dmat(eng, stream, outap, inap, waits=()):
        return sch.op(eng, lambda e, o=outap, i=inap: e.dma_start_transpose(o, i),
                      waits=waits, sem=stream, amt=16, dma=True)

    csem = {"tensor": "cT", "vector": "cV", "scalar": "cS", "gpsimd": "cG"}

    def cop(eng, fn, waits=(), tok=False):
        return sch.op(eng, fn, waits=waits, sem=csem[eng] if tok else None)

    dma("sync", "const", tri0[:, :], tri0_d[:, :])
    dma("sync", "const", tri4[:, :], tri4_d[:, :])
    dma("sync", "const", bqs[:, :], bq_d[:, :])
    dma("sync", "const", bks[:, :], bk_d[:, :])
    dma("sync", "const", scr[0:1, :], bv_d[:, :])
    dma("sync", "const", aof[0][0:1, :], bo_d[:, :])
    dma("sync", "const", mur_f[:, :], mur_d[:, :])
    t_const = dma("sync", "const", muk_f[:, :], muk_d[:, :])
    t_t0 = t_t4 = t_bq = t_bk = t_bv = t_bo = t_mr = t_mk = t_const
    t_small = cop("vector", lambda e: e.memset(neg4[:, :], -4.0), tok=True)
    t_ones = cop("vector", lambda e: e.memset(onesb[:, :], 1.0), tok=True)
    t_bvc = cop("vector", lambda e: e.tensor_copy(bvr[:, :], scr[0:1, :]), waits=[t_bv], tok=True)
    t_boc = cop("vector", lambda e: e.tensor_copy(bor[:, :], aof[0][0:1, :]), waits=[t_bo], tok=True)
    t_mrc = cop("vector", lambda e: e.tensor_copy(murb[:, :], mur_f[:, :]), waits=[t_mr], tok=True)
    t_mkc = cop("vector", lambda e: e.tensor_copy(mukb[:, :], muk_f[:, :]), waits=[t_mk], tok=True)
    t_vones = []
    for b in range(2):
        for tc in range(4):
            ap = bass.AP(vext[b][:, :].tensor, 1040 * tc + 64, [[4160, 128], [65, 16]])
            t_vones.append(cop("vector", lambda e, a=ap: e.memset(a, 1.0), tok=True))
    t_h2l = cop("vector", lambda e: e.memset(h2last[:, :], 0.0), tok=True)
    t_pk0 = cop("vector", lambda e: e.memset(pk2[0][:, :], 0.0), tok=True)
    t_pk1 = cop("vector", lambda e: e.memset(pk2[1][:, :], 0.0), tok=True)
    buf[("pk", 0)] = t_pk0
    buf[("pk", 1)] = t_pk1

    stg = [scr, aof[0]]
    t_stg = [t_bvc, t_boc]
    t_wres = {}
    t_wscr = {}
    prev_ws_write = {0: None, 1: None}
    kk = 0
    for wn in wnames:
        toks = []
        for ci in range(8):
            sl = kk % 2
            tl = dma("sync", "pr%d" % sl, stg[sl][:, :], wdram[wn][128 * ci:128 * (ci + 1), :],
                     waits=[t_stg[sl]])
            if wn in W:
                tc = cop("vector", lambda e, w=W[wn], c=ci, s=stg[sl]: e.tensor_copy(
                    w[:, D * c:D * (c + 1)], s[:, :]), waits=[tl], tok=True)
            else:
                di = 0 if wn != "wv2" else 1
                dst = WS[di]
                tc = cop("vector", lambda e, w=dst, c=ci, s=stg[sl]: e.tensor_copy(
                    w[:, D * c:D * (c + 1)], s[:, :]),
                    waits=[tl, prev_ws_write[di]], tok=True)
                tw = dma("scalar", "wbf", wbf[wn][ci, :, :], dst[:, D * ci:D * (ci + 1)], waits=[tc])
                toks.append(tw)
            t_stg[sl] = tc
            kk += 1
        if wn in W:
            t_wres[wn] = t_stg[(kk - 1) % 2]
        else:
            t_wscr[wn] = toks
            prev_ws_write[0 if wn != "wv2" else 1] = toks[-1]
    t_wscr_all = t_wscr["wv2"][-1]
    for wn in t_wscr:
        t_wscr[wn] = [t_wscr_all]
    buf["WS0"] = t_wscr_all
    buf["WS1"] = t_wscr_all
    buf["scr"] = t_stg[0]
    buf[("aof", 0)] = t_stg[1]

    def ln_stats(xap, waits):
        t1 = cop("vector", lambda e: e.reduce_sum(ssum[:, :], xap, axis=mybir.AxisListType.X),
                 waits=list(waits), tok=True)
        cop("vector", lambda e: e.tensor_scalar(nmt[:, :], ssum[:, :], -1.0 / D, None, ALU.mult))
        cop("vector", lambda e: e.scalar_tensor_tensor(
            scr[:, :], xap, nmt[:, :], xap, ALU.add, ALU.mult, accum_out=svt[:, :]),
            waits=[buf.get("scr")])
        t4 = cop("vector", lambda e: e.tensor_scalar(vart[:, :], svt[:, :], 1.0 / D, EPS,
                                                     ALU.mult, ALU.add), tok=True)
        buf["scr"] = t4
        t5 = cop("scalar", lambda e: e.activation(out=sdt[:, :], in_=vart[:, :], func=AF.Sqrt),
                 waits=[t4], tok=True)
        t6 = cop("vector", lambda e: e.reciprocal(rst[:, :], sdt[:, :]), waits=[t5], tok=True)
        return t6

    def stream_weight(wn, slot, after):
        ws = [buf.get("WS%d" % slot)] + t_wscr[wn] + list(after)
        t = dma("gpsimd", "w%d" % slot, WS[slot][:, :].rearrange("p (c d) -> p c d", c=8),
                wbf[wn][:, :, :].rearrange("c p d -> p c d"), waits=ws)
        buf["WS%d" % slot] = None
        return t

    t_z_stores = []
    x_free = [None, None]
    prev_z = None
    kv_cons = [None, None]
    kv_tok = [{}, {}]
    h2l_tok = t_h2l
    h2l_cons = None
    next_wo = next_wr = None
    buf["qx"] = None
    buf["ax"] = None
    buf["hfm"] = None
    buf["htm"] = None
    buf["pk"] = None

    def planA(it2):
        rows0a = (it2 // NB) * S + (it2 % NB) * BS
        t_ln = [None] * 4
        t_x1 = [None] * 4
        for j in range(4):
            sl = j % 2
            w = [x_free[sl]] if j < 2 else [t_ln[j - 2]]
            t_x1[j] = dma("sync", "x%d" % sl, xsl[sl][:, :],
                          xin[rows0a + 128 * j: rows0a + 128 * (j + 1), :], waits=w)
            t_rs = ln_stats(xsl[sl][:, :], [t_x1[j]])
            t_ln[j] = cop("vector", lambda e, sl=sl, j=j: e.tensor_scalar(
                htm[:, D * j:D * (j + 1)], xsl[sl][:, :], nmt[:, :], rst[:, :], ALU.add, ALU.mult),
                waits=[t_rs, buf.get("htm")], tok=True)
        t_hw = [dma("scalar", "hw", hstage[128 * j:128 * (j + 1), :], htm[:, D * j:D * (j + 1)],
                    waits=[t_ln[j]]) for j in range(4)]
        t_hr = [dmat("sync", "hfm", hfm[:, BS * c:BS * (c + 1)], hstage[:, 128 * c:128 * (c + 1)],
                     waits=[t_hw[-1], buf.get("hfm")]) for c in range(8)]
        return t_ln, t_hw, [t_hr[-1]] * 8

    pendingA = None
    for it in range(niter):
        s, n = it // NB, it % NB
        par = n % 2
        rows0 = s * S + n * BS

        if it == 0:
            t_wo = stream_weight("wo", 0, [])
            t_wr = stream_weight("wr", 1, [])
            t_ln, t_hw, t_hr = planA(0)
        else:
            t_wo, t_wr = next_wo, next_wr
            t_ln, t_hw, t_hr = pendingA

        t_cos = {}

        def load_cos(p, n):
            slp = p % 2
            tc1 = dma("sync", "cos%d" % slp, cosb[slp][:, :], cos_d[p, :, n * BS:(n + 1) * BS],
                      waits=[buf.get(("cos", slp))])
            tc2 = dma("sync", "sin%d" % slp, sinb[slp][:, :], sin_d[p, :, n * BS:(n + 1) * BS],
                      waits=[buf.get(("cos", slp))])
            t_cos[p] = (tc1, tc2)

        def proj_pair(wsb, wwait, dst, dst_free, bias_sb, p, banks):
            mm_toks = []
            for bi, dc in enumerate((p, p + 4)):
                bk = banks[bi]
                def mmgroup(e, bk=bk, dc=dc, wsb=wsb):
                    for ci in range(8):
                        ins = e.matmul(pb[bk][:, :],
                                       wsb[:, D * ci + 128 * dc:D * ci + 128 * (dc + 1)],
                                       hfm[:, BS * ci:BS * (ci + 1)],
                                       start=(ci == 0), stop=(ci == 7))
                    return ins
                mm_toks.append(cop("tensor", mmgroup,
                                   waits=[bank_free[bk], wwait] + t_hr, tok=True))
            slp = p % 2
            blo, bhi = banks
            cw = [t_cos[p][0], t_cos[p][1], t_bq, t_bk, buf.get("scr"), dst_free] + mm_toks
            cop("vector", lambda e: e.scalar_tensor_tensor(
                scr[:, 0:BS], pb[blo][:, :], bias_sb[:, p:p + 1], cosb[slp][:, :],
                ALU.add, ALU.mult), waits=cw)
            cop("vector", lambda e: e.scalar_tensor_tensor(
                scr[:, BS:2 * BS], pb[bhi][:, :], bias_sb[:, p + 4:p + 5], sinb[slp][:, :],
                ALU.add, ALU.mult))
            t3 = cop("vector", lambda e: e.tensor_sub(
                dst[:, BS * p:BS * (p + 1)], scr[:, 0:BS], scr[:, BS:2 * BS]), tok=True)
            cop("vector", lambda e: e.scalar_tensor_tensor(
                scr[:, 0:BS], pb[blo][:, :], bias_sb[:, p:p + 1], sinb[slp][:, :],
                ALU.add, ALU.mult))
            t5 = cop("vector", lambda e: e.scalar_tensor_tensor(
                scr[:, BS:2 * BS], pb[bhi][:, :], bias_sb[:, p + 4:p + 5], cosb[slp][:, :],
                ALU.add, ALU.mult), tok=True)
            t6 = cop("vector", lambda e: e.tensor_add(
                dst[:, BS * (p + 4):BS * (p + 5)], scr[:, 0:BS], scr[:, BS:2 * BS]), tok=True)
            bank_free[blo] = t5
            bank_free[bhi] = t5
            buf["scr"] = t6
            buf[("cos", slp)] = t6
            return [t3, t6]

        q_toks = []
        k_toks = []
        load_cos(0, n)
        load_cos(1, n)
        for p in range(4):
            q_toks += proj_pair(W["wq"], t_wres["wq"], qx, buf.get("qx"), bqs, p,
                                (0, 1) if p % 2 == 0 else (2, 3))
            k_toks += proj_pair(W["wk"], t_wres["wk"], krot[par], kv_cons[par], bks, p,
                                (4, 5) if p % 2 == 0 else (6, 7))
            if p + 2 <= 3:
                load_cos(p + 2, n)
        kv_tok[par]["k"] = list(k_toks)

        v_toks = []
        for tj in range(4):
            for hh in range(2):
                bk = (tj * 2 + hh) % 2
                def vmm(e, bk=bk, tj=tj, hh=hh):
                    for ci in range(8):
                        e.matmul(pb[bk][:, :],
                                 hfm[:, BS * ci + 128 * tj:BS * ci + 128 * (tj + 1)],
                                 W["wv"][:, D * ci + BS * hh:D * ci + BS * (hh + 1)],
                                 start=(ci == 0), stop=False)
                    return e.matmul(pb[bk][:, :], onesb[:, :], bvr[:, BS * hh:BS * (hh + 1)],
                                    start=False, stop=True)
                tmm = cop("tensor", vmm, waits=[bank_free[bk], t_wres["wv"], t_bvc, t_ones,
                                                kv_cons[par]], tok=True)
                oap = bass.AP(vext[par][:, :].tensor, 1040 * tj + 520 * hh,
                              [[4160, 128], [65, 8], [1, 64]])
                tev = cop("scalar", lambda e, oap=oap, bk=bk: e.copy(oap, pb[bk][:, :]),
                          waits=[tmm] + t_vones, tok=True)
                bank_free[bk] = tev
                v_toks.append(tev)
        kv_tok[par]["v"] = v_toks

        sc_banks = [0, 1, 2]
        t_attn_evac = []
        t_sc_all = []
        chunks = [c for c in range(8) if valid_arange(n, c) is not None]
        av_segs = []
        hi_mark = None
        for c in chunks:
            alo, ahi = valid_arange(n, c)
            if hi_mark is None:
                av_segs.append((c, alo, ahi))
                hi_mark = ahi
            else:
                if alo <= hi_mark:
                    av_segs.append((c, alo, min(ahi, hi_mark)))
                if ahi > hi_mark:
                    av_segs.append((c, hi_mark + 1, ahi))
                    hi_mark = ahi
        head_order = [0, 1, 8, 9, 2, 3, 10, 11, 4, 5, 12, 13, 6, 7, 14, 15]
        hstate = {}

        def plan_scores(h):
            md, po = h // 2, 64 * (h % 2)
            qdep = q_toks[2 * md] if md < 4 else q_toks[2 * (md - 4) + 1]
            kdep = k_toks[2 * md] if md < 4 else k_toks[2 * (md - 4) + 1]
            pku = pk2[h % 2]
            exp_toks = []
            for idx, c in enumerate(chunks):
                alo, ahi = valid_arange(n, c)
                qlo, qhi = 128 * alo, 128 * (ahi + 1)
                bk = sc_banks[idx % 3]
                src_par = par if c >= 4 else 1 - par
                tc = c % 4
                ksrc = krot[src_par]
                kw = [kdep] if src_par == par else kv_tok[src_par]["k"]
                tmm = cop("tensor", lambda e, bk=bk, ksrc=ksrc, md=md, po=po, tc=tc,
                          qlo=qlo, qhi=qhi: e.matmul(
                    pb[bk][:, qlo:qhi],
                    ksrc[po:po + 64, BS * md + 128 * tc:BS * md + 128 * (tc + 1)],
                    qx[po:po + 64, BS * md + qlo:BS * md + qhi],
                    start=True, stop=True),
                    waits=[bank_free[bk], qdep] + kw, tok=True)
                t_sc_all.append(tmm)
                texp = cop("scalar", lambda e, bk=bk, c=c, qlo=qlo, qhi=qhi, pku=pku: e.activation(
                    out=pku[:, BS * c + qlo:BS * c + qhi], in_=pb[bk][:, qlo:qhi],
                    func=AF.Exp, bias=neg4[:, :], scale=0.125),
                    waits=[tmm, t_small, buf.get(("pk", h % 2))], tok=True)
                exp_toks.append(texp)
                bank_free[bk] = texp
            gm = []
            pkt = pku[:, :].tensor
            if n > 0:
                ap = bass.AP(pkt, 0, [[8 * BS, 128], [640, 4], [1, 128]])
                m0 = bass.AP(tri0[:, :].tensor, 0, [[128, 128], [0, 4], [1, 128]])
                gm.append(cop("gpsimd", lambda e, ap=ap, m0=m0: e.tensor_mul(ap, ap, m0),
                              waits=[exp_toks[3], t_t0], tok=True))
            ap4 = bass.AP(pkt, 2048, [[8 * BS, 128], [640, 4], [1, 128]])
            m4 = bass.AP(tri4[:, :].tensor, 0, [[128, 128], [0, 4], [1, 128]])
            gm.append(cop("gpsimd", lambda e, ap=ap4, m4=m4: e.tensor_mul(ap, ap, m4),
                          waits=[exp_toks[-1], t_t4], tok=True))
            hstate[h] = (exp_toks, gm, pku)

        tails = {}

        def plan_av(h):
            exp_toks, gm, pku = hstate.pop(h)
            ub = 3 + h % 3
            def avgroup(e, ub=ub, par=par, h=h, pku=pku, segs=tuple(av_segs)):
                for i, (c, alo, ahi) in enumerate(segs):
                    src_par = par if c >= 4 else 1 - par
                    tc = c % 4
                    ins = e.matmul(pb[ub][0:65, 128 * alo:128 * (ahi + 1)],
                                   vext[src_par][:, 1040 * tc + 65 * h:1040 * tc + 65 * (h + 1)],
                                   pku[:, BS * c + 128 * alo:BS * c + 128 * (ahi + 1)],
                                   start=(i == 0), stop=(i == len(segs) - 1))
                return ins
            avw = [bank_free[ub]] + exp_toks + gm + kv_tok[par]["v"] + \
                  (kv_tok[1 - par].get("v", []) if n > 0 else [])
            t_av = cop("tensor", avgroup, waits=avw, tok=True)
            buf[("pk", h % 2)] = t_av
            uslot = h % 2
            t_uc = cop("scalar", lambda e, u=usb[uslot], ub=ub: e.copy(u, pb[ub][0:64, :]),
                       waits=[t_av, buf.get(("usb", uslot))], tok=True)
            t_rc = cop("vector", lambda e, rb=recipb[uslot], ub=ub: e.reciprocal(
                rb, pb[ub][64:65, :]), waits=[t_av, buf.get(("recip", uslot)), prev_z], tok=True)
            tails[h] = (ub, uslot, t_uc, t_rc)
            return t_av

        def plan_norm(h):
            md, po = h // 2, 64 * (h % 2)
            ub, uslot, t_uc, t_rc = tails.pop(h)
            bb = 6 + h % 2
            t_bc = cop("tensor", lambda e, bb=bb, rb=recipb[uslot]: e.matmul(
                pb[bb][0:64, :], onesb[:, 0:64], rb, start=True, stop=True),
                waits=[t_rc, bank_free[bb], t_ones], tok=True)
            buf[("recip", uslot)] = t_bc
            t_ae = cop("vector", lambda e, md=md, po=po, bb=bb, u=usb[uslot]: e.tensor_mul(
                ax[po:po + 64, BS * md:BS * (md + 1)], u, pb[bb][0:64, :]),
                waits=[t_bc, t_uc, buf.get("ax")], tok=True)
            bank_free[bb] = t_ae
            buf[("usb", uslot)] = t_ae
            bank_free[ub] = t_ae
            t_attn_evac.append(t_ae)

        t_av_last = None
        for i in range(NH + 2):
            if i < NH:
                plan_scores(head_order[i])
            if 1 <= i <= NH:
                t_av_last = plan_av(head_order[i - 1])
            if i >= 2:
                plan_norm(head_order[i - 2])
        kv_cons[0] = kv_cons[1] = t_av_last

        wo_users = []
        t_y = [None] * 4
        t_x2 = [None] * 4
        for j in range(2):
            t_x2[j] = dma("sync", "x%d" % (j % 2), xsl[j % 2][:, :],
                          xin[rows0 + 128 * j: rows0 + 128 * (j + 1), :],
                          waits=[t_ln[j + 2]])
        for tj in range(4):
            t_oev = []
            for hh in range(2):
                bk = hh
                def omm(e, bk=bk, tj=tj, hh=hh):
                    for dc in range(8):
                        e.matmul(pb[bk][:, :], ax[:, BS * dc + 128 * tj:BS * dc + 128 * (tj + 1)],
                                 WS[0][:, D * dc + BS * hh:D * dc + BS * (hh + 1)],
                                 start=(dc == 0), stop=False)
                    return e.matmul(pb[bk][:, :], onesb[:, :], bor[:, BS * hh:BS * hh + BS],
                                    start=False, stop=True)
                tmm = cop("tensor", omm, waits=[bank_free[bk], t_wo, t_boc] + t_attn_evac,
                          tok=True)
                wo_users.append(tmm)
                sl = tj % 2
                tev = cop("scalar", lambda e, sl=sl, hh=hh, bk=bk: e.copy(
                    aof[sl][:, BS * hh:BS * (hh + 1)], pb[bk][:, :]),
                    waits=[tmm, buf.get(("aof", sl))], tok=True)
                t_oev.append(tev)
                bank_free[bk] = tev
            sl = tj % 2
            t_rs = ln_stats(aof[sl][:, :], t_oev)
            tt1 = cop("vector", lambda e, sl=sl: e.tensor_scalar(
                scr[:, :], aof[sl][:, :], nmt[:, :], rst[:, :], ALU.add, ALU.mult),
                waits=[t_rs, buf.get("scr")], tok=True)
            buf[("aof", sl)] = tt1
            t_y[tj] = cop("gpsimd", lambda e, sl=sl, tj=tj: e.tensor_add(
                ybuf[:, D * tj:D * (tj + 1)], scr[:, :], xsl[sl][:, :]),
                waits=[tt1, t_x2[tj], prev_z], tok=True)
            buf["scr"] = t_y[tj]
            x_free[sl] = t_y[tj]
            if tj < 2:
                t_x2[tj + 2] = dma("sync", "x%d" % sl, xsl[sl][:, :],
                                   xin[rows0 + 128 * (tj + 2): rows0 + 128 * (tj + 3), :],
                                   waits=[t_y[tj]])
        t_h2 = []
        for j in range(4):
            t_rs = ln_stats(ybuf[:, D * j:D * (j + 1)], [t_y[j]])
            t_h2.append(cop("vector", lambda e, j=j: e.tensor_scalar(
                htm[:, D * j:D * (j + 1)], ybuf[:, D * j:D * (j + 1)], nmt[:, :], rst[:, :],
                ALU.add, ALU.mult), waits=[t_rs, t_hw[-1]], tok=True))
        t_h2w = [dma("scalar", "hw", h2stage[128 * j:128 * (j + 1), :], htm[:, D * j:D * (j + 1)],
                     waits=[t_h2[j]]) for j in range(4)]
        buf["htm"] = t_h2w[-1]
        t_h2r = [dmat("sync", "hfm", hfm[:, BS * c:BS * (c + 1)], h2stage[:, 128 * c:128 * (c + 1)],
                      waits=[t_h2w[-1]] + q_toks + k_toks + v_toks + t_sc_all) for c in range(8)]
        t_h2r = [t_h2r[-1]] * 8

        t_wk2 = stream_weight("wk2", 0, wo_users)
        t_mixr = []
        t_mixk = []
        for ci in range(8):
            off = BS * ci
            tdc = cop("gpsimd", lambda e, off=off: e.tensor_sub(
                pk2[1][:, 0:BS - 1], hfm[:, off + 1:off + BS], hfm[:, off:off + BS - 1]),
                waits=t_h2r + [buf.get(("pk", 1)), h2l_cons,
                               t_mixk[-2] if t_mixk else None], tok=True)
            txr = cop("vector", lambda e, off=off, ci=ci: e.scalar_tensor_tensor(
                qx[:, off + 1:off + BS], pk2[1][:, 0:BS - 1], murb[:, ci:ci + 1],
                hfm[:, off:off + BS - 1], ALU.mult, ALU.add),
                waits=[tdc, t_mrc] + t_sc_all, tok=True)
            txk = cop("vector", lambda e, off=off, ci=ci: e.scalar_tensor_tensor(
                ax[:, off + 1:off + BS], pk2[1][:, 0:BS - 1], mukb[:, ci:ci + 1],
                hfm[:, off:off + BS - 1], ALU.mult, ALU.add),
                waits=[t_mkc, tdc] + wo_users, tok=True)
            tcol = cop("gpsimd", lambda e, off=off, ci=ci: e.tensor_sub(
                mixcol[:, :], hfm[:, off:off + 1], h2last[:, ci:ci + 1]),
                waits=[h2l_tok, t_mixk[-1] if t_mixk else None], tok=True)
            tr0 = cop("vector", lambda e, off=off, ci=ci: e.scalar_tensor_tensor(
                qx[:, off:off + 1], mixcol[:, :], murb[:, ci:ci + 1],
                h2last[:, ci:ci + 1], ALU.mult, ALU.add),
                waits=[tcol], tok=True)
            tk0 = cop("vector", lambda e, off=off, ci=ci: e.scalar_tensor_tensor(
                ax[:, off:off + 1], mixcol[:, :], mukb[:, ci:ci + 1],
                h2last[:, ci:ci + 1], ALU.mult, ALU.add),
                waits=[tcol], tok=True)
            t_mixr += [txr, tr0]
            t_mixk += [txk, tk0]
        buf["pk"] = t_mixk[-1]
        lastap = bass.AP(hfm[:, :].tensor, BS - 1, [[8 * BS, 128], [BS, 8]])
        if n == NB - 1:
            h2l_tok = cop("vector", lambda e: e.memset(h2last[:, :], 0.0),
                          waits=t_mixk, tok=True)
        else:
            h2l_tok = cop("vector", lambda e, a=lastap: e.tensor_copy(h2last[:, :], a),
                          waits=t_mixk, tok=True)

        buf["hfm"] = h2l_tok
        if it + 1 < niter:
            pendingA = planA(it + 1)


        wr_users = []
        t_rsg = []
        for tj in range(4):
            tms = []
            for hh in range(2):
                bk = 2 + hh
                def rmm(e, bk=bk, tj=tj, hh=hh):
                    for ci in range(8):
                        ins = e.matmul(pb[bk][:, :],
                                       qx[:, BS * ci + 128 * tj:BS * ci + 128 * (tj + 1)],
                                       WS[1][:, D * ci + BS * hh:D * ci + BS * (hh + 1)],
                                       start=(ci == 0), stop=(ci == 7))
                    return ins
                tms.append(cop("tensor", rmm, waits=[bank_free[bk], t_wr] + t_mixr, tok=True))
            wr_users += tms
            for hh in range(2):
                bk = 2 + hh
                rap = bass.AP(qx[:, :].tensor, 2048 * hh + 128 * tj,
                              [[4096, 128], [512, 4], [1, 128]])
                trs = cop("scalar", lambda e, bk=bk, rap=rap: e.activation(
                    out=rap, in_=pb[bk][:, :], func=AF.Sigmoid),
                    waits=tms + [buf.get(("r", tj, hh))], tok=True)
                t_rsg.append(trs)
                bank_free[bk] = trs

        t_wv2 = stream_weight("wv2", 1, wr_users)

        wk2_users = []
        t_ka = []
        for dk in range(8):
            bk = 4 + dk % 2
            def kmm(e, bk=bk, dk=dk):
                for ci in range(8):
                    ins = e.matmul(pb[bk][:, :],
                                   WS[0][:, D * ci + 128 * dk:D * ci + 128 * (dk + 1)],
                                   ax[:, BS * ci:BS * (ci + 1)],
                                   start=(ci == 0), stop=(ci == 7))
                return ins
            tmm = cop("tensor", kmm, waits=[bank_free[bk], t_wk2] + t_mixk, tok=True)
            wk2_users.append(tmm)
            trel = cop("scalar", lambda e, bk=bk: e.activation(
                out=relu_t[:, :], in_=pb[bk][:, :], func=AF.Relu),
                waits=[tmm, buf.get("relu_t"), t_attn_evac[-1], t_attn_evac[-2]], tok=True)
            tka = cop("vector", lambda e, dk=dk: e.tensor_mul(
                pk2[0][:, BS * dk:BS * (dk + 1)], relu_t[:, :], relu_t[:, :]),
                waits=[trel, buf.get(("pk", 0))] + t_mixk + t_mixr, tok=True)
            buf["relu_t"] = tka
            t_ka.append(tka)
            bank_free[bk] = trel

        wv2_users = []
        t_fe_all = []
        for tj in range(4):
            t_fev = []
            for hh in range(2):
                bk = 6 + hh
                def v2mm(e, bk=bk, tj=tj, hh=hh):
                    for dk in range(8):
                        ins = e.matmul(pb[bk][:, :],
                                       pk2[0][:, BS * dk + 128 * tj:BS * dk + 128 * (tj + 1)],
                                       WS[1][:, D * dk + BS * hh:D * dk + BS * (hh + 1)],
                                       start=(dk == 0), stop=(dk == 7))
                    return ins
                tmm = cop("tensor", v2mm, waits=[bank_free[bk], t_wv2] + t_ka, tok=True)
                wv2_users.append(tmm)
                sl = tj % 2
                rap = bass.AP(qx[:, :].tensor, 2048 * hh + 128 * tj,
                              [[4096, 128], [512, 4], [1, 128]])
                pin = bass.AP(pb[bk][:, :].tensor, 0, [[BS, 128], [128, 4], [1, 128]])
                tfe = cop("vector", lambda e, sl=sl, hh=hh, pin=pin, rap=rap: e.tensor_mul(
                    aof[sl][:, BS * hh:BS * (hh + 1)], pin, rap),
                    waits=[tmm, buf.get(("aof", sl))] + t_rsg, tok=True)
                t_fev.append(tfe)
                t_fe_all.append(tfe)
                buf[("r", tj, hh)] = tfe
                bank_free[bk] = tfe
            sl = tj % 2
            t_rs = ln_stats(aof[sl][:, :], t_fev)
            tt1 = cop("vector", lambda e, sl=sl: e.tensor_scalar(
                scr[:, :], aof[sl][:, :], nmt[:, :], rst[:, :], ALU.add, ALU.mult),
                waits=[t_rs, buf.get("scr")], tok=True)
            buf[("aof", sl)] = tt1
            tz = cop("gpsimd", lambda e, sl=sl, tj=tj: e.tensor_add(
                ybuf[:, D * tj:D * (tj + 1)], scr[:, :], ybuf[:, D * tj:D * (tj + 1)]),
                waits=[tt1], tok=True)
            buf["scr"] = tz
            if tj == 3:
                buf[("pk", 1)] = tt1
            tzd = dma("scalar", "z", out[rows0 + 128 * tj: rows0 + 128 * (tj + 1), :],
                      ybuf[:, D * tj:D * (tj + 1)], waits=[tz])
            t_z_stores.append(tzd)
        h2l_cons = t_mixk[-1]
        prev_z = t_z_stores[-1]
        buf["htm"] = t_h2w[-1]
        buf["hfm"] = h2l_tok
        buf["qx"] = t_fe_all[-1]
        buf["ax"] = wk2_users[-1]
        buf[("pk", 0)] = wv2_users[-1]
        buf[("usb", 0)] = t_ka[-1]
        buf[("usb", 1)] = t_ka[-1]

        if it + 1 < niter:
            next_wo = stream_weight("wo", 0, wk2_users)
            next_wr = stream_weight("wr", 1, wv2_users)

    total_z = sch.sems.get("z", 0)
    sch.op("sync", lambda e: e.wait_ge(semh["z"], total_z))

    with nc.allow_low_precision(reason="bf16 matmul operand production"):
        with nc.Block() as block:
            @block.sync
            def _(eng):
                sch.emit("sync", eng, semh)

            @block.tensor
            def _(eng):
                sch.emit("tensor", eng, semh)

            @block.vector
            def _(eng):
                sch.emit("vector", eng, semh)

            @block.scalar
            def _(eng):
                sch.emit("scalar", eng, semh)

            @block.gpsimd
            def _(eng):
                sch.emit("gpsimd", eng, semh)

    st.close()
    return nc


def make_consts():
    half = D // 2
    j = np.arange(half, dtype=np.float64)
    inv = 1.0 / (10000.0 ** (j / half))
    t = np.arange(S, dtype=np.float64)
    ang = t[None, :] * inv[:, None]
    cos = np.cos(ang).astype(np.float32).reshape(4, 128, S)
    sin = np.sin(ang).astype(np.float32).reshape(4, 128, S)
    iden = np.eye(128, dtype=ml_dtypes.bfloat16)
    ii = np.arange(128)
    tri0 = np.where(ii[:, None] >= ii[None, :], 1.0, 0.0).astype(ml_dtypes.bfloat16)
    tri4 = np.where(ii[:, None] <= ii[None, :], 1.0, 0.0).astype(ml_dtypes.bfloat16)
    return cos, sin, iden, tri0, tri4


def make_inmap(shard, weights):
    cos, sin, iden, tri0, tri4 = make_consts()
    m = {
        "x": np.ascontiguousarray(np.asarray(shard).reshape(NSEQ * S, D)),
        "cosd": cos, "sind": sin, "tri0": tri0, "tri4": tri4,
    }
    (Wq, bq, Wk, bk, Wv, bv, Wo, bo, Wr, Wk2, Wv2, mur, muk) = weights
    for nme, w in [("wq", Wq), ("wk", Wk), ("wv", Wv), ("wo", Wo),
                   ("wr", Wr), ("wk2", Wk2), ("wv2", Wv2)]:
        m[nme] = np.ascontiguousarray(np.asarray(w).T)
    m["bq"] = np.ascontiguousarray(np.asarray(bq).reshape(8, 128).T)
    m["bk"] = np.ascontiguousarray(np.asarray(bk).reshape(8, 128).T)
    m["bv"] = np.ascontiguousarray(np.asarray(bv).reshape(1, D))
    m["bo"] = np.ascontiguousarray(np.asarray(bo).reshape(1, D))
    m["mur"] = np.ascontiguousarray(np.asarray(mur).reshape(8, 128).T)
    m["muk"] = np.ascontiguousarray(np.asarray(muk).reshape(8, 128).T)
    return m


def kernel(L0_attn_hidden_state, L0_attn_Wq, L0_attn_bq, L0_attn_Wk, L0_attn_bk,
           L0_attn_Wv, L0_attn_bv, L0_attn_Wo, L0_attn_bo,
           L0_ffn_W_r, L0_ffn_W_k, L0_ffn_W_v, L0_ffn_mu_r, L0_ffn_mu_k):
    x = np.asarray(L0_attn_hidden_state)
    weights = (L0_attn_Wq, L0_attn_bq, L0_attn_Wk, L0_attn_bk, L0_attn_Wv, L0_attn_bv,
               L0_attn_Wo, L0_attn_bo, L0_ffn_W_r, L0_ffn_W_k, L0_ffn_W_v,
               L0_ffn_mu_r, L0_ffn_mu_k)
    nc = build_core()
    in_maps = [make_inmap(x[4 * i:4 * (i + 1)], weights) for i in range(8)]
    res = run_bass_kernel_spmd(nc, in_maps, core_ids=list(range(8)))
    outs = [np.asarray(r["out"]).reshape(NSEQ, S, D) for r in res.results]
    return np.concatenate(outs, axis=0)


# version 08
# speedup vs baseline: 1.2724x; 1.0062x over previous
import numpy as np
import ml_dtypes
import concourse.bass as bass
import concourse.mybir as mybir
from concourse.bass_utils import run_bass_kernel_spmd
from contextlib import ExitStack

F32 = mybir.dt.float32
BF16 = mybir.dt.bfloat16
AF = mybir.ActivationFunctionType
ALU = mybir.AluOpType

D = 1024
S = 2048
BS = 512
NB = S // BS
NSEQ = 4
NH = 16
NEG = -1.0e38
EPS = 1e-5

ENGINES = ("sync", "tensor", "vector", "scalar", "gpsimd")


class Tok:
    __slots__ = ("sem", "val", "eng", "dma")

    def __init__(self, sem, val, eng, dma):
        self.sem, self.val, self.eng, self.dma = sem, val, eng, dma


OWN_SEM = {"vector": "cV", "scalar": "cS", "gpsimd": "cG"}


class Sched:

    def __init__(self):
        self.sems = {}
        self.ops = {e: [] for e in ENGINES}

    def op(self, eng, fn, waits=(), sem=None, amt=1, dma=False):
        own = OWN_SEM.get(eng)
        if sem is None and own is not None and not dma:
            sem, amt = own, 1
        ws = []
        for w in waits:
            if w is None:
                continue
            if w.eng == eng and not w.dma and w.sem != OWN_SEM.get(eng):
                continue
            if w.eng == eng and w.sem == own:
                continue
            ws.append(w)
        tok = None
        if sem is not None:
            v = self.sems.get(sem, 0) + amt
            self.sems[sem] = v
            tok = Tok(sem, v, eng, dma)
        self.ops[eng].append((fn, ws, sem, amt))
        return tok

    def emit(self, eng_name, eng, semh):
        own = OWN_SEM.get(eng_name)
        waited = {}
        running = {}
        last_own = 0
        for fn, ws, sem, amt in self.ops[eng_name]:
            if own is not None and last_own > 0 and waited.get(own, -1) < last_own:
                eng.wait_ge(semh[own], last_own)
                waited[own] = last_own
            for w in ws:
                if waited.get(w.sem, -1) >= w.val:
                    continue
                eng.wait_ge(semh[w.sem], w.val)
                waited[w.sem] = w.val
            inst = fn(eng)
            if sem is not None:
                running[sem] = running.get(sem, 0) + amt
                inst.then_inc(semh[sem], amt)
                if sem == own:
                    last_own = running[sem]


def valid_arange(n, c):
    if n == 0:
        if c < 4:
            return None
        alo, ahi = c - 4, 3
    else:
        alo, ahi = max(0, c - 4), min(3, c)
    return (alo, ahi) if alo <= ahi else None


def build_core(niter=NSEQ * NB):
    nc = bass.Bass("TRN2", target_bir_lowering=False)
    st = ExitStack()

    xin = nc.dram_tensor("x", [NSEQ * S, D], F32, kind="ExternalInput").ap()
    out = nc.dram_tensor("out", [NSEQ * S, D], F32, kind="ExternalOutput").ap()
    wnames = ["wq", "wk", "wv", "wo", "wr", "wk2", "wv2"]
    wdram = {n: nc.dram_tensor(n, [D, D], F32, kind="ExternalInput").ap() for n in wnames}
    bq_d = nc.dram_tensor("bq", [128, 8], F32, kind="ExternalInput").ap()
    bk_d = nc.dram_tensor("bk", [128, 8], F32, kind="ExternalInput").ap()
    bv_d = nc.dram_tensor("bv", [1, D], F32, kind="ExternalInput").ap()
    bo_d = nc.dram_tensor("bo", [1, D], F32, kind="ExternalInput").ap()
    mur_d = nc.dram_tensor("mur", [128, 8], F32, kind="ExternalInput").ap()
    muk_d = nc.dram_tensor("muk", [128, 8], F32, kind="ExternalInput").ap()
    cos_d = nc.dram_tensor("cosd", [4, 128, S], F32, kind="ExternalInput").ap()
    sin_d = nc.dram_tensor("sind", [4, 128, S], F32, kind="ExternalInput").ap()
    tri0_d = nc.dram_tensor("tri0", [128, 128], BF16, kind="ExternalInput").ap()
    tri4_d = nc.dram_tensor("tri4", [128, 128], BF16, kind="ExternalInput").ap()
    hstage = nc.dram_tensor("hstage", [BS, D], BF16).ap()
    h2stage = nc.dram_tensor("h2stage", [BS, D], BF16).ap()
    wbf = {n: nc.dram_tensor(n + "_bf", [8, 128, D], BF16).ap() for n in ["wo", "wr", "wk2", "wv2"]}

    sb = lambda name, shape, dtype: st.enter_context(nc.sbuf_tensor(name, shape, dtype))
    W = {n: sb("W" + n, [128, 8 * D], BF16) for n in ["wq", "wk", "wv"]}
    WS = [sb("WS%d" % i, [128, 8 * D], BF16) for i in range(2)]
    cosb = [sb("cosb%d" % i, [128, BS], F32) for i in range(2)]
    sinb = [sb("sinb%d" % i, [128, BS], F32) for i in range(2)]
    xsl = [sb("xsl%d" % i, [128, D], F32) for i in range(2)]
    htm = sb("htm", [128, 4 * D], BF16)
    hfm = sb("hfm", [128, 8 * BS], BF16)
    qx = sb("qx", [128, 8 * BS], BF16)
    ax = sb("ax", [128, 8 * BS], BF16)
    krot = [sb("krot%d" % i, [128, 8 * BS], BF16) for i in range(2)]
    vext = [sb("vext%d" % i, [128, 4 * 1040], BF16) for i in range(2)]
    pk2 = [sb("pk%d" % i, [128, 8 * BS], BF16) for i in range(2)]
    pk = pk2[0]
    aof = [sb("aof0", [128, D], F32), pk2[1][:, 2048:4096].bitcast(F32)]
    ybuf = sb("ybuf", [128, 4 * D], F32)
    scr = sb("scr", [128, D], F32)
    urt = sb("urt", [128, BS], BF16)
    usb = [urt[0:64, :], urt[64:128, :]]
    recipb = [ybuf[0:1, 256 * i:256 * (i + 1)].bitcast(BF16) for i in range(2)]
    tri0 = sb("tri0s", [128, 128], BF16)
    tri4 = sb("tri4s", [128, 128], BF16)
    onesb = sb("onesb", [1, 128], BF16)
    neg4 = sb("neg4", [128, 1], F32)
    bqs = sb("bqs", [128, 8], F32)
    bks = sb("bks", [128, 8], F32)
    bvr = sb("bvr", [1, D], BF16)
    bor = sb("bor", [1, D], BF16)
    mixcol = sb("mixcol", [128, 1], BF16)
    relu_t = urt
    mur_f = sb("mur_f", [128, 8], F32)
    muk_f = sb("muk_f", [128, 8], F32)
    murb = sb("murb", [128, 8], BF16)
    mukb = sb("mukb", [128, 8], BF16)
    h2last = sb("h2last", [128, 8], BF16)
    ssum = sb("ssum", [128, 1], F32)
    nmt = sb("nmt", [128, 1], F32)
    svt = sb("svt", [128, 1], F32)
    vart = sb("vart", [128, 1], F32)
    sdt = sb("sdt", [128, 1], F32)
    rst = sb("rst", [128, 1], F32)

    pb = [st.enter_context(nc.psum_tensor("pb%d" % i, [128, BS], F32)) for i in range(8)]

    semnames = ["cT", "cV", "cS", "cG", "const", "pr0", "pr1", "x0", "x1",
                "cos0", "cos1", "sin0", "sin1", "w0", "w1", "hfm", "hw", "z", "wbf"]
    semh = {n: st.enter_context(nc.semaphore(n)) for n in semnames}

    sch = Sched()
    bank_free = [None] * 8
    buf = {}

    def dma(eng, stream, outap, inap, waits=()):
        return sch.op(eng, lambda e, o=outap, i=inap: e.dma_start(out=o, in_=i),
                      waits=waits, sem=stream, amt=16, dma=True)

    def dmat(eng, stream, outap, inap, waits=()):
        return sch.op(eng, lambda e, o=outap, i=inap: e.dma_start_transpose(o, i),
                      waits=waits, sem=stream, amt=16, dma=True)

    csem = {"tensor": "cT", "vector": "cV", "scalar": "cS", "gpsimd": "cG"}

    def cop(eng, fn, waits=(), tok=False):
        return sch.op(eng, fn, waits=waits, sem=csem[eng] if tok else None)

    dma("sync", "const", tri0[:, :], tri0_d[:, :])
    dma("sync", "const", tri4[:, :], tri4_d[:, :])
    dma("sync", "const", bqs[:, :], bq_d[:, :])
    dma("sync", "const", bks[:, :], bk_d[:, :])
    dma("sync", "const", scr[0:1, :], bv_d[:, :])
    dma("sync", "const", aof[0][0:1, :], bo_d[:, :])
    dma("sync", "const", mur_f[:, :], mur_d[:, :])
    t_const = dma("sync", "const", muk_f[:, :], muk_d[:, :])
    t_t0 = t_t4 = t_bq = t_bk = t_bv = t_bo = t_mr = t_mk = t_const
    t_small = cop("vector", lambda e: e.memset(neg4[:, :], -4.0), tok=True)
    t_ones = cop("vector", lambda e: e.memset(onesb[:, :], 1.0), tok=True)
    t_bvc = cop("vector", lambda e: e.tensor_copy(bvr[:, :], scr[0:1, :]), waits=[t_bv], tok=True)
    t_boc = cop("vector", lambda e: e.tensor_copy(bor[:, :], aof[0][0:1, :]), waits=[t_bo], tok=True)
    t_mrc = cop("vector", lambda e: e.tensor_copy(murb[:, :], mur_f[:, :]), waits=[t_mr], tok=True)
    t_mkc = cop("vector", lambda e: e.tensor_copy(mukb[:, :], muk_f[:, :]), waits=[t_mk], tok=True)
    t_vones = []
    for b in range(2):
        for tc in range(4):
            ap = bass.AP(vext[b][:, :].tensor, 1040 * tc + 64, [[4160, 128], [65, 16]])
            t_vones.append(cop("vector", lambda e, a=ap: e.memset(a, 1.0), tok=True))
    t_h2l = cop("vector", lambda e: e.memset(h2last[:, :], 0.0), tok=True)
    t_pk0 = cop("vector", lambda e: e.memset(pk2[0][:, :], 0.0), tok=True)
    t_pk1 = cop("vector", lambda e: e.memset(pk2[1][:, :], 0.0), tok=True)
    buf[("pk", 0)] = t_pk0
    buf[("pk", 1)] = t_pk1

    stg = [scr, aof[0]]
    t_stg = [t_bvc, t_boc]
    t_wres = {}
    t_wscr = {}
    prev_ws_write = {0: None, 1: None}
    kk = 0
    for wn in wnames:
        toks = []
        for ci in range(8):
            sl = kk % 2
            tl = dma("sync", "pr%d" % sl, stg[sl][:, :], wdram[wn][128 * ci:128 * (ci + 1), :],
                     waits=[t_stg[sl]])
            if wn in W:
                tc = cop("vector", lambda e, w=W[wn], c=ci, s=stg[sl]: e.tensor_copy(
                    w[:, D * c:D * (c + 1)], s[:, :]), waits=[tl], tok=True)
            else:
                di = 0 if wn != "wv2" else 1
                dst = WS[di]
                tc = cop("vector", lambda e, w=dst, c=ci, s=stg[sl]: e.tensor_copy(
                    w[:, D * c:D * (c + 1)], s[:, :]),
                    waits=[tl, prev_ws_write[di]], tok=True)
                tw = dma("scalar", "wbf", wbf[wn][ci, :, :], dst[:, D * ci:D * (ci + 1)], waits=[tc])
                toks.append(tw)
            t_stg[sl] = tc
            kk += 1
        if wn in W:
            t_wres[wn] = t_stg[(kk - 1) % 2]
        else:
            t_wscr[wn] = toks
            prev_ws_write[0 if wn != "wv2" else 1] = toks[-1]
    t_wscr_all = t_wscr["wv2"][-1]
    for wn in t_wscr:
        t_wscr[wn] = [t_wscr_all]
    buf["WS0"] = t_wscr_all
    buf["WS1"] = t_wscr_all
    buf["scr"] = t_stg[0]
    buf[("aof", 0)] = t_stg[1]

    lnsink = [None]

    def ln_stats(xap, waits):
        sink = lnsink[0]
        t1 = cop("scalar", lambda e, sink=sink: e.activation(
            out=sink, in_=xap, func=AF.Identity, accum_out=ssum[:, :]),
            waits=list(waits), tok=True)
        t2 = cop("scalar", lambda e, sink=sink: e.activation(
            out=sink, in_=xap, func=AF.Square, accum_out=svt[:, :]), tok=True)
        cop("vector", lambda e: e.tensor_scalar(nmt[:, :], ssum[:, :], -1.0 / D, None, ALU.mult),
            waits=[t1])
        cop("vector", lambda e: e.tensor_scalar(vart[:, :], svt[:, :], 1.0 / D, EPS,
                                                ALU.mult, ALU.add), waits=[t2])
        cop("vector", lambda e: e.tensor_scalar(svt[:, :], nmt[:, :], nmt[:, :], None, ALU.mult))
        t4 = cop("vector", lambda e: e.tensor_sub(vart[:, :], vart[:, :], svt[:, :]), tok=True)
        t5 = cop("scalar", lambda e: e.activation(out=sdt[:, :], in_=vart[:, :], func=AF.Sqrt),
                 waits=[t4], tok=True)
        t6 = cop("vector", lambda e: e.reciprocal(rst[:, :], sdt[:, :]), waits=[t5], tok=True)
        return t6

    def stream_weight(wn, slot, after):
        ws = [buf.get("WS%d" % slot)] + t_wscr[wn] + list(after)
        t = dma("gpsimd", "w%d" % slot, WS[slot][:, :].rearrange("p (c d) -> p c d", c=8),
                wbf[wn][:, :, :].rearrange("c p d -> p c d"), waits=ws)
        buf["WS%d" % slot] = None
        return t

    t_z_stores = []
    x_free = [None, None]
    prev_z = None
    kv_cons = [None, None]
    kv_tok = [{}, {}]
    h2l_tok = t_h2l
    h2l_cons = None
    next_wo = next_wr = None
    buf["qx"] = None
    buf["ax"] = None
    buf["hfm"] = None
    buf["htm"] = None
    buf["pk"] = None

    def planA(it2):
        rows0a = (it2 // NB) * S + (it2 % NB) * BS
        t_ln = [None] * 4
        t_x1 = [None] * 4
        for j in range(4):
            sl = j % 2
            w = [x_free[sl]] if j < 2 else [t_ln[j - 2]]
            t_x1[j] = dma("sync", "x%d" % sl, xsl[sl][:, :],
                          xin[rows0a + 128 * j: rows0a + 128 * (j + 1), :], waits=w)
            t_rs = ln_stats(xsl[sl][:, :], [t_x1[j]])
            t_ln[j] = cop("vector", lambda e, sl=sl, j=j: e.tensor_scalar(
                htm[:, D * j:D * (j + 1)], xsl[sl][:, :], nmt[:, :], rst[:, :], ALU.add, ALU.mult),
                waits=[t_rs, buf.get("htm")], tok=True)
        t_hw = [dma("scalar", "hw", hstage[128 * j:128 * (j + 1), :], htm[:, D * j:D * (j + 1)],
                    waits=[t_ln[j]]) for j in range(4)]
        t_hr = [dmat("sync", "hfm", hfm[:, BS * c:BS * (c + 1)], hstage[:, 128 * c:128 * (c + 1)],
                     waits=[t_hw[-1], buf.get("hfm")]) for c in range(8)]
        return t_ln, t_hw, [t_hr[-1]] * 8

    pendingA = None
    ones_restore = None
    for it in range(niter):
        s, n = it // NB, it % NB
        par = n % 2
        rows0 = s * S + n * BS
        lnsink[0] = vext[1 - par][:, 0:2048].bitcast(F32)
        onesap = bass.AP(vext[par][:, :].tensor, 64, [[4160, 128], [65, 64]])
        ones_restore = cop("vector", lambda e, a=onesap: e.memset(a, 1.0),
                           waits=[kv_cons[par]], tok=True)

        if it == 0:
            t_wo = stream_weight("wo", 0, [])
            t_wr = stream_weight("wr", 1, [])
            t_ln, t_hw, t_hr = planA(0)
        else:
            t_wo, t_wr = next_wo, next_wr
            t_ln, t_hw, t_hr = pendingA

        t_cos = {}

        def load_cos(p, n):
            slp = p % 2
            tc1 = dma("sync", "cos%d" % slp, cosb[slp][:, :], cos_d[p, :, n * BS:(n + 1) * BS],
                      waits=[buf.get(("cos", slp))])
            tc2 = dma("sync", "sin%d" % slp, sinb[slp][:, :], sin_d[p, :, n * BS:(n + 1) * BS],
                      waits=[buf.get(("cos", slp))])
            t_cos[p] = (tc1, tc2)

        def proj_pair(wsb, wwait, dst, dst_free, bias_sb, p, banks):
            mm_toks = []
            for bi, dc in enumerate((p, p + 4)):
                bk = banks[bi]
                def mmgroup(e, bk=bk, dc=dc, wsb=wsb):
                    for ci in range(8):
                        ins = e.matmul(pb[bk][:, :],
                                       wsb[:, D * ci + 128 * dc:D * ci + 128 * (dc + 1)],
                                       hfm[:, BS * ci:BS * (ci + 1)],
                                       start=(ci == 0), stop=(ci == 7))
                    return ins
                mm_toks.append(cop("tensor", mmgroup,
                                   waits=[bank_free[bk], wwait] + t_hr, tok=True))
            slp = p % 2
            blo, bhi = banks
            cw = [t_cos[p][0], t_cos[p][1], t_bq, t_bk, buf.get("scr"), dst_free] + mm_toks
            cop("vector", lambda e: e.scalar_tensor_tensor(
                scr[:, 0:BS], pb[blo][:, :], bias_sb[:, p:p + 1], cosb[slp][:, :],
                ALU.add, ALU.mult), waits=cw)
            cop("vector", lambda e: e.scalar_tensor_tensor(
                scr[:, BS:2 * BS], pb[bhi][:, :], bias_sb[:, p + 4:p + 5], sinb[slp][:, :],
                ALU.add, ALU.mult))
            t3 = cop("vector", lambda e: e.tensor_sub(
                dst[:, BS * p:BS * (p + 1)], scr[:, 0:BS], scr[:, BS:2 * BS]), tok=True)
            cop("vector", lambda e: e.scalar_tensor_tensor(
                scr[:, 0:BS], pb[blo][:, :], bias_sb[:, p:p + 1], sinb[slp][:, :],
                ALU.add, ALU.mult))
            t5 = cop("vector", lambda e: e.scalar_tensor_tensor(
                scr[:, BS:2 * BS], pb[bhi][:, :], bias_sb[:, p + 4:p + 5], cosb[slp][:, :],
                ALU.add, ALU.mult), tok=True)
            t6 = cop("vector", lambda e: e.tensor_add(
                dst[:, BS * (p + 4):BS * (p + 5)], scr[:, 0:BS], scr[:, BS:2 * BS]), tok=True)
            bank_free[blo] = t5
            bank_free[bhi] = t5
            buf["scr"] = t6
            buf[("cos", slp)] = t6
            return [t3, t6]

        q_toks = []
        k_toks = []
        load_cos(0, n)
        load_cos(1, n)
        for p in range(4):
            q_toks += proj_pair(W["wq"], t_wres["wq"], qx, buf.get("qx"), bqs, p,
                                (0, 1) if p % 2 == 0 else (2, 3))
            k_toks += proj_pair(W["wk"], t_wres["wk"], krot[par], kv_cons[par], bks, p,
                                (4, 5) if p % 2 == 0 else (6, 7))
            if p + 2 <= 3:
                load_cos(p + 2, n)
        kv_tok[par]["k"] = list(k_toks)

        v_toks = []
        for tj in range(4):
            for hh in range(2):
                bk = (tj * 2 + hh) % 2
                def vmm(e, bk=bk, tj=tj, hh=hh):
                    for ci in range(8):
                        e.matmul(pb[bk][:, :],
                                 hfm[:, BS * ci + 128 * tj:BS * ci + 128 * (tj + 1)],
                                 W["wv"][:, D * ci + BS * hh:D * ci + BS * (hh + 1)],
                                 start=(ci == 0), stop=False)
                    return e.matmul(pb[bk][:, :], onesb[:, :], bvr[:, BS * hh:BS * (hh + 1)],
                                    start=False, stop=True)
                tmm = cop("tensor", vmm, waits=[bank_free[bk], t_wres["wv"], t_bvc, t_ones,
                                                kv_cons[par]], tok=True)
                oap = bass.AP(vext[par][:, :].tensor, 1040 * tj + 520 * hh,
                              [[4160, 128], [65, 8], [1, 64]])
                tev = cop("scalar", lambda e, oap=oap, bk=bk: e.copy(oap, pb[bk][:, :]),
                          waits=[tmm, ones_restore], tok=True)
                bank_free[bk] = tev
                v_toks.append(tev)
        kv_tok[par]["v"] = v_toks

        sc_banks = [0, 1, 2]
        t_attn_evac = []
        t_sc_all = []
        chunks = [c for c in range(8) if valid_arange(n, c) is not None]
        av_segs = []
        hi_mark = None
        for c in chunks:
            alo, ahi = valid_arange(n, c)
            if hi_mark is None:
                av_segs.append((c, alo, ahi))
                hi_mark = ahi
            else:
                if alo <= hi_mark:
                    av_segs.append((c, alo, min(ahi, hi_mark)))
                if ahi > hi_mark:
                    av_segs.append((c, hi_mark + 1, ahi))
                    hi_mark = ahi
        head_order = [0, 1, 8, 9, 2, 3, 10, 11, 4, 5, 12, 13, 6, 7, 14, 15]
        hstate = {}

        def plan_scores(h):
            md, po = h // 2, 64 * (h % 2)
            qdep = q_toks[2 * md] if md < 4 else q_toks[2 * (md - 4) + 1]
            kdep = k_toks[2 * md] if md < 4 else k_toks[2 * (md - 4) + 1]
            pku = pk2[h % 2]
            exp_toks = []
            for idx, c in enumerate(chunks):
                alo, ahi = valid_arange(n, c)
                qlo, qhi = 128 * alo, 128 * (ahi + 1)
                bk = sc_banks[idx % 3]
                src_par = par if c >= 4 else 1 - par
                tc = c % 4
                ksrc = krot[src_par]
                kw = [kdep] if src_par == par else kv_tok[src_par]["k"]
                tmm = cop("tensor", lambda e, bk=bk, ksrc=ksrc, md=md, po=po, tc=tc,
                          qlo=qlo, qhi=qhi: e.matmul(
                    pb[bk][:, qlo:qhi],
                    ksrc[po:po + 64, BS * md + 128 * tc:BS * md + 128 * (tc + 1)],
                    qx[po:po + 64, BS * md + qlo:BS * md + qhi],
                    start=True, stop=True),
                    waits=[bank_free[bk], qdep] + kw, tok=True)
                t_sc_all.append(tmm)
                texp = cop("scalar", lambda e, bk=bk, c=c, qlo=qlo, qhi=qhi, pku=pku: e.activation(
                    out=pku[:, BS * c + qlo:BS * c + qhi], in_=pb[bk][:, qlo:qhi],
                    func=AF.Exp, bias=neg4[:, :], scale=0.125),
                    waits=[tmm, t_small, buf.get(("pk", h % 2))], tok=True)
                exp_toks.append(texp)
                bank_free[bk] = texp
            gm = []
            pkt = pku[:, :].tensor
            if n > 0:
                ap = bass.AP(pkt, 0, [[8 * BS, 128], [640, 4], [1, 128]])
                m0 = bass.AP(tri0[:, :].tensor, 0, [[128, 128], [0, 4], [1, 128]])
                gm.append(cop("gpsimd", lambda e, ap=ap, m0=m0: e.tensor_mul(ap, ap, m0),
                              waits=[exp_toks[3], t_t0], tok=True))
            ap4 = bass.AP(pkt, 2048, [[8 * BS, 128], [640, 4], [1, 128]])
            m4 = bass.AP(tri4[:, :].tensor, 0, [[128, 128], [0, 4], [1, 128]])
            gm.append(cop("gpsimd", lambda e, ap=ap4, m4=m4: e.tensor_mul(ap, ap, m4),
                          waits=[exp_toks[-1], t_t4], tok=True))
            hstate[h] = (exp_toks, gm, pku)

        tails = {}

        def plan_av(h):
            exp_toks, gm, pku = hstate.pop(h)
            ub = 3 + h % 3
            def avgroup(e, ub=ub, par=par, h=h, pku=pku, segs=tuple(av_segs)):
                for i, (c, alo, ahi) in enumerate(segs):
                    src_par = par if c >= 4 else 1 - par
                    tc = c % 4
                    ins = e.matmul(pb[ub][0:65, 128 * alo:128 * (ahi + 1)],
                                   vext[src_par][:, 1040 * tc + 65 * h:1040 * tc + 65 * (h + 1)],
                                   pku[:, BS * c + 128 * alo:BS * c + 128 * (ahi + 1)],
                                   start=(i == 0), stop=(i == len(segs) - 1))
                return ins
            avw = [bank_free[ub]] + exp_toks + gm + kv_tok[par]["v"] + \
                  (kv_tok[1 - par].get("v", []) if n > 0 else [])
            t_av = cop("tensor", avgroup, waits=avw, tok=True)
            buf[("pk", h % 2)] = t_av
            uslot = h % 2
            t_uc = cop("scalar", lambda e, u=usb[uslot], ub=ub: e.copy(u, pb[ub][0:64, :]),
                       waits=[t_av, buf.get(("usb", uslot))], tok=True)
            t_rc = cop("vector", lambda e, rb=recipb[uslot], ub=ub: e.reciprocal(
                rb, pb[ub][64:65, :]), waits=[t_av, buf.get(("recip", uslot)), prev_z], tok=True)
            tails[h] = (ub, uslot, t_uc, t_rc)
            return t_av

        def plan_norm(h):
            md, po = h // 2, 64 * (h % 2)
            ub, uslot, t_uc, t_rc = tails.pop(h)
            bb = 6 + h % 2
            t_bc = cop("tensor", lambda e, bb=bb, rb=recipb[uslot]: e.matmul(
                pb[bb][0:64, :], onesb[:, 0:64], rb, start=True, stop=True),
                waits=[t_rc, bank_free[bb], t_ones], tok=True)
            buf[("recip", uslot)] = t_bc
            t_ae = cop("vector", lambda e, md=md, po=po, bb=bb, u=usb[uslot]: e.tensor_mul(
                ax[po:po + 64, BS * md:BS * (md + 1)], u, pb[bb][0:64, :]),
                waits=[t_bc, t_uc, buf.get("ax")], tok=True)
            bank_free[bb] = t_ae
            buf[("usb", uslot)] = t_ae
            bank_free[ub] = t_ae
            t_attn_evac.append(t_ae)

        t_av_last = None
        for i in range(NH + 2):
            if i < NH:
                plan_scores(head_order[i])
            if 1 <= i <= NH:
                t_av_last = plan_av(head_order[i - 1])
            if i >= 2:
                plan_norm(head_order[i - 2])
        kv_cons[0] = kv_cons[1] = t_av_last

        wo_users = []
        t_y = [None] * 4
        t_x2 = [None] * 4
        for j in range(2):
            t_x2[j] = dma("sync", "x%d" % (j % 2), xsl[j % 2][:, :],
                          xin[rows0 + 128 * j: rows0 + 128 * (j + 1), :],
                          waits=[t_ln[j + 2]])
        for tj in range(4):
            t_oev = []
            for hh in range(2):
                bk = hh
                def omm(e, bk=bk, tj=tj, hh=hh):
                    for dc in range(8):
                        e.matmul(pb[bk][:, :], ax[:, BS * dc + 128 * tj:BS * dc + 128 * (tj + 1)],
                                 WS[0][:, D * dc + BS * hh:D * dc + BS * (hh + 1)],
                                 start=(dc == 0), stop=False)
                    return e.matmul(pb[bk][:, :], onesb[:, :], bor[:, BS * hh:BS * hh + BS],
                                    start=False, stop=True)
                tmm = cop("tensor", omm, waits=[bank_free[bk], t_wo, t_boc] + t_attn_evac,
                          tok=True)
                wo_users.append(tmm)
                sl = tj % 2
                tev = cop("scalar", lambda e, sl=sl, hh=hh, bk=bk: e.copy(
                    aof[sl][:, BS * hh:BS * (hh + 1)], pb[bk][:, :]),
                    waits=[tmm, buf.get(("aof", sl))], tok=True)
                t_oev.append(tev)
                bank_free[bk] = tev
            sl = tj % 2
            t_rs = ln_stats(aof[sl][:, :], t_oev)
            tt1 = cop("vector", lambda e, sl=sl: e.tensor_scalar(
                scr[:, :], aof[sl][:, :], nmt[:, :], rst[:, :], ALU.add, ALU.mult),
                waits=[t_rs, buf.get("scr")], tok=True)
            buf[("aof", sl)] = tt1
            t_y[tj] = cop("gpsimd", lambda e, sl=sl, tj=tj: e.tensor_add(
                ybuf[:, D * tj:D * (tj + 1)], scr[:, :], xsl[sl][:, :]),
                waits=[tt1, t_x2[tj], prev_z], tok=True)
            buf["scr"] = t_y[tj]
            x_free[sl] = t_y[tj]
            if tj < 2:
                t_x2[tj + 2] = dma("sync", "x%d" % sl, xsl[sl][:, :],
                                   xin[rows0 + 128 * (tj + 2): rows0 + 128 * (tj + 3), :],
                                   waits=[t_y[tj]])
        t_h2 = []
        for j in range(4):
            t_rs = ln_stats(ybuf[:, D * j:D * (j + 1)], [t_y[j]])
            t_h2.append(cop("vector", lambda e, j=j: e.tensor_scalar(
                htm[:, D * j:D * (j + 1)], ybuf[:, D * j:D * (j + 1)], nmt[:, :], rst[:, :],
                ALU.add, ALU.mult), waits=[t_rs, t_hw[-1]], tok=True))
        t_h2w = [dma("scalar", "hw", h2stage[128 * j:128 * (j + 1), :], htm[:, D * j:D * (j + 1)],
                     waits=[t_h2[j]]) for j in range(4)]
        buf["htm"] = t_h2w[-1]
        t_h2r = [dmat("sync", "hfm", hfm[:, BS * c:BS * (c + 1)], h2stage[:, 128 * c:128 * (c + 1)],
                      waits=[t_h2w[-1]] + q_toks + k_toks + v_toks + t_sc_all) for c in range(8)]
        t_h2r = [t_h2r[-1]] * 8

        t_wk2 = stream_weight("wk2", 0, wo_users)
        t_mixr = []
        t_mixk = []
        for ci in range(8):
            off = BS * ci
            mo = 511 * (ci % 4)
            tdc = cop("gpsimd", lambda e, off=off, mo=mo: e.tensor_sub(
                pk2[1][:, mo:mo + BS - 1], hfm[:, off + 1:off + BS], hfm[:, off:off + BS - 1]),
                waits=t_h2r + [buf.get(("pk", 1)), h2l_cons,
                               t_mixk[2 * (ci - 4)] if ci >= 4 else None], tok=True)
            txr = cop("vector", lambda e, off=off, ci=ci, mo=mo: e.scalar_tensor_tensor(
                qx[:, off + 1:off + BS], pk2[1][:, mo:mo + BS - 1], murb[:, ci:ci + 1],
                hfm[:, off:off + BS - 1], ALU.mult, ALU.add),
                waits=[tdc, t_mrc] + t_sc_all, tok=True)
            txk = cop("vector", lambda e, off=off, ci=ci, mo=mo: e.scalar_tensor_tensor(
                ax[:, off + 1:off + BS], pk2[1][:, mo:mo + BS - 1], mukb[:, ci:ci + 1],
                hfm[:, off:off + BS - 1], ALU.mult, ALU.add),
                waits=[t_mkc, tdc] + wo_users, tok=True)
            tcol = cop("gpsimd", lambda e, off=off, ci=ci: e.tensor_sub(
                mixcol[:, :], hfm[:, off:off + 1], h2last[:, ci:ci + 1]),
                waits=[h2l_tok, t_mixk[-1] if t_mixk else None], tok=True)
            tr0 = cop("vector", lambda e, off=off, ci=ci: e.scalar_tensor_tensor(
                qx[:, off:off + 1], mixcol[:, :], murb[:, ci:ci + 1],
                h2last[:, ci:ci + 1], ALU.mult, ALU.add),
                waits=[tcol], tok=True)
            tk0 = cop("vector", lambda e, off=off, ci=ci: e.scalar_tensor_tensor(
                ax[:, off:off + 1], mixcol[:, :], mukb[:, ci:ci + 1],
                h2last[:, ci:ci + 1], ALU.mult, ALU.add),
                waits=[tcol], tok=True)
            t_mixr += [txr, tr0]
            t_mixk += [txk, tk0]
        buf["pk"] = t_mixk[-1]
        lastap = bass.AP(hfm[:, :].tensor, BS - 1, [[8 * BS, 128], [BS, 8]])
        if n == NB - 1:
            h2l_tok = cop("vector", lambda e: e.memset(h2last[:, :], 0.0),
                          waits=t_mixk, tok=True)
        else:
            h2l_tok = cop("vector", lambda e, a=lastap: e.tensor_copy(h2last[:, :], a),
                          waits=t_mixk, tok=True)

        buf["hfm"] = h2l_tok
        if it + 1 < niter:
            pendingA = planA(it + 1)


        wr_users = []
        t_rsg = []
        for tj in range(4):
            tms = []
            for hh in range(2):
                bk = 2 + hh
                def rmm(e, bk=bk, tj=tj, hh=hh):
                    for ci in range(8):
                        ins = e.matmul(pb[bk][:, :],
                                       qx[:, BS * ci + 128 * tj:BS * ci + 128 * (tj + 1)],
                                       WS[1][:, D * ci + BS * hh:D * ci + BS * (hh + 1)],
                                       start=(ci == 0), stop=(ci == 7))
                    return ins
                tms.append(cop("tensor", rmm, waits=[bank_free[bk], t_wr] + t_mixr, tok=True))
            wr_users += tms
            for hh in range(2):
                bk = 2 + hh
                rap = bass.AP(qx[:, :].tensor, 2048 * hh + 128 * tj,
                              [[4096, 128], [512, 4], [1, 128]])
                trs = cop("scalar", lambda e, bk=bk, rap=rap: e.activation(
                    out=rap, in_=pb[bk][:, :], func=AF.Sigmoid),
                    waits=tms + [buf.get(("r", tj, hh))], tok=True)
                t_rsg.append(trs)
                bank_free[bk] = trs

        t_wv2 = stream_weight("wv2", 1, wr_users)

        wk2_users = []
        t_ka = []
        for dk in range(8):
            bk = 4 + dk % 2
            def kmm(e, bk=bk, dk=dk):
                for ci in range(8):
                    ins = e.matmul(pb[bk][:, :],
                                   WS[0][:, D * ci + 128 * dk:D * ci + 128 * (dk + 1)],
                                   ax[:, BS * ci:BS * (ci + 1)],
                                   start=(ci == 0), stop=(ci == 7))
                return ins
            tmm = cop("tensor", kmm, waits=[bank_free[bk], t_wk2] + t_mixk, tok=True)
            wk2_users.append(tmm)
            trel = cop("scalar", lambda e, bk=bk: e.activation(
                out=relu_t[:, :], in_=pb[bk][:, :], func=AF.Relu),
                waits=[tmm, buf.get("relu_t"), t_attn_evac[-1], t_attn_evac[-2]], tok=True)
            tka = cop("vector", lambda e, dk=dk: e.tensor_mul(
                pk2[0][:, BS * dk:BS * (dk + 1)], relu_t[:, :], relu_t[:, :]),
                waits=[trel, buf.get(("pk", 0))] + t_mixk + t_mixr, tok=True)
            buf["relu_t"] = tka
            t_ka.append(tka)
            bank_free[bk] = trel

        wv2_users = []
        t_fe_all = []
        for tj in range(4):
            t_fev = []
            for hh in range(2):
                bk = 6 + hh
                def v2mm(e, bk=bk, tj=tj, hh=hh):
                    for dk in range(8):
                        ins = e.matmul(pb[bk][:, :],
                                       pk2[0][:, BS * dk + 128 * tj:BS * dk + 128 * (tj + 1)],
                                       WS[1][:, D * dk + BS * hh:D * dk + BS * (hh + 1)],
                                       start=(dk == 0), stop=(dk == 7))
                    return ins
                tmm = cop("tensor", v2mm, waits=[bank_free[bk], t_wv2] + t_ka, tok=True)
                wv2_users.append(tmm)
                sl = tj % 2
                rap = bass.AP(qx[:, :].tensor, 2048 * hh + 128 * tj,
                              [[4096, 128], [512, 4], [1, 128]])
                pin = bass.AP(pb[bk][:, :].tensor, 0, [[BS, 128], [128, 4], [1, 128]])
                tfe = cop("vector", lambda e, sl=sl, hh=hh, pin=pin, rap=rap: e.tensor_mul(
                    aof[sl][:, BS * hh:BS * (hh + 1)], pin, rap),
                    waits=[tmm, buf.get(("aof", sl))] + t_rsg, tok=True)
                t_fev.append(tfe)
                t_fe_all.append(tfe)
                buf[("r", tj, hh)] = tfe
                bank_free[bk] = tfe
            sl = tj % 2
            t_rs = ln_stats(aof[sl][:, :], t_fev)
            tt1 = cop("vector", lambda e, sl=sl: e.tensor_scalar(
                scr[:, :], aof[sl][:, :], nmt[:, :], rst[:, :], ALU.add, ALU.mult),
                waits=[t_rs, buf.get("scr")], tok=True)
            buf[("aof", sl)] = tt1
            tz = cop("gpsimd", lambda e, sl=sl, tj=tj: e.tensor_add(
                ybuf[:, D * tj:D * (tj + 1)], scr[:, :], ybuf[:, D * tj:D * (tj + 1)]),
                waits=[tt1], tok=True)
            buf["scr"] = tz
            if tj == 3:
                buf[("pk", 1)] = tt1
            tzd = dma("scalar", "z", out[rows0 + 128 * tj: rows0 + 128 * (tj + 1), :],
                      ybuf[:, D * tj:D * (tj + 1)], waits=[tz])
            t_z_stores.append(tzd)
        h2l_cons = t_mixk[-1]
        prev_z = t_z_stores[-1]
        buf["htm"] = t_h2w[-1]
        buf["hfm"] = h2l_tok
        buf["qx"] = t_fe_all[-1]
        buf["ax"] = wk2_users[-1]
        buf[("pk", 0)] = wv2_users[-1]
        buf[("usb", 0)] = t_ka[-1]
        buf[("usb", 1)] = t_ka[-1]

        if it + 1 < niter:
            next_wo = stream_weight("wo", 0, wk2_users)
            next_wr = stream_weight("wr", 1, wv2_users)

    total_z = sch.sems.get("z", 0)
    sch.op("sync", lambda e: e.wait_ge(semh["z"], total_z))

    with nc.allow_low_precision(reason="bf16 matmul operand production"):
        with nc.Block() as block:
            @block.sync
            def _(eng):
                sch.emit("sync", eng, semh)

            @block.tensor
            def _(eng):
                sch.emit("tensor", eng, semh)

            @block.vector
            def _(eng):
                sch.emit("vector", eng, semh)

            @block.scalar
            def _(eng):
                sch.emit("scalar", eng, semh)

            @block.gpsimd
            def _(eng):
                sch.emit("gpsimd", eng, semh)

    st.close()
    return nc


def make_consts():
    half = D // 2
    j = np.arange(half, dtype=np.float64)
    inv = 1.0 / (10000.0 ** (j / half))
    t = np.arange(S, dtype=np.float64)
    ang = t[None, :] * inv[:, None]
    cos = np.cos(ang).astype(np.float32).reshape(4, 128, S)
    sin = np.sin(ang).astype(np.float32).reshape(4, 128, S)
    iden = np.eye(128, dtype=ml_dtypes.bfloat16)
    ii = np.arange(128)
    tri0 = np.where(ii[:, None] >= ii[None, :], 1.0, 0.0).astype(ml_dtypes.bfloat16)
    tri4 = np.where(ii[:, None] <= ii[None, :], 1.0, 0.0).astype(ml_dtypes.bfloat16)
    return cos, sin, iden, tri0, tri4


def make_inmap(shard, weights):
    cos, sin, iden, tri0, tri4 = make_consts()
    m = {
        "x": np.ascontiguousarray(np.asarray(shard).reshape(NSEQ * S, D)),
        "cosd": cos, "sind": sin, "tri0": tri0, "tri4": tri4,
    }
    (Wq, bq, Wk, bk, Wv, bv, Wo, bo, Wr, Wk2, Wv2, mur, muk) = weights
    for nme, w in [("wq", Wq), ("wk", Wk), ("wv", Wv), ("wo", Wo),
                   ("wr", Wr), ("wk2", Wk2), ("wv2", Wv2)]:
        m[nme] = np.ascontiguousarray(np.asarray(w).T)
    m["bq"] = np.ascontiguousarray(np.asarray(bq).reshape(8, 128).T)
    m["bk"] = np.ascontiguousarray(np.asarray(bk).reshape(8, 128).T)
    m["bv"] = np.ascontiguousarray(np.asarray(bv).reshape(1, D))
    m["bo"] = np.ascontiguousarray(np.asarray(bo).reshape(1, D))
    m["mur"] = np.ascontiguousarray(np.asarray(mur).reshape(8, 128).T)
    m["muk"] = np.ascontiguousarray(np.asarray(muk).reshape(8, 128).T)
    return m


def kernel(L0_attn_hidden_state, L0_attn_Wq, L0_attn_bq, L0_attn_Wk, L0_attn_bk,
           L0_attn_Wv, L0_attn_bv, L0_attn_Wo, L0_attn_bo,
           L0_ffn_W_r, L0_ffn_W_k, L0_ffn_W_v, L0_ffn_mu_r, L0_ffn_mu_k):
    x = np.asarray(L0_attn_hidden_state)
    weights = (L0_attn_Wq, L0_attn_bq, L0_attn_Wk, L0_attn_bk, L0_attn_Wv, L0_attn_bv,
               L0_attn_Wo, L0_attn_bo, L0_ffn_W_r, L0_ffn_W_k, L0_ffn_W_v,
               L0_ffn_mu_r, L0_ffn_mu_k)
    nc = build_core()
    in_maps = [make_inmap(x[4 * i:4 * (i + 1)], weights) for i in range(8)]
    res = run_bass_kernel_spmd(nc, in_maps, core_ids=list(range(8)))
    outs = [np.asarray(r["out"]).reshape(NSEQ, S, D) for r in res.results]
    return np.concatenate(outs, axis=0)
